# version 02
# baseline (speedup 1.0000x reference)
import numpy as np
import ml_dtypes
import concourse.bass as bass
import concourse.mybir as mybir
from concourse.alu_op_type import AluOpType as A
from concourse.bass_utils import run_bass_kernel_spmd
from contextlib import ExitStack

F32 = mybir.dt.float32
BF = mybir.dt.bfloat16
F16 = mybir.dt.float16
AF = mybir.ActivationFunctionType
AX = mybir.AxisListType.X

D = 4096
NDC = 32
TOK = 2048
NSLOT = 16
SFULL = 4096
NQC = 16
NVC = 32
NIC = 56
NH = 8
HD = 256
NE = 16
EPS = 1e-5

KBCS = [(s + 2) // 2 for s in range(NSLOT)]
KB_PRE = [0]
for s in range(NSLOT):
    KB_PRE.append(KB_PRE[-1] + KBCS[s])
KB_TOT = KB_PRE[-1]


def build_nc(dbg=False, upto=99):
    nc = bass.Bass("TRN2", target_bir_lowering=False)

    x_full = nc.dram_tensor("x_full", [32, 128, D], F32, kind="ExternalInput").ap()
    x_mine = nc.dram_tensor("x_mine", [NSLOT, 128, D], F32, kind="ExternalInput").ap()
    wq = nc.dram_tensor("wq", [NQC, 128, NDC, 128], F32, kind="ExternalInput").ap()
    wkv = nc.dram_tensor("wkv", [NVC, 128, NDC, 128], F32, kind="ExternalInput").ap()
    wu = nc.dram_tensor("wu", [NIC, 128, NDC, 128], F32, kind="ExternalInput").ap()
    wg = nc.dram_tensor("wg", [NIC, 128, NDC, 128], F32, kind="ExternalInput").ap()
    wd = nc.dram_tensor("wd", [NIC, 128, D], F32, kind="ExternalInput").ap()
    wo = nc.dram_tensor("wo", [NQC, 128, D], F32, kind="ExternalInput").ap()
    gwt = nc.dram_tensor("gwt", [128, NDC, NE], F32, kind="ExternalInput").ap()
    bq_in = nc.dram_tensor("bq_in", [128, NQC], F32, kind="ExternalInput").ap()
    bkv_in = nc.dram_tensor("bkv_in", [128, NVC], F32, kind="ExternalInput").ap()
    bo_in = nc.dram_tensor("bo_in", [1, D], F32, kind="ExternalInput").ap()
    eye_in = nc.dram_tensor("eye_in", [128, 128], F32, kind="ExternalInput").ap()
    ones_in = nc.dram_tensor("ones_in", [1, 128], F32, kind="ExternalInput").ap()
    smat_in = nc.dram_tensor("smat_in", [NE, NIC * 128], F32, kind="ExternalInput").ap()
    masks_in = nc.dram_tensor("masks_in", [NSLOT, 128, 512], F32, kind="ExternalInput").ap()

    out_mine = nc.dram_tensor("out_mine", [NSLOT, 128, D], F32, kind="ExternalOutput").ap()

    sk = "ExternalOutput" if dbg else "Internal"
    q_sc = nc.dram_tensor("q_sc", [NQC, 128, TOK], BF, kind=sk).ap()
    k_sc = nc.dram_tensor("k_sc", [NQC, 128, SFULL], BF, kind=sk).ap()
    v_sc = nc.dram_tensor("v_sc", [32, 128, 2048], BF, kind=sk).ap()
    h_sc = nc.dram_tensor("h_sc", [NIC, 128, TOK], BF, kind=sk).ap()
    s_sc = nc.dram_tensor("s_sc", [NSLOT, 128, D], F32, kind=sk).ap()
    vals_sc = nc.dram_tensor("vals_sc", [NSLOT, 128, TOK], F16, kind=sk).ap()
    if dbg:
        xn_dbg = nc.dram_tensor("xn_dbg", [128, NDC, TOK], BF, kind="ExternalOutput").ap()
        wtb_dbg = nc.dram_tensor("wtb_dbg", [NE, TOK], BF, kind="ExternalOutput").ap()

    root = ExitStack()
    idbf = root.enter_context(nc.sbuf_tensor("idbf", [128, 128], BF))
    idfp = root.enter_context(nc.sbuf_tensor("idfp", [128, 128], F16))
    ones_bf = root.enter_context(nc.sbuf_tensor("ones_bf", [1, 128], BF))

    BQ = root.enter_context(nc.sbuf_tensor("BQ", [128, NQC], F32))
    BKV = root.enter_context(nc.sbuf_tensor("BKV", [128, NVC], F32))
    ld1 = root.enter_context(nc.semaphore("ld1"))
    ld2 = root.enter_context(nc.semaphore("ld2"))

    mid = ExitStack()
    XN = mid.enter_context(nc.sbuf_tensor("XN", [128, NDC, TOK], BF))
    SQD = mid.enter_context(nc.sbuf_tensor("SQD", [128, D], BF))
    mid2 = ExitStack()
    WTB = mid2.enter_context(nc.sbuf_tensor("WTB", [NE, TOK], BF))
    S_sb = mid2.enter_context(nc.sbuf_tensor("S_sb", [NE, NIC * 128], BF))
    GWS = mid2.enter_context(nc.sbuf_tensor("GWS", [128, NDC, NE], BF))

    def ln_alloc(es, tag):
        b = {}
        b["xsb"] = [es.enter_context(nc.sbuf_tensor(f"{tag}xsb{i}", [128, D], F32)) for i in range(2)]
        b["xnt"] = es.enter_context(nc.sbuf_tensor(f"{tag}xnt", [128, D], BF))
        b["ssq"] = [es.enter_context(nc.sbuf_tensor(f"{tag}ssq{i}", [128, 1], F32)) for i in range(2)]
        b["sm"] = es.enter_context(nc.sbuf_tensor(f"{tag}sm", [128, 1], F32))
        b["mn"] = [es.enter_context(nc.sbuf_tensor(f"{tag}mn{i}", [128, 1], F32)) for i in range(2)]
        b["ex2"] = es.enter_context(nc.sbuf_tensor(f"{tag}ex2", [128, 1], F32))
        b["msq"] = es.enter_context(nc.sbuf_tensor(f"{tag}msq", [128, 1], F32))
        b["var"] = es.enter_context(nc.sbuf_tensor(f"{tag}var", [128, 1], F32))
        b["ve"] = es.enter_context(nc.sbuf_tensor(f"{tag}ve", [128, 1], F32))
        b["rv"] = [es.enter_context(nc.sbuf_tensor(f"{tag}rv{i}", [128, 1], F32)) for i in range(2)]
        b["rsd"] = [es.enter_context(nc.sbuf_tensor(f"{tag}rsd{i}", [128, 1], F32)) for i in range(2)]
        b["pt"] = [es.enter_context(nc.psum_tensor(f"{tag}pt{i}", [128, 4, 128], BF)) for i in range(2)]
        for s in ["ldx", "fre", "sqs", "rvd", "rsb", "xnr", "ptd", "evd", "gd"]:
            b[s] = es.enter_context(nc.semaphore(f"{tag}{s}"))
        b["gdn"] = [0]
        return b

    def ln_sync(sync, b, x_ap, tiles, g0):
        for i, t in enumerate(tiles):
            gs = g0 + i
            if gs >= 2:
                sync.wait_ge(b["fre"], 2 * (gs - 1))
            sync.dma_start(out=b["xsb"][gs % 2][:], in_=x_ap[t]).then_inc(b["ldx"], 16)

    def ln_act(act, b, nsub, g0):
        for i in range(nsub):
            gs = g0 + i
            act.wait_ge(b["ldx"], 16 * (gs + 1))
            act.activation(out=SQD[:], in_=b["xsb"][gs % 2][:], func=AF.Square,
                           bias=0.0, scale=1.0,
                           accum_out=b["ssq"][gs % 2][:]).then_inc(b["fre"], 1)
            act.maybe_drain_then_inc((b["sqs"], 1))
            act.wait_ge(b["rvd"], gs + 1)
            act.activation(out=b["rsd"][gs % 2][:], in_=b["rv"][gs % 2][:], func=AF.Sqrt,
                           bias=0.0, scale=1.0)
            act.maybe_drain_then_inc((b["rsb"], 1))
            for bt in range(8):
                gb = 8 * gs + bt
                act.wait_ge(b["ptd"], gb + 1)
                act.copy(out=XN[:, 4 * bt:4 * bt + 4, i * 128:(i + 1) * 128],
                         in_=b["pt"][bt % 2][:]).then_inc(b["evd"], 1)

    def ln_dve(v, b, nsub, g0):
        def G():
            b["gdn"][0] += 1
            v.maybe_drain_then_inc((b["gd"], 1))
            v.wait_ge(b["gd"], b["gdn"][0])
        for i in range(nsub):
            gs = g0 + i
            v.wait_ge(b["ldx"], 16 * (gs + 1))
            v.reduce_sum(b["sm"][:], b["xsb"][gs % 2][:], axis=AX)
            G()
            v.tensor_scalar(out=b["mn"][gs % 2][:], in0=b["sm"][:], scalar1=1.0 / D,
                            scalar2=None, op0=A.mult)
            v.wait_ge(b["sqs"], gs + 1)
            v.tensor_scalar(out=b["ex2"][:], in0=b["ssq"][gs % 2][:], scalar1=1.0 / D,
                            scalar2=None, op0=A.mult)
            G()
            v.tensor_tensor(out=b["msq"][:], in0=b["mn"][gs % 2][:], in1=b["mn"][gs % 2][:], op=A.mult)
            G()
            v.tensor_tensor(out=b["var"][:], in0=b["ex2"][:], in1=b["msq"][:], op=A.subtract)
            G()
            v.tensor_scalar(out=b["ve"][:], in0=b["var"][:], scalar1=EPS, scalar2=None, op0=A.add)
            G()
            v.reciprocal(b["rv"][gs % 2][:], b["ve"][:])
            v.maybe_drain_then_inc((b["rvd"], 1))
            v.wait_ge(b["rsb"], gs + 1)
            if gs >= 1:
                v.wait_ge(b["ptd"], 8 * gs)
            v.tensor_scalar(out=b["xnt"][:], in0=b["xsb"][gs % 2][:],
                            scalar1=b["mn"][gs % 2][:], scalar2=b["rsd"][gs % 2][:],
                            op0=A.subtract, op1=A.mult).then_inc(b["fre"], 1)
            v.maybe_drain_then_inc((b["xnr"], 1))

    def ln_pe(t, b, nsub, g0):
        for i in range(nsub):
            gs = g0 + i
            t.wait_ge(b["xnr"], gs + 1)
            for bt in range(8):
                gb = 8 * gs + bt
                if gb >= 2:
                    t.wait_ge(b["evd"], gb - 1)
                for j in range(4):
                    t.transpose(b["pt"][bt % 2][:, j, :],
                                b["xnt"][:, (4 * bt + j) * 128:(4 * bt + j) * 128 + 128],
                                idbf[:])
                t.maybe_drain_then_inc((b["ptd"], 1))

    es0 = ExitStack()
    lnb = ln_alloc(es0, "l0")
    with nc.Block() as blk:
        @blk.gpsimd
        def _(g):
            g.dma_start(out=idbf[:], in_=eye_in).then_inc(ld1, 16)
            g.dma_start(out=idfp[:], in_=eye_in).then_inc(ld1, 16)
            g.dma_start(out=ones_bf[:], in_=ones_in).then_inc(ld1, 16)
            g.dma_start(out=GWS[:], in_=gwt).then_inc(ld1, 16)
            g.dma_start(out=S_sb[:], in_=smat_in).then_inc(ld1, 16)

        @blk.sync
        def _(sync):
            sync.dma_start(out=BQ[:], in_=bq_in).then_inc(ld2, 16)
            sync.dma_start(out=BKV[:], in_=bkv_in).then_inc(ld2, 16)
            ln_sync(sync, lnb, x_mine, list(range(16)), 0)

        @blk.scalar
        def _(act):
            ln_act(act, lnb, 16, 0)

        @blk.vector
        def _(v):
            ln_dve(v, lnb, 16, 0)

        @blk.tensor
        def _(t):
            t.wait_ge(ld1, 16)
            ln_pe(t, lnb, 16, 0)
    es0.close()
    if upto < 1:
        mid2.close(); mid.close(); root.close(); return nc

    es1 = ExitStack()
    WQB = [es1.enter_context(nc.sbuf_tensor(f"WQB{i}", [128, NDC, 128], BF)) for i in range(2)]
    QTILE = [es1.enter_context(nc.sbuf_tensor(f"QTILE{i}", [128, TOK], BF)) for i in range(2)]
    QP = [es1.enter_context(nc.psum_tensor(f"QP{i}", [128, 512], F32)) for i in range(8)]
    wld = es1.enter_context(nc.semaphore("wld"))
    qmm = es1.enter_context(nc.semaphore("qmm"))
    qtl = es1.enter_context(nc.semaphore("qtl"))
    qst = es1.enter_context(nc.semaphore("qst"))
    with nc.Block() as blk:
        @blk.gpsimd
        def _(g):
            for qc in range(NQC):
                if qc >= 2:
                    g.wait_ge(qmm, qc - 1)
                g.dma_start(out=WQB[qc % 2][:], in_=wq[qc]).then_inc(wld, 16)

        @blk.tensor
        def _(t):
            for qc in range(NQC):
                t.wait_ge(wld, 16 * (qc + 1))
                if qc >= 2:
                    t.wait_ge(qtl, qc - 1)
                bs = (qc % 2) * 4
                for dc in range(NDC):
                    for ts in range(4):
                        t.matmul(QP[bs + ts][:], WQB[qc % 2][:, dc, :],
                                 XN[:, dc, ts * 512:(ts + 1) * 512],
                                 start=(dc == 0), stop=(dc == NDC - 1))
                t.maybe_drain_then_inc((qmm, 1))

        @blk.vector
        def _(v):
            v.wait_ge(ld2, 16)
            for qc in range(NQC):
                v.wait_ge(qmm, qc + 1)
                if qc >= 2:
                    v.wait_ge(qst, 16 * (qc - 1))
                bs = (qc % 2) * 4
                for ts in range(4):
                    v.tensor_scalar(out=QTILE[qc % 2][:, ts * 512:(ts + 1) * 512],
                                    in0=QP[bs + ts][:], scalar1=BQ[:, qc:qc + 1],
                                    scalar2=None, op0=A.add)
                v.maybe_drain_then_inc((qtl, 1))

        @blk.sync
        def _(sync):
            for qc in range(NQC):
                sync.wait_ge(qtl, qc + 1)
                sync.dma_start(out=q_sc[qc], in_=QTILE[qc % 2][:]).then_inc(qst, 16)
            sync.wait_ge(qst, 16 * NQC)
            if dbg and upto == 1:
                sync.dma_start(out=xn_dbg, in_=XN[:]).then_inc(qst, 16)
                sync.wait_ge(qst, 16 * NQC + 16)
    es1.close()
    if upto < 2:
        mid2.close(); mid.close(); root.close(); return nc

    es2 = ExitStack()
    RPB = es2.enter_context(nc.psum_tensor("RPB", [128, 256], F32))
    TPW = [es2.enter_context(nc.psum_tensor(f"TPW{i}", [16, 128], BF)) for i in range(2)]
    WBF = es2.enter_context(nc.sbuf_tensor("WBF", [128, 256], BF))
    rlg = es2.enter_context(nc.sbuf_tensor("rlg", [128, NE], F32))
    rm1 = es2.enter_context(nc.sbuf_tensor("rm1", [128, 1], F32))
    rnm = es2.enter_context(nc.sbuf_tensor("rnm", [128, 1], F32))
    rmsk = es2.enter_context(nc.sbuf_tensor("rmsk", [128, NE], F32))
    rlm = es2.enter_context(nc.sbuf_tensor("rlm", [128, NE], F32))
    rm2 = es2.enter_context(nc.sbuf_tensor("rm2", [128, 1], F32))
    rsel = es2.enter_context(nc.sbuf_tensor("rsel", [128, NE], F32))
    rel = es2.enter_context(nc.sbuf_tensor("rel", [128, NE], F32))
    rwn = es2.enter_context(nc.sbuf_tensor("rwn", [128, NE], F32))
    rden = es2.enter_context(nc.sbuf_tensor("rden", [128, 1], F32))
    rrd = es2.enter_context(nc.sbuf_tensor("rrd", [128, 1], F32))
    rwf = es2.enter_context(nc.sbuf_tensor("rwf", [128, NE], F32))
    rmm = es2.enter_context(nc.semaphore("rmm"))
    rwb = es2.enter_context(nc.semaphore("rwb"))
    rtw = es2.enter_context(nc.semaphore("rtw"))
    rwt = es2.enter_context(nc.semaphore("rwt"))
    rex = es2.enter_context(nc.semaphore("rex"))
    rnd = es2.enter_context(nc.semaphore("rnd"))
    rgd = es2.enter_context(nc.semaphore("rgd"))
    with nc.Block() as blk:
        @blk.tensor
        def _(t):
            t.wait_ge(ld1, 80)
            for sub in range(16):
                for dc in range(NDC):
                    t.matmul(RPB[:, sub * 16:sub * 16 + 16],
                             XN[:, dc, sub * 128:sub * 128 + 128],
                             GWS[:, dc, :], start=(dc == 0), stop=(dc == NDC - 1))
            t.maybe_drain_then_inc((rmm, 1))
            for sub in range(16):
                t.wait_ge(rwb, sub + 1)
                if sub >= 2:
                    t.wait_ge(rwt, sub - 1)
                t.transpose(TPW[sub % 2][:], WBF[:, sub * 16:sub * 16 + 16], idbf[:])
                t.maybe_drain_then_inc((rtw, 1))

        @blk.vector
        def _(v):
            g = [0]
            def G():
                g[0] += 1
                v.maybe_drain_then_inc((rgd, 1))
                v.wait_ge(rgd, g[0])
            v.wait_ge(rmm, 1)
            for sub in range(16):
                v.tensor_copy(rlg[:], RPB[:, sub * 16:sub * 16 + 16])
                G()
                v.reduce_max(rm1[:], rlg[:], axis=AX)
                G()
                v.tensor_scalar(out=rnm[:], in0=rm1[:], scalar1=-1.0, scalar2=None, op0=A.mult)
                v.tensor_scalar(out=rmsk[:], in0=rlg[:], scalar1=rm1[:], scalar2=None, op0=A.is_ge)
                G()
                v.scalar_tensor_tensor(out=rlm[:], in0=rmsk[:], scalar=-30000.0, in1=rlg[:],
                                       op0=A.mult, op1=A.add)
                G()
                v.reduce_max(rm2[:], rlm[:], axis=AX)
                G()
                v.tensor_scalar(out=rsel[:], in0=rlg[:], scalar1=rm2[:], scalar2=None, op0=A.is_ge)
                v.maybe_drain_then_inc((rnd, 1))
                v.wait_ge(rex, sub + 1)
                v.tensor_tensor(out=rwn[:], in0=rel[:], in1=rsel[:], op=A.mult)
                G()
                v.reduce_sum(rden[:], rwn[:], axis=AX)
                G()
                v.reciprocal(rrd[:], rden[:])
                G()
                v.tensor_scalar(out=rwf[:], in0=rwn[:], scalar1=rrd[:], scalar2=None, op0=A.mult)
                G()
                v.tensor_copy(WBF[:, sub * 16:sub * 16 + 16], rwf[:])
                v.maybe_drain_then_inc((rwb, 1))
            for sub in range(16):
                v.wait_ge(rtw, sub + 1)
                v.tensor_copy(WTB[:, sub * 128:(sub + 1) * 128], TPW[sub % 2][:])
                v.maybe_drain_then_inc((rwt, 1))

        @blk.scalar
        def _(act):
            for sub in range(16):
                act.wait_ge(rnd, sub + 1)
                act.activation(out=rel[:], in_=rlg[:], func=AF.Exp, bias=rnm[:], scale=1.0)
                act.maybe_drain_then_inc((rex, 1))
    es2.close()
    if upto < 3:
        mid2.close(); mid.close(); root.close(); return nc

    es3 = ExitStack()
    UB = [es3.enter_context(nc.sbuf_tensor(f"UB{i}", [128, NDC, 128], BF)) for i in range(2)]
    GB = [es3.enter_context(nc.sbuf_tensor(f"GB{i}", [128, NDC, 128], BF)) for i in range(2)]
    SG = es3.enter_context(nc.sbuf_tensor("SG", [128, TOK], BF))
    H1 = es3.enter_context(nc.sbuf_tensor("H1", [128, TOK], BF))
    HW = [es3.enter_context(nc.sbuf_tensor(f"HW{i}", [128, TOK], BF)) for i in range(2)]
    UP = [es3.enter_context(nc.psum_tensor(f"UP{i}", [128, 512], F32)) for i in range(4)]
    GP = [es3.enter_context(nc.psum_tensor(f"GP{i}", [128, 512], F32)) for i in range(4)]
    uld = es3.enter_context(nc.semaphore("uld"))
    ugm = es3.enter_context(nc.semaphore("ugm"))
    wpm = es3.enter_context(nc.semaphore("wpm"))
    sgd = es3.enter_context(nc.semaphore("sgd"))
    h1d = es3.enter_context(nc.semaphore("h1d"))
    hwd = es3.enter_context(nc.semaphore("hwd"))
    hst = es3.enter_context(nc.semaphore("hst"))
    with nc.Block() as blk:
        @blk.gpsimd
        def _(g):
            for ic in range(NIC):
                if ic >= 2:
                    g.wait_ge(ugm, ic - 1)
                g.dma_start(out=UB[ic % 2][:], in_=wu[ic]).then_inc(uld, 16)
                g.dma_start(out=GB[ic % 2][:], in_=wg[ic]).then_inc(uld, 16)

        @blk.tensor
        def _(t):
            for ic in range(NIC):
                t.wait_ge(uld, 32 * (ic + 1))
                if ic >= 1:
                    t.wait_ge(hwd, ic)
                    t.wait_ge(sgd, ic)
                for dc in range(NDC):
                    for ts in range(4):
                        t.matmul(UP[ts][:], UB[ic % 2][:, dc, :],
                                 XN[:, dc, ts * 512:(ts + 1) * 512],
                                 start=(dc == 0), stop=(dc == NDC - 1))
                    for ts in range(4):
                        t.matmul(GP[ts][:], GB[ic % 2][:, dc, :],
                                 XN[:, dc, ts * 512:(ts + 1) * 512],
                                 start=(dc == 0), stop=(dc == NDC - 1))
                t.maybe_drain_then_inc((ugm, 1))
                t.wait_ge(h1d, ic + 1)
                for ts in range(4):
                    t.matmul(UP[ts][:], S_sb[:, ic * 128:(ic + 1) * 128],
                             WTB[:, ts * 512:(ts + 1) * 512], start=True, stop=True)
                t.maybe_drain_then_inc((wpm, 1))

        @blk.scalar
        def _(act):
            for ic in range(NIC):
                act.wait_ge(ugm, ic + 1)
                for ts in range(4):
                    act.activation(out=SG[:, ts * 512:(ts + 1) * 512], in_=GP[ts][:],
                                   func=AF.Silu, bias=0.0, scale=1.0)
                act.maybe_drain_then_inc((sgd, 1))

        @blk.vector
        def _(v):
            for ic in range(NIC):
                v.wait_ge(sgd, ic + 1)
                for ts in range(4):
                    v.tensor_tensor(out=H1[:, ts * 512:(ts + 1) * 512], in0=UP[ts][:],
                                    in1=SG[:, ts * 512:(ts + 1) * 512], op=A.mult)
                v.maybe_drain_then_inc((h1d, 1))
                v.wait_ge(wpm, ic + 1)
                if ic >= 2:
                    v.wait_ge(hst, 16 * (ic - 1))
                for ts in range(4):
                    v.tensor_tensor(out=HW[ic % 2][:, ts * 512:(ts + 1) * 512], in0=UP[ts][:],
                                    in1=H1[:, ts * 512:(ts + 1) * 512], op=A.mult)
                v.maybe_drain_then_inc((hwd, 1))

        @blk.sync
        def _(sync):
            for ic in range(NIC):
                sync.wait_ge(hwd, ic + 1)
                sync.dma_start(out=h_sc[ic], in_=HW[ic % 2][:]).then_inc(hst, 16)
            sync.wait_ge(hst, 16 * NIC)
            if dbg:
                sync.dma_start(out=xn_dbg, in_=XN[:]).then_inc(hst, 16)
                sync.dma_start(out=wtb_dbg, in_=WTB[:]).then_inc(hst, 16)
                sync.wait_ge(hst, 16 * NIC + 32)
    es3.close()
    mid2.close()
    if upto < 4:
        mid.close(); root.close(); return nc

    def kfree_evt(i):
        hf_, vc_ = divmod(i, NVC)
        if vc_ < NQC:
            kcnt = sum(1 for j in range(i + 1) if (j % NVC) < NQC)
            return ("ksk", 16 * kcnt)
        gv_ = hf_ * 16 + (vc_ - NQC)
        return ("vtm", 4 * (gv_ + 1))

    es4 = ExitStack()
    lnb4 = ln_alloc(es4, "l4")
    WVB = [es4.enter_context(nc.sbuf_tensor(f"WVB{i}", [128, NDC, 128], BF)) for i in range(2)]
    KVT = [es4.enter_context(nc.sbuf_tensor(f"KVT{i}", [128, TOK], BF)) for i in range(2)]
    VTT = [es4.enter_context(nc.sbuf_tensor(f"VTT{i}", [128, 4, 128], BF)) for i in range(2)]
    KVP = [es4.enter_context(nc.psum_tensor(f"KVP{i}", [128, 512], F32)) for i in range(4)]
    VPT = [es4.enter_context(nc.psum_tensor(f"VPT{i}", [128, 4, 128], BF)) for i in range(2)]
    kwl = es4.enter_context(nc.semaphore("kwl"))
    kvm = es4.enter_context(nc.semaphore("kvm"))
    kte = es4.enter_context(nc.semaphore("kte"))
    ksk = es4.enter_context(nc.semaphore("ksk"))
    vst = es4.enter_context(nc.semaphore("vst"))
    vtm = es4.enter_context(nc.semaphore("vtm"))
    vte = es4.enter_context(nc.semaphore("vte"))
    with nc.Block() as blk:
        @blk.sync
        def _(sync):
            for hf in range(2):
                ln_sync(sync, lnb4, x_full, list(range(hf * 16, hf * 16 + 16)), hf * 16)
                for vc in range(NVC):
                    i = hf * NVC + vc
                    if vc < NQC:
                        sync.wait_ge(kte, i + 1)
                        sync.dma_start(out=k_sc[vc][:, hf * 2048:(hf + 1) * 2048],
                                       in_=KVT[i % 2][:]).then_inc(ksk, 16)
                    else:
                        gv = hf * 16 + (vc - NQC)
                        for bt in range(4):
                            gvb = gv * 4 + bt
                            sync.wait_ge(vte, gvb + 1)
                            for j in range(4):
                                tsub = bt * 4 + j
                                sync.dma_start(
                                    out=v_sc[hf * 16 + tsub][:, (vc - NQC) * 128:(vc - NQC) * 128 + 128],
                                    in_=VTT[gvb % 2][:, j, :]).then_inc(vst, 16)
            sync.wait_ge(ksk, 16 * 32)
            sync.wait_ge(vst, 16 * 4 * 128)

        @blk.gpsimd
        def _(g):
            for hf in range(2):
                for vc in range(NVC):
                    i = hf * NVC + vc
                    if i >= 2:
                        g.wait_ge(kvm, i - 1)
                    g.dma_start(out=WVB[i % 2][:], in_=wkv[vc]).then_inc(kwl, 16)

        @blk.scalar
        def _(act):
            for hf in range(2):
                ln_act(act, lnb4, 16, hf * 16)
                for vc in range(NQC, NVC):
                    gv = hf * 16 + (vc - NQC)
                    for bt in range(4):
                        gvb = gv * 4 + bt
                        act.wait_ge(vtm, gvb + 1)
                        if gvb >= 2:
                            act.wait_ge(vst, 16 * 4 * (gvb - 1))
                        act.copy(out=VTT[gvb % 2][:], in_=VPT[gvb % 2][:])
                        act.maybe_drain_then_inc((vte, 1))

        @blk.vector
        def _(v):
            v.wait_ge(ld2, 32)
            for hf in range(2):
                ln_dve(v, lnb4, 16, hf * 16)
                for vc in range(NVC):
                    i = hf * NVC + vc
                    v.wait_ge(kvm, i + 1)
                    if i >= 2:
                        sem, val = kfree_evt(i - 2)
                        v.wait_ge(ksk if sem == "ksk" else vtm, val)
                    for ts in range(4):
                        v.tensor_scalar(out=KVT[i % 2][:, ts * 512:(ts + 1) * 512],
                                        in0=KVP[ts][:], scalar1=BKV[:, vc:vc + 1],
                                        scalar2=None, op0=A.add)
                    v.maybe_drain_then_inc((kte, 1))

        @blk.tensor
        def _(t):
            for hf in range(2):
                ln_pe(t, lnb4, 16, hf * 16)
                for vc in range(NVC):
                    i = hf * NVC + vc
                    t.wait_ge(kwl, 16 * (i + 1))
                    if i >= 1:
                        t.wait_ge(kte, i)
                    for dc in range(NDC):
                        for ts in range(4):
                            t.matmul(KVP[ts][:], WVB[i % 2][:, dc, :],
                                     XN[:, dc, ts * 512:(ts + 1) * 512],
                                     start=(dc == 0), stop=(dc == NDC - 1))
                    t.maybe_drain_then_inc((kvm, 1))
                    if vc >= NQC:
                        gv = hf * 16 + (vc - NQC)
                        t.wait_ge(kte, i + 1)
                        for bt in range(4):
                            gvb = gv * 4 + bt
                            if gvb >= 2:
                                t.wait_ge(vte, gvb - 1)
                            for j in range(4):
                                tsub = bt * 4 + j
                                t.transpose(VPT[gvb % 2][:, j, :],
                                            KVT[i % 2][:, tsub * 128:(tsub + 1) * 128],
                                            idbf[:])
                            t.maybe_drain_then_inc((vtm, 1))
    es4.close()
    mid.close()
    if upto < 5:
        root.close(); return nc

    es5 = ExitStack()
    SALL = es5.enter_context(nc.sbuf_tensor("SALL", [128, NH, SFULL], F16))
    QS = [es5.enter_context(nc.sbuf_tensor(f"QS{i}", [128, NQC, 128], BF)) for i in range(2)]
    KBB = [es5.enter_context(nc.sbuf_tensor(f"KBB{i}", [128, NQC, 512], BF)) for i in range(2)]
    VBB = [es5.enter_context(nc.sbuf_tensor(f"VBB{i}", [128, 4, 2048], BF)) for i in range(2)]
    MS = [es5.enter_context(nc.sbuf_tensor(f"MS{i}", [128, 512], F32)) for i in range(2)]
    MBAL = es5.enter_context(nc.sbuf_tensor("MBAL", [128, NH * 8], F32))
    MMAX = es5.enter_context(nc.sbuf_tensor("MMAX", [128, NH], F32))
    NEGM = es5.enter_context(nc.sbuf_tensor("NEGM", [128, NH], F32))
    LSUM = es5.enter_context(nc.sbuf_tensor("LSUM", [128, NH], F32))
    RLV = es5.enter_context(nc.sbuf_tensor("RLV", [128, NH], F32))
    PTS = [es5.enter_context(nc.sbuf_tensor(f"PTS{i}", [128, 4, 128], F16)) for i in range(NH)]
    OSB = es5.enter_context(nc.sbuf_tensor("OSB", [128, NH * 256], F16))
    VALT = es5.enter_context(nc.sbuf_tensor("VALT", [128, NQC, 128], F16))
    SPS = [es5.enter_context(nc.psum_tensor(f"SPS{i}", [128, 512], F32)) for i in range(2)]
    PTP = [es5.enter_context(nc.psum_tensor(f"PTP{i}", [128, 4, 128], F16)) for i in range(2)]
    OPP = [es5.enter_context(nc.psum_tensor(f"OPP{i}", [128, 2, 256], F32)) for i in range(4)]
    qld = es5.enter_context(nc.semaphore("qld"))
    msl = es5.enter_context(nc.semaphore("msl"))
    msf = es5.enter_context(nc.semaphore("msf"))
    kld = es5.enter_context(nc.semaphore("kld"))
    vld = es5.enter_context(nc.semaphore("vld"))
    smm = es5.enter_context(nc.semaphore("smm"))
    sev = es5.enter_context(nc.semaphore("sev"))
    nmd = es5.enter_context(nc.semaphore("nmd"))
    expd = es5.enter_context(nc.semaphore("expd"))
    ptm = es5.enter_context(nc.semaphore("ptm"))
    ptsc = es5.enter_context(nc.semaphore("ptsc"))
    avm = es5.enter_context(nc.semaphore("avm"))
    osd = es5.enter_context(nc.semaphore("osd"))
    vst2 = es5.enter_context(nc.semaphore("vst2"))
    gd5 = es5.enter_context(nc.semaphore("gd5"))

    PTSC_AT_VALS = [8 * KB_PRE[s + 1] + 4 * (s + 1) for s in range(NSLOT)]
    with nc.Block() as blk:
        @blk.sync
        def _(sync):
            for s in range(NSLOT):
                kbc = KBCS[s]
                if s >= 2:
                    sync.wait_ge(smm, 8 * KB_PRE[s - 1])
                sync.dma_start(out=QS[s % 2][:],
                               in_=q_sc[:, :, s * 128:(s + 1) * 128].rearrange("q p t -> p q t")
                               ).then_inc(qld, 16)
                if s >= 2:
                    sync.wait_ge(msf, s - 1)
                sync.dma_start(out=MS[s % 2][:], in_=masks_in[s]).then_inc(msl, 16)
                for kb in range(kbc):
                    g = KB_PRE[s] + kb
                    if g >= 2:
                        sync.wait_ge(smm, 8 * (g - 1))
                    sync.dma_start(out=KBB[g % 2][:],
                                   in_=k_sc[:, :, kb * 512:(kb + 1) * 512].rearrange("c p t -> p c t")
                                   ).then_inc(kld, 16)
                for kb2 in range(kbc):
                    g2 = KB_PRE[s] + kb2
                    if g2 >= 2:
                        sync.wait_ge(avm, g2 - 1)
                    sync.dma_start(out=VBB[g2 % 2][:],
                                   in_=v_sc[kb2 * 4:kb2 * 4 + 4].rearrange("v p c -> p v c")
                                   ).then_inc(vld, 16)
                sync.wait_ge(ptsc, PTSC_AT_VALS[s])
                sync.dma_start(out=vals_sc[s],
                               in_=VALT[:].rearrange("p q t -> p (q t)")).then_inc(vst2, 16)
            sync.wait_ge(vst2, 16 * NSLOT)

        @blk.tensor
        def _(t):
            ptpu = [0]
            for s in range(NSLOT):
                kbc = KBCS[s]
                t.wait_ge(qld, 16 * (s + 1))
                for kb in range(kbc):
                    g = KB_PRE[s] + kb
                    t.wait_ge(kld, 16 * (g + 1))
                    for h in range(NH):
                        ghb = 8 * g + h
                        if ghb >= 2:
                            t.wait_ge(sev, ghb - 1)
                        for qi in range(2):
                            t.matmul(SPS[ghb % 2][:], QS[s % 2][:, 2 * h + qi, :],
                                     KBB[g % 2][:, 2 * h + qi, :],
                                     start=(qi == 0), stop=(qi == 1))
                        t.maybe_drain_then_inc((smm, 1))
                if s >= 1:
                    t.wait_ge(osd, s)
                for kb2 in range(kbc):
                    g2 = KB_PRE[s] + kb2
                    t.wait_ge(vld, 16 * (g2 + 1))
                    for h in range(NH):
                        t.wait_ge(expd, 8 * s + h + 1)
                        if ptpu[0] >= 2:
                            t.wait_ge(ptsc, ptpu[0] - 1)
                        for j in range(4):
                            t.transpose(PTP[ptpu[0] % 2][:, j, :],
                                        SALL[:, h, (kb2 * 4 + j) * 128:(kb2 * 4 + j) * 128 + 128],
                                        idfp[:])
                        t.maybe_drain_then_inc((ptm, 1))
                        ptpu[0] += 1
                        t.wait_ge(ptsc, ptpu[0])
                        for j in range(4):
                            t.matmul(OPP[h // 2][:, h % 2, :], PTS[h][:, j, :],
                                     VBB[g2 % 2][:, j, h * 256:(h + 1) * 256],
                                     start=(kb2 == 0 and j == 0 and h % 2 == 0),
                                     stop=(kb2 == kbc - 1 and j == 3))
                    t.maybe_drain_then_inc((avm, 1))
                t.wait_ge(osd, s + 1)
                for vb in range(4):
                    if ptpu[0] >= 2:
                        t.wait_ge(ptsc, ptpu[0] - 1)
                    for j in range(4):
                        cc = 4 * vb + j
                        t.transpose(PTP[ptpu[0] % 2][:, j, :],
                                    OSB[:, cc * 128:(cc + 1) * 128], idfp[:])
                    t.maybe_drain_then_inc((ptm, 1))
                    ptpu[0] += 1

        @blk.vector
        def _(v):
            g5 = [0]
            def G():
                g5[0] += 1
                v.maybe_drain_then_inc((gd5, 1))
                v.wait_ge(gd5, g5[0])
            for s in range(NSLOT):
                kbc = KBCS[s]
                diag = kbc - 1
                for kb in range(kbc):
                    g = KB_PRE[s] + kb
                    if kb == diag:
                        v.wait_ge(msl, 16 * (s + 1))
                    for h in range(NH):
                        ghb = 8 * g + h
                        v.wait_ge(smm, ghb + 1)
                        if kb == diag:
                            v.tensor_tensor(out=SPS[ghb % 2][:], in0=SPS[ghb % 2][:],
                                            in1=MS[s % 2][:], op=A.add)
                        v.tensor_scalar(out=SALL[:, h, kb * 512:(kb + 1) * 512],
                                        in0=SPS[ghb % 2][:], scalar1=0.0, scalar2=None,
                                        op0=A.add, op1=A.max,
                                        accum_out=MBAL[:, h * 8 + kb:h * 8 + kb + 1]
                                        ).then_inc(sev, 1)
                v.maybe_drain_then_inc((msf, 1))
                G()
                for h in range(NH):
                    v.reduce_max(MMAX[:, h:h + 1], MBAL[:, h * 8:h * 8 + kbc], axis=AX)
                G()
                v.tensor_scalar(out=NEGM[:], in0=MMAX[:], scalar1=-1.0, scalar2=None,
                                op0=A.mult)
                v.maybe_drain_then_inc((nmd, 1))
                for h in range(NH):
                    v.wait_ge(expd, 8 * s + h + 1)
                    v.reciprocal(RLV[:, h:h + 1], LSUM[:, h:h + 1])
                G()
                v.wait_ge(avm, KB_PRE[s + 1])
                for h in range(NH):
                    v.tensor_scalar(out=OSB[:, h * 256:(h + 1) * 256],
                                    in0=OPP[h // 2][:, h % 2, :],
                                    scalar1=RLV[:, h:h + 1], scalar2=None, op0=A.mult)
                v.maybe_drain_then_inc((osd, 1))

        @blk.scalar
        def _(act):
            ptpu = [0]
            for s in range(NSLOT):
                kbc = KBCS[s]
                klen = kbc * 512
                act.wait_ge(nmd, s + 1)
                for h in range(NH):
                    act.activation(out=SALL[:, h, 0:klen], in_=SALL[:, h, 0:klen],
                                   func=AF.Exp, bias=NEGM[:, h:h + 1], scale=1.0,
                                   accum_out=LSUM[:, h:h + 1])
                    act.maybe_drain_then_inc((expd, 1))
                for kb2 in range(kbc):
                    for h in range(NH):
                        act.wait_ge(ptm, ptpu[0] + 1)
                        act.copy(out=PTS[h][:], in_=PTP[ptpu[0] % 2][:])
                        act.maybe_drain_then_inc((ptsc, 1))
                        ptpu[0] += 1
                if s >= 1:
                    act.wait_ge(vst2, 16 * s)
                for vb in range(4):
                    act.wait_ge(ptm, ptpu[0] + 1)
                    act.copy(out=VALT[:, 4 * vb:4 * vb + 4, :], in_=PTP[ptpu[0] % 2][:])
                    act.maybe_drain_then_inc((ptsc, 1))
                    ptpu[0] += 1
    es5.close()
    if upto < 6:
        root.close(); return nc

    es6 = ExitStack()
    VALB = [es6.enter_context(nc.sbuf_tensor(f"VALB{i}", [128, NQC, 128], F16)) for i in range(4)]
    WOB = [es6.enter_context(nc.sbuf_tensor(f"WOB{i}", [128, 1024], BF)) for i in range(2)]
    SOT = [es6.enter_context(nc.sbuf_tensor(f"SOT{i}", [128, 1024], F32)) for i in range(2)]
    SPO = [es6.enter_context(nc.psum_tensor(f"SPO{i}", [128, 512], F32)) for i in range(8)]
    vll = es6.enter_context(nc.semaphore("vll"))
    wol = es6.enter_context(nc.semaphore("wol"))
    wof = es6.enter_context(nc.semaphore("wof"))
    somm = es6.enter_context(nc.semaphore("somm"))
    soev = es6.enter_context(nc.semaphore("soev"))
    sst = es6.enter_context(nc.semaphore("sst"))
    with nc.Block() as blk:
        @blk.sync
        def _(sync):
            for grp in range(4):
                if grp >= 1:
                    sync.wait_ge(somm, 4 * grp)
                for sl in range(4):
                    sync.dma_start(out=VALB[sl][:],
                                   in_=vals_sc[grp * 4 + sl].rearrange("p (q t) -> p q t", q=NQC)
                                   ).then_inc(vll, 16)
                for jq in range(4):
                    gj = grp * 4 + jq
                    for sl in range(4):
                        gsl = gj * 4 + sl
                        sync.wait_ge(soev, gsl + 1)
                        sync.dma_start(out=s_sc[grp * 4 + sl][:, jq * 1024:(jq + 1) * 1024],
                                       in_=SOT[gsl % 2][:]).then_inc(sst, 16)
            sync.wait_ge(sst, 16 * 64)

        @blk.gpsimd
        def _(g):
            wi = 0
            for grp in range(4):
                for jq in range(4):
                    for cc in range(NQC):
                        if wi >= 2:
                            g.wait_ge(wof, wi - 1)
                        g.dma_start(out=WOB[wi % 2][:],
                                    in_=wo[cc][:, jq * 1024:(jq + 1) * 1024]).then_inc(wol, 16)
                        wi += 1

        @blk.tensor
        def _(t):
            wi = 0
            for grp in range(4):
                t.wait_ge(vll, 16 * 4 * (grp + 1))
                for jq in range(4):
                    gj = grp * 4 + jq
                    if gj >= 1:
                        t.wait_ge(soev, 4 * gj)
                    for cc in range(NQC):
                        t.wait_ge(wol, 16 * (wi + 1))
                        for sl in range(4):
                            for jb in range(2):
                                mm = t.matmul(SPO[sl * 2 + jb][:], VALB[sl][:, cc, :],
                                              WOB[wi % 2][:, jb * 512:(jb + 1) * 512],
                                              start=(cc == 0), stop=(cc == NQC - 1))
                                if sl == 3 and jb == 1:
                                    mm.then_inc(wof, 1)
                        wi += 1
                    t.maybe_drain_then_inc((somm, 1))

        @blk.vector
        def _(v):
            for grp in range(4):
                for jq in range(4):
                    gj = grp * 4 + jq
                    v.wait_ge(somm, gj + 1)
                    for sl in range(4):
                        gsl = gj * 4 + sl
                        if gsl >= 2:
                            v.wait_ge(sst, 16 * (gsl - 1))
                        for jb in range(2):
                            v.tensor_copy(SOT[gsl % 2][:, jb * 512:(jb + 1) * 512],
                                          SPO[sl * 2 + jb][:])
                        v.maybe_drain_then_inc((soev, 1))
    es6.close()
    if upto < 7:
        root.close(); return nc

    es7 = ExitStack()
    BOS = es7.enter_context(nc.sbuf_tensor("BOS", [1, D], BF))
    BOB = es7.enter_context(nc.sbuf_tensor("BOB", [128, D], BF))
    DT = es7.enter_context(nc.sbuf_tensor("DT", [128, NIC, 1024], BF))
    HTA = es7.enter_context(nc.sbuf_tensor("HTA", [128, 28, 512], BF))
    HTB = es7.enter_context(nc.sbuf_tensor("HTB", [128, 28, 512], BF))
    SS = es7.enter_context(nc.sbuf_tensor("SS", [128, 1024], F32))
    XS = es7.enter_context(nc.sbuf_tensor("XS", [128, 1024], F32))
    GG = [es7.enter_context(nc.sbuf_tensor(f"GG{i}", [128, 512], F32)) for i in range(2)]
    SFB = es7.enter_context(nc.sbuf_tensor("SFB", [128, 1024], F32))
    OUTB = [es7.enter_context(nc.sbuf_tensor(f"OUTB{i}", [128, 512], F32)) for i in range(2)]
    PSD = [es7.enter_context(nc.psum_tensor(f"PSD{i}", [128, 512], F32)) for i in range(8)]
    bld = es7.enter_context(nc.semaphore("bld"))
    bpm7 = es7.enter_context(nc.semaphore("bpm7"))
    bobd = es7.enter_context(nc.semaphore("bobd"))
    dld = es7.enter_context(nc.semaphore("dld"))
    hld = es7.enter_context(nc.semaphore("hld"))
    haf = es7.enter_context(nc.semaphore("haf"))
    hbf = es7.enter_context(nc.semaphore("hbf"))
    dmm = es7.enter_context(nc.semaphore("dmm"))
    gsd = es7.enter_context(nc.semaphore("gsd"))
    gfr = es7.enter_context(nc.semaphore("gfr"))
    sxl = es7.enter_context(nc.semaphore("sxl"))
    ssfre = es7.enter_context(nc.semaphore("ssfre"))
    outd = es7.enter_context(nc.semaphore("outd"))
    ost = es7.enter_context(nc.semaphore("ost"))
    with nc.Block() as blk:
        @blk.gpsimd
        def _(g):
            g.dma_start(out=BOS[:], in_=bo_in).then_inc(bld, 16)
            for jq in range(4):
                if jq >= 1:
                    g.wait_ge(dmm, 4 * jq)
                for pc in range(8):
                    g.dma_start(out=DT[:, 7 * pc:7 * pc + 7, :],
                                in_=wd[7 * pc:7 * pc + 7, :, jq * 1024:(jq + 1) * 1024]
                                .rearrange("i p j -> p i j")).then_inc(dld, 16)
                for tt in range(4):
                    gtt = 4 * jq + tt
                    if gtt >= 1:
                        g.wait_ge(haf, gtt)
                    for pc in range(4):
                        g.dma_start(out=HTA[:, 7 * pc:7 * pc + 7, :],
                                    in_=h_sc[7 * pc:7 * pc + 7, :, tt * 512:(tt + 1) * 512]
                                    .rearrange("i p t -> p i t")).then_inc(hld, 16)
                    if gtt >= 1:
                        g.wait_ge(hbf, gtt)
                    for pc in range(4):
                        g.dma_start(out=HTB[:, 7 * pc:7 * pc + 7, :],
                                    in_=h_sc[28 + 7 * pc:28 + 7 * pc + 7, :, tt * 512:(tt + 1) * 512]
                                    .rearrange("i p t -> p i t")).then_inc(hld, 16)

        @blk.tensor
        def _(t):
            t.wait_ge(bld, 16)
            for jb in range(8):
                t.matmul(PSD[jb][:], ones_bf[:], BOS[:, jb * 512:(jb + 1) * 512],
                         start=True, stop=True)
            t.maybe_drain_then_inc((bpm7, 1))
            for jq in range(4):
                t.wait_ge(dld, 16 * 8 * (jq + 1))
                for tt in range(4):
                    gtt = 4 * jq + tt
                    t.wait_ge(hld, 16 * 4 * (2 * gtt + 1))
                    if gtt == 0:
                        t.wait_ge(bobd, 1)
                    else:
                        t.wait_ge(gsd, 8 * gtt)
                    for ic in range(NIC):
                        if ic == 28:
                            t.wait_ge(hld, 16 * 4 * (2 * gtt + 2))
                        ht = HTA if ic < 28 else HTB
                        icl = ic if ic < 28 else ic - 28
                        for tsub in range(4):
                            for jb in range(2):
                                mm = t.matmul(PSD[tsub * 2 + jb][:],
                                              ht[:, icl, tsub * 128:(tsub + 1) * 128],
                                              DT[:, ic, jb * 512:(jb + 1) * 512],
                                              start=(ic == 0), stop=(ic == NIC - 1))
                                if tsub == 3 and jb == 1 and ic == 27:
                                    mm.then_inc(haf, 1)
                                if tsub == 3 and jb == 1 and ic == NIC - 1:
                                    mm.then_inc(hbf, 1)

                    t.maybe_drain_then_inc((dmm, 1))

        @blk.sync
        def _(sync):
            for jq in range(4):
                for tt in range(4):
                    for tsub in range(4):
                        ts7 = (4 * jq + tt) * 4 + tsub
                        slot = tt * 4 + tsub
                        if ts7 >= 1:
                            sync.wait_ge(ssfre, ts7)
                        sync.dma_start(out=SS[:],
                                       in_=s_sc[slot][:, jq * 1024:(jq + 1) * 1024]
                                       ).then_inc(sxl, 16)
                        sync.dma_start(out=XS[:],
                                       in_=x_mine[slot][:, jq * 1024:(jq + 1) * 1024]
                                       ).then_inc(sxl, 16)
                        for jb in range(2):
                            ko = ts7 * 2 + jb
                            sync.wait_ge(outd, ko + 1)
                            sync.dma_start(out=out_mine[slot][:, jq * 1024 + jb * 512:
                                                              jq * 1024 + jb * 512 + 512],
                                           in_=OUTB[ko % 2][:]).then_inc(ost, 16)
            sync.wait_ge(ost, 16 * 128)

        @blk.scalar
        def _(act):
            for jq in range(4):
                for tt in range(4):
                    gtt = 4 * jq + tt
                    act.wait_ge(dmm, gtt + 1)
                    for tsub in range(4):
                        for jb in range(2):
                            k = gtt * 8 + tsub * 2 + jb
                            if k >= 2:
                                act.wait_ge(gfr, k - 1)
                            act.activation(out=GG[k % 2][:], in_=PSD[tsub * 2 + jb][:],
                                           func=AF.Silu, bias=0.0, scale=1.0)
                            act.maybe_drain_then_inc((gsd, 1))

        @blk.vector
        def _(v):
            v.wait_ge(bpm7, 1)
            for jb in range(8):
                v.tensor_copy(BOB[:, jb * 512:(jb + 1) * 512], PSD[jb][:])
            v.maybe_drain_then_inc((bobd, 1))
            for jq in range(4):
                for tt in range(4):
                    for tsub in range(4):
                        ts7 = (4 * jq + tt) * 4 + tsub
                        v.wait_ge(sxl, 32 * (ts7 + 1))
                        v.tensor_tensor(out=SFB[:], in0=SS[:],
                                        in1=BOB[:, jq * 1024:(jq + 1) * 1024], op=A.add)
                        for jb in range(2):
                            k = (4 * jq + tt) * 8 + tsub * 2 + jb
                            ko = ts7 * 2 + jb
                            v.wait_ge(gsd, k + 1)
                            if ko >= 2:
                                v.wait_ge(ost, 16 * (ko - 1))
                            v.tensor_tensor(out=OUTB[ko % 2][:],
                                            in0=SFB[:, jb * 512:(jb + 1) * 512],
                                            in1=GG[k % 2][:], op=A.mult).then_inc(gfr, 1)
                            last = v.tensor_tensor(out=OUTB[ko % 2][:], in0=OUTB[ko % 2][:],
                                                   in1=XS[:, jb * 512:(jb + 1) * 512],
                                                   op=A.add)
                            if jb == 1:
                                last.then_inc(ssfre, 1)
                            v.maybe_drain_then_inc((outd, 1))
    es7.close()
    root.close()
    return nc



def _lay_w(wt):
    d_in, d_out = wt.shape
    ndc = d_in // 128
    noc = d_out // 128
    return np.ascontiguousarray(wt.reshape(ndc, 128, noc, 128).transpose(2, 1, 0, 3))


def _host_prepare(inputs):
    x = np.asarray(inputs["L0_attn_x"], np.float32)
    w_qkv = np.asarray(inputs["L0_attn_w_qkv"], np.float32)
    b_qkv = np.asarray(inputs["L0_attn_b_qkv"], np.float32)
    w_o = np.asarray(inputs["L0_attn_w_o"], np.float32)
    b_o = np.asarray(inputs["L0_attn_b_o"], np.float32)
    gate_w = np.asarray(inputs["L0_ffn_gate_w"], np.float32)
    u = np.asarray(inputs["L0_ffn_u"], np.float32)
    g = np.asarray(inputs["L0_ffn_g"], np.float32)
    d = np.asarray(inputs["L0_ffn_d"], np.float32)

    shared = {}
    shared["wq"] = _lay_w(np.ascontiguousarray(w_qkv[0:2048].T))
    shared["wkv"] = _lay_w(np.ascontiguousarray(w_qkv[2048:6144].T))
    shared["wu"] = _lay_w(np.ascontiguousarray(u.T))
    shared["wg"] = _lay_w(np.ascontiguousarray(g.T))
    shared["wd"] = np.ascontiguousarray(d.T.reshape(NIC, 128, D))
    shared["wo"] = np.ascontiguousarray(w_o.T.reshape(NQC, 128, D))
    shared["gwt"] = np.ascontiguousarray(gate_w.T.reshape(NDC, 128, NE).transpose(1, 0, 2))
    shared["bq_in"] = np.ascontiguousarray(b_qkv[0:2048].reshape(NQC, 128).T)
    shared["bkv_in"] = np.ascontiguousarray(b_qkv[2048:6144].reshape(NVC, 128).T)
    shared["bo_in"] = np.ascontiguousarray(b_o.reshape(1, D))
    shared["eye_in"] = np.eye(128, dtype=np.float32)
    shared["ones_in"] = np.ones((1, 128), np.float32)
    smat = np.zeros((NE, NIC * 128), np.float32)
    rows = np.arange(NIC * 128)
    smat[rows // 448, rows] = 1.0
    shared["smat_in"] = smat

    def masks_for(parity):
        m = np.zeros((NSLOT, 128, 512), np.float32)
        for s in range(NSLOT):
            qt = 2 * s + parity
            base = (KBCS[s] - 1) * 512
            qpos = qt * 128 + np.arange(128)[:, None]
            kpos = base + np.arange(512)[None, :]
            m[s] = np.where(kpos <= qpos, 0.0, -30000.0).astype(np.float32)
        return m

    masks = [masks_for(0), masks_for(1)]

    in_maps = []
    for c in range(8):
        b, p = c // 2, c % 2
        xb = x[b].reshape(32, 128, D)
        im = dict(shared)
        im["x_full"] = np.ascontiguousarray(xb)
        im["x_mine"] = np.ascontiguousarray(xb[p::2])
        im["masks_in"] = masks[p]
        for kk in im:
            assert im[kk].dtype == np.float32, (kk, im[kk].dtype)
        in_maps.append(im)
    return in_maps


def kernel(**inputs):
    in_maps = _host_prepare(inputs)
    nc = build_nc(dbg=False)
    res = run_bass_kernel_spmd(nc, in_maps, core_ids=list(range(8)))
    out = np.empty((4, SFULL, D), np.float32)
    for c in range(8):
        b, p = c // 2, c % 2
        om = np.asarray(res.results[c]["out_mine"])
        ov = out[b].reshape(32, 128, D)
        ov[p::2] = om
    return out


# version 03
# speedup vs baseline: 1.1321x; 1.1321x over previous
import numpy as np
import ml_dtypes
import concourse.bass as bass
import concourse.mybir as mybir
from concourse.alu_op_type import AluOpType as A
from concourse.bass_utils import run_bass_kernel_spmd
from contextlib import ExitStack

F32 = mybir.dt.float32
BF = mybir.dt.bfloat16
F16 = mybir.dt.float16
AF = mybir.ActivationFunctionType
AX = mybir.AxisListType.X

D = 4096
NDC = 32
TOK = 2048
NSLOT = 16
SFULL = 4096
NQC = 16
NVC = 32
NIC = 56
NH = 8
HD = 256
NE = 16
EPS = 1e-5

KBCS = [(s + 2) // 2 for s in range(NSLOT)]
KB_PRE = [0]
for s in range(NSLOT):
    KB_PRE.append(KB_PRE[-1] + KBCS[s])
KB_TOT = KB_PRE[-1]


def build_nc(dbg=False, upto=99):
    nc = bass.Bass("TRN2", target_bir_lowering=False)

    x_full = nc.dram_tensor("x_full", [32, 128, D], F32, kind="ExternalInput").ap()
    x_mine = nc.dram_tensor("x_mine", [NSLOT, 128, D], F32, kind="ExternalInput").ap()
    wq = nc.dram_tensor("wq", [NQC, 128, NDC, 128], F32, kind="ExternalInput").ap()
    wkv = nc.dram_tensor("wkv", [NVC, 128, NDC, 128], F32, kind="ExternalInput").ap()
    wu = nc.dram_tensor("wu", [NIC, 128, NDC, 128], F32, kind="ExternalInput").ap()
    wg = nc.dram_tensor("wg", [NIC, 128, NDC, 128], F32, kind="ExternalInput").ap()
    wd = nc.dram_tensor("wd", [NIC, 128, D], F32, kind="ExternalInput").ap()
    wo = nc.dram_tensor("wo", [NQC, 128, D], F32, kind="ExternalInput").ap()
    gwt = nc.dram_tensor("gwt", [128, NDC, NE], F32, kind="ExternalInput").ap()
    bq_in = nc.dram_tensor("bq_in", [128, NQC], F32, kind="ExternalInput").ap()
    bkv_in = nc.dram_tensor("bkv_in", [128, NVC], F32, kind="ExternalInput").ap()
    bo_in = nc.dram_tensor("bo_in", [1, D], F32, kind="ExternalInput").ap()
    eye_in = nc.dram_tensor("eye_in", [128, 128], F32, kind="ExternalInput").ap()
    ones_in = nc.dram_tensor("ones_in", [1, 128], F32, kind="ExternalInput").ap()
    smat_in = nc.dram_tensor("smat_in", [NE, NIC * 128], F32, kind="ExternalInput").ap()
    masks_in = nc.dram_tensor("masks_in", [NSLOT, 128, 512], F32, kind="ExternalInput").ap()

    out_mine = nc.dram_tensor("out_mine", [NSLOT, 128, D], F32, kind="ExternalOutput").ap()

    sk = "ExternalOutput" if dbg else "Internal"
    q_sc = nc.dram_tensor("q_sc", [NQC, 128, TOK], BF, kind=sk).ap()
    k_sc = nc.dram_tensor("k_sc", [NQC, 128, SFULL], BF, kind=sk).ap()
    v_sc = nc.dram_tensor("v_sc", [32, 128, 2048], BF, kind=sk).ap()
    h_sc = nc.dram_tensor("h_sc", [NIC, 128, TOK], BF, kind=sk).ap()
    s_sc = nc.dram_tensor("s_sc", [NSLOT, 128, D], F32, kind=sk).ap()
    vals_sc = nc.dram_tensor("vals_sc", [NSLOT, 128, TOK], F16, kind=sk).ap()
    if dbg:
        xn_dbg = nc.dram_tensor("xn_dbg", [128, NDC, TOK], BF, kind="ExternalOutput").ap()
        wtb_dbg = nc.dram_tensor("wtb_dbg", [NE, TOK], BF, kind="ExternalOutput").ap()

    root = ExitStack()
    idbf = root.enter_context(nc.sbuf_tensor("idbf", [128, 128], BF))
    idfp = root.enter_context(nc.sbuf_tensor("idfp", [128, 128], F16))
    ones_bf = root.enter_context(nc.sbuf_tensor("ones_bf", [1, 128], BF))

    BQ = root.enter_context(nc.sbuf_tensor("BQ", [128, NQC], F32))
    BKV = root.enter_context(nc.sbuf_tensor("BKV", [128, NVC], F32))
    ld1 = root.enter_context(nc.semaphore("ld1"))
    ld2 = root.enter_context(nc.semaphore("ld2"))

    mid = ExitStack()
    XN = mid.enter_context(nc.sbuf_tensor("XN", [128, NDC, TOK], BF))
    SQD = mid.enter_context(nc.sbuf_tensor("SQD", [128, D], BF))
    mid2 = ExitStack()
    WTB = mid2.enter_context(nc.sbuf_tensor("WTB", [NE, TOK], BF))
    S_sb = mid2.enter_context(nc.sbuf_tensor("S_sb", [NE, NIC * 128], BF))
    GWS = mid2.enter_context(nc.sbuf_tensor("GWS", [128, NDC, NE], BF))

    def ln_alloc(es, tag):
        b = {}
        b["xsb"] = [es.enter_context(nc.sbuf_tensor(f"{tag}xsb{i}", [128, D], F32)) for i in range(2)]
        b["xnt"] = es.enter_context(nc.sbuf_tensor(f"{tag}xnt", [128, D], BF))
        b["ssq"] = [es.enter_context(nc.sbuf_tensor(f"{tag}ssq{i}", [128, 1], F32)) for i in range(2)]
        b["sm"] = es.enter_context(nc.sbuf_tensor(f"{tag}sm", [128, 1], F32))
        b["mn"] = [es.enter_context(nc.sbuf_tensor(f"{tag}mn{i}", [128, 1], F32)) for i in range(2)]
        b["ex2"] = es.enter_context(nc.sbuf_tensor(f"{tag}ex2", [128, 1], F32))
        b["msq"] = es.enter_context(nc.sbuf_tensor(f"{tag}msq", [128, 1], F32))
        b["var"] = es.enter_context(nc.sbuf_tensor(f"{tag}var", [128, 1], F32))
        b["ve"] = es.enter_context(nc.sbuf_tensor(f"{tag}ve", [128, 1], F32))
        b["rv"] = [es.enter_context(nc.sbuf_tensor(f"{tag}rv{i}", [128, 1], F32)) for i in range(2)]
        b["rsd"] = [es.enter_context(nc.sbuf_tensor(f"{tag}rsd{i}", [128, 1], F32)) for i in range(2)]
        b["pt"] = [es.enter_context(nc.psum_tensor(f"{tag}pt{i}", [128, 4, 128], BF)) for i in range(2)]
        for s in ["ldx", "fre", "sqs", "rvd", "rsb", "xnr", "ptd", "evd", "gd"]:
            b[s] = es.enter_context(nc.semaphore(f"{tag}{s}"))
        b["gdn"] = [0]
        return b

    def ln_sync(sync, b, x_ap, tiles, g0):
        for i, t in enumerate(tiles):
            gs = g0 + i
            if gs >= 2:
                sync.wait_ge(b["fre"], 2 * (gs - 1))
            sync.dma_start(out=b["xsb"][gs % 2][:], in_=x_ap[t]).then_inc(b["ldx"], 16)

    def ln_act(act, b, nsub, g0):
        for i in range(nsub):
            gs = g0 + i
            act.wait_ge(b["ldx"], 16 * (gs + 1))
            act.activation(out=SQD[:], in_=b["xsb"][gs % 2][:], func=AF.Square,
                           bias=0.0, scale=1.0,
                           accum_out=b["ssq"][gs % 2][:]).then_inc(b["fre"], 1)
            act.maybe_drain_then_inc((b["sqs"], 1))
            act.wait_ge(b["rvd"], gs + 1)
            act.activation(out=b["rsd"][gs % 2][:], in_=b["rv"][gs % 2][:], func=AF.Sqrt,
                           bias=0.0, scale=1.0)
            act.maybe_drain_then_inc((b["rsb"], 1))
            for bt in range(8):
                gb = 8 * gs + bt
                act.wait_ge(b["ptd"], gb + 1)
                act.copy(out=XN[:, 4 * bt:4 * bt + 4, i * 128:(i + 1) * 128],
                         in_=b["pt"][bt % 2][:]).then_inc(b["evd"], 1)

    def ln_dve(v, b, nsub, g0):
        def G():
            b["gdn"][0] += 1
            v.maybe_drain_then_inc((b["gd"], 1))
            v.wait_ge(b["gd"], b["gdn"][0])
        for i in range(nsub):
            gs = g0 + i
            v.wait_ge(b["ldx"], 16 * (gs + 1))
            v.reduce_sum(b["sm"][:], b["xsb"][gs % 2][:], axis=AX)
            G()
            v.tensor_scalar(out=b["mn"][gs % 2][:], in0=b["sm"][:], scalar1=1.0 / D,
                            scalar2=None, op0=A.mult)
            v.wait_ge(b["sqs"], gs + 1)
            v.tensor_scalar(out=b["ex2"][:], in0=b["ssq"][gs % 2][:], scalar1=1.0 / D,
                            scalar2=None, op0=A.mult)
            G()
            v.tensor_tensor(out=b["msq"][:], in0=b["mn"][gs % 2][:], in1=b["mn"][gs % 2][:], op=A.mult)
            G()
            v.tensor_tensor(out=b["var"][:], in0=b["ex2"][:], in1=b["msq"][:], op=A.subtract)
            G()
            v.tensor_scalar(out=b["ve"][:], in0=b["var"][:], scalar1=EPS, scalar2=None, op0=A.add)
            G()
            v.reciprocal(b["rv"][gs % 2][:], b["ve"][:])
            v.maybe_drain_then_inc((b["rvd"], 1))
            v.wait_ge(b["rsb"], gs + 1)
            if gs >= 1:
                v.wait_ge(b["ptd"], 8 * gs)
            v.tensor_scalar(out=b["xnt"][:], in0=b["xsb"][gs % 2][:],
                            scalar1=b["mn"][gs % 2][:], scalar2=b["rsd"][gs % 2][:],
                            op0=A.subtract, op1=A.mult).then_inc(b["fre"], 1)
            v.maybe_drain_then_inc((b["xnr"], 1))

    def ln_pe(t, b, nsub, g0):
        for i in range(nsub):
            gs = g0 + i
            t.wait_ge(b["xnr"], gs + 1)
            for bt in range(8):
                gb = 8 * gs + bt
                if gb >= 2:
                    t.wait_ge(b["evd"], gb - 1)
                for j in range(4):
                    t.transpose(b["pt"][bt % 2][:, j, :],
                                b["xnt"][:, (4 * bt + j) * 128:(4 * bt + j) * 128 + 128],
                                idbf[:])
                t.maybe_drain_then_inc((b["ptd"], 1))

    es0 = ExitStack()
    lnb = ln_alloc(es0, "l0")
    with nc.Block() as blk:
        @blk.gpsimd
        def _(g):
            g.dma_start(out=idbf[:], in_=eye_in).then_inc(ld1, 16)
            g.dma_start(out=idfp[:], in_=eye_in).then_inc(ld1, 16)
            g.dma_start(out=ones_bf[:], in_=ones_in).then_inc(ld1, 16)
            g.dma_start(out=GWS[:], in_=gwt).then_inc(ld1, 16)
            g.dma_start(out=S_sb[:], in_=smat_in).then_inc(ld1, 16)

        @blk.sync
        def _(sync):
            sync.dma_start(out=BQ[:], in_=bq_in).then_inc(ld2, 16)
            sync.dma_start(out=BKV[:], in_=bkv_in).then_inc(ld2, 16)
            ln_sync(sync, lnb, x_mine, list(range(16)), 0)

        @blk.scalar
        def _(act):
            ln_act(act, lnb, 16, 0)

        @blk.vector
        def _(v):
            ln_dve(v, lnb, 16, 0)

        @blk.tensor
        def _(t):
            t.wait_ge(ld1, 16)
            ln_pe(t, lnb, 16, 0)
    es0.close()
    if upto < 1:
        mid2.close(); mid.close(); root.close(); return nc

    es1 = ExitStack()
    WQB = [es1.enter_context(nc.sbuf_tensor(f"WQB{i}", [128, NDC, 128], BF)) for i in range(2)]
    QTILE = [es1.enter_context(nc.sbuf_tensor(f"QTILE{i}", [128, TOK], BF)) for i in range(2)]
    QP = [es1.enter_context(nc.psum_tensor(f"QP{i}", [128, 512], F32)) for i in range(8)]
    wld = es1.enter_context(nc.semaphore("wld"))
    qmm = es1.enter_context(nc.semaphore("qmm"))
    qtl = es1.enter_context(nc.semaphore("qtl"))
    qst = es1.enter_context(nc.semaphore("qst"))
    with nc.Block() as blk:
        @blk.gpsimd
        def _(g):
            for qc in range(NQC):
                if qc >= 2:
                    g.wait_ge(qmm, qc - 1)
                g.dma_start(out=WQB[qc % 2][:], in_=wq[qc]).then_inc(wld, 16)

        @blk.tensor
        def _(t):
            for qc in range(NQC):
                t.wait_ge(wld, 16 * (qc + 1))
                if qc >= 2:
                    t.wait_ge(qtl, qc - 1)
                bs = (qc % 2) * 4
                for dc in range(NDC):
                    for ts in range(4):
                        t.matmul(QP[bs + ts][:], WQB[qc % 2][:, dc, :],
                                 XN[:, dc, ts * 512:(ts + 1) * 512],
                                 start=(dc == 0), stop=(dc == NDC - 1))
                t.maybe_drain_then_inc((qmm, 1))

        @blk.vector
        def _(v):
            v.wait_ge(ld2, 16)
            for qc in range(NQC):
                v.wait_ge(qmm, qc + 1)
                if qc >= 2:
                    v.wait_ge(qst, 16 * (qc - 1))
                bs = (qc % 2) * 4
                for ts in range(4):
                    v.tensor_scalar(out=QTILE[qc % 2][:, ts * 512:(ts + 1) * 512],
                                    in0=QP[bs + ts][:], scalar1=BQ[:, qc:qc + 1],
                                    scalar2=None, op0=A.add)
                v.maybe_drain_then_inc((qtl, 1))

        @blk.sync
        def _(sync):
            for qc in range(NQC):
                sync.wait_ge(qtl, qc + 1)
                sync.dma_start(out=q_sc[qc], in_=QTILE[qc % 2][:]).then_inc(qst, 16)
            sync.wait_ge(qst, 16 * NQC)
            if dbg and upto == 1:
                sync.dma_start(out=xn_dbg, in_=XN[:]).then_inc(qst, 16)
                sync.wait_ge(qst, 16 * NQC + 16)
    es1.close()
    if upto < 2:
        mid2.close(); mid.close(); root.close(); return nc

    es2 = ExitStack()
    RPB = es2.enter_context(nc.psum_tensor("RPB", [128, 256], F32))
    TPW = [es2.enter_context(nc.psum_tensor(f"TPW{i}", [16, 128], BF)) for i in range(2)]
    WBF = es2.enter_context(nc.sbuf_tensor("WBF", [128, 256], BF))
    rlg = es2.enter_context(nc.sbuf_tensor("rlg", [128, NE], F32))
    rm1 = es2.enter_context(nc.sbuf_tensor("rm1", [128, 1], F32))
    rnm = es2.enter_context(nc.sbuf_tensor("rnm", [128, 1], F32))
    rmsk = es2.enter_context(nc.sbuf_tensor("rmsk", [128, NE], F32))
    rlm = es2.enter_context(nc.sbuf_tensor("rlm", [128, NE], F32))
    rm2 = es2.enter_context(nc.sbuf_tensor("rm2", [128, 1], F32))
    rsel = es2.enter_context(nc.sbuf_tensor("rsel", [128, NE], F32))
    rel = es2.enter_context(nc.sbuf_tensor("rel", [128, NE], F32))
    rwn = es2.enter_context(nc.sbuf_tensor("rwn", [128, NE], F32))
    rden = es2.enter_context(nc.sbuf_tensor("rden", [128, 1], F32))
    rrd = es2.enter_context(nc.sbuf_tensor("rrd", [128, 1], F32))
    rwf = es2.enter_context(nc.sbuf_tensor("rwf", [128, NE], F32))
    rmm = es2.enter_context(nc.semaphore("rmm"))
    rwb = es2.enter_context(nc.semaphore("rwb"))
    rtw = es2.enter_context(nc.semaphore("rtw"))
    rwt = es2.enter_context(nc.semaphore("rwt"))
    rex = es2.enter_context(nc.semaphore("rex"))
    rnd = es2.enter_context(nc.semaphore("rnd"))
    rgd = es2.enter_context(nc.semaphore("rgd"))
    with nc.Block() as blk:
        @blk.tensor
        def _(t):
            t.wait_ge(ld1, 80)
            for sub in range(16):
                for dc in range(NDC):
                    t.matmul(RPB[:, sub * 16:sub * 16 + 16],
                             XN[:, dc, sub * 128:sub * 128 + 128],
                             GWS[:, dc, :], start=(dc == 0), stop=(dc == NDC - 1))
            t.maybe_drain_then_inc((rmm, 1))
            for sub in range(16):
                t.wait_ge(rwb, sub + 1)
                if sub >= 2:
                    t.wait_ge(rwt, sub - 1)
                t.transpose(TPW[sub % 2][:], WBF[:, sub * 16:sub * 16 + 16], idbf[:])
                t.maybe_drain_then_inc((rtw, 1))

        @blk.vector
        def _(v):
            g = [0]
            def G():
                g[0] += 1
                v.maybe_drain_then_inc((rgd, 1))
                v.wait_ge(rgd, g[0])
            v.wait_ge(rmm, 1)
            for sub in range(16):
                v.tensor_copy(rlg[:], RPB[:, sub * 16:sub * 16 + 16])
                G()
                v.reduce_max(rm1[:], rlg[:], axis=AX)
                G()
                v.tensor_scalar(out=rnm[:], in0=rm1[:], scalar1=-1.0, scalar2=None, op0=A.mult)
                v.tensor_scalar(out=rmsk[:], in0=rlg[:], scalar1=rm1[:], scalar2=None, op0=A.is_ge)
                G()
                v.scalar_tensor_tensor(out=rlm[:], in0=rmsk[:], scalar=-30000.0, in1=rlg[:],
                                       op0=A.mult, op1=A.add)
                G()
                v.reduce_max(rm2[:], rlm[:], axis=AX)
                G()
                v.tensor_scalar(out=rsel[:], in0=rlg[:], scalar1=rm2[:], scalar2=None, op0=A.is_ge)
                v.maybe_drain_then_inc((rnd, 1))
                v.wait_ge(rex, sub + 1)
                v.tensor_tensor(out=rwn[:], in0=rel[:], in1=rsel[:], op=A.mult)
                G()
                v.reduce_sum(rden[:], rwn[:], axis=AX)
                G()
                v.reciprocal(rrd[:], rden[:])
                G()
                v.tensor_scalar(out=rwf[:], in0=rwn[:], scalar1=rrd[:], scalar2=None, op0=A.mult)
                G()
                v.tensor_copy(WBF[:, sub * 16:sub * 16 + 16], rwf[:])
                v.maybe_drain_then_inc((rwb, 1))
            for sub in range(16):
                v.wait_ge(rtw, sub + 1)
                v.tensor_copy(WTB[:, sub * 128:(sub + 1) * 128], TPW[sub % 2][:])
                v.maybe_drain_then_inc((rwt, 1))

        @blk.scalar
        def _(act):
            for sub in range(16):
                act.wait_ge(rnd, sub + 1)
                act.activation(out=rel[:], in_=rlg[:], func=AF.Exp, bias=rnm[:], scale=1.0)
                act.maybe_drain_then_inc((rex, 1))
    es2.close()
    if upto < 3:
        mid2.close(); mid.close(); root.close(); return nc

    es3 = ExitStack()
    UB = [es3.enter_context(nc.sbuf_tensor(f"UB{i}", [128, NDC, 128], BF)) for i in range(2)]
    GB = [es3.enter_context(nc.sbuf_tensor(f"GB{i}", [128, NDC, 128], BF)) for i in range(2)]
    SG = es3.enter_context(nc.sbuf_tensor("SG", [128, TOK], BF))
    H1 = es3.enter_context(nc.sbuf_tensor("H1", [128, TOK], BF))
    HW = [es3.enter_context(nc.sbuf_tensor(f"HW{i}", [128, TOK], BF)) for i in range(2)]
    UP = [es3.enter_context(nc.psum_tensor(f"UP{i}", [128, 512], F32)) for i in range(4)]
    GP = [es3.enter_context(nc.psum_tensor(f"GP{i}", [128, 512], F32)) for i in range(4)]
    uld = es3.enter_context(nc.semaphore("uld"))
    ugm = es3.enter_context(nc.semaphore("ugm"))
    wpm = es3.enter_context(nc.semaphore("wpm"))
    sgd = es3.enter_context(nc.semaphore("sgd"))
    h1d = es3.enter_context(nc.semaphore("h1d"))
    hwd = es3.enter_context(nc.semaphore("hwd"))
    hst = es3.enter_context(nc.semaphore("hst"))
    with nc.Block() as blk:
        @blk.gpsimd
        def _(g):
            for ic in range(NIC):
                if ic >= 2:
                    g.wait_ge(ugm, ic - 1)
                g.dma_start(out=UB[ic % 2][:], in_=wu[ic]).then_inc(uld, 16)
                g.dma_start(out=GB[ic % 2][:], in_=wg[ic]).then_inc(uld, 16)

        @blk.tensor
        def _(t):
            for ic in range(NIC):
                t.wait_ge(uld, 32 * (ic + 1))
                if ic >= 1:
                    t.wait_ge(hwd, ic)
                    t.wait_ge(sgd, ic)
                for dc in range(NDC):
                    for ts in range(4):
                        t.matmul(UP[ts][:], UB[ic % 2][:, dc, :],
                                 XN[:, dc, ts * 512:(ts + 1) * 512],
                                 start=(dc == 0), stop=(dc == NDC - 1))
                    for ts in range(4):
                        t.matmul(GP[ts][:], GB[ic % 2][:, dc, :],
                                 XN[:, dc, ts * 512:(ts + 1) * 512],
                                 start=(dc == 0), stop=(dc == NDC - 1))
                t.maybe_drain_then_inc((ugm, 1))
                t.wait_ge(h1d, ic + 1)
                for ts in range(4):
                    t.matmul(UP[ts][:], S_sb[:, ic * 128:(ic + 1) * 128],
                             WTB[:, ts * 512:(ts + 1) * 512], start=True, stop=True)
                t.maybe_drain_then_inc((wpm, 1))

        @blk.scalar
        def _(act):
            for ic in range(NIC):
                act.wait_ge(ugm, ic + 1)
                for ts in range(4):
                    act.activation(out=SG[:, ts * 512:(ts + 1) * 512], in_=GP[ts][:],
                                   func=AF.Silu, bias=0.0, scale=1.0)
                act.maybe_drain_then_inc((sgd, 1))

        @blk.vector
        def _(v):
            for ic in range(NIC):
                v.wait_ge(sgd, ic + 1)
                for ts in range(4):
                    v.tensor_tensor(out=H1[:, ts * 512:(ts + 1) * 512], in0=UP[ts][:],
                                    in1=SG[:, ts * 512:(ts + 1) * 512], op=A.mult)
                v.maybe_drain_then_inc((h1d, 1))
                v.wait_ge(wpm, ic + 1)
                if ic >= 2:
                    v.wait_ge(hst, 16 * (ic - 1))
                for ts in range(4):
                    v.tensor_tensor(out=HW[ic % 2][:, ts * 512:(ts + 1) * 512], in0=UP[ts][:],
                                    in1=H1[:, ts * 512:(ts + 1) * 512], op=A.mult)
                v.maybe_drain_then_inc((hwd, 1))

        @blk.sync
        def _(sync):
            for ic in range(NIC):
                sync.wait_ge(hwd, ic + 1)
                sync.dma_start(out=h_sc[ic], in_=HW[ic % 2][:]).then_inc(hst, 16)
            sync.wait_ge(hst, 16 * NIC)
            if dbg:
                sync.dma_start(out=xn_dbg, in_=XN[:]).then_inc(hst, 16)
                sync.dma_start(out=wtb_dbg, in_=WTB[:]).then_inc(hst, 16)
                sync.wait_ge(hst, 16 * NIC + 32)
    es3.close()
    mid2.close()
    if upto < 4:
        mid.close(); root.close(); return nc

    def kfree_evt(i):
        hf_, vc_ = divmod(i, NVC)
        if vc_ < NQC:
            kcnt = sum(1 for j in range(i + 1) if (j % NVC) < NQC)
            return ("ksk", 16 * kcnt)
        gv_ = hf_ * 16 + (vc_ - NQC)
        return ("vtm", 4 * (gv_ + 1))

    es4 = ExitStack()
    lnb4 = ln_alloc(es4, "l4")
    WVB = [es4.enter_context(nc.sbuf_tensor(f"WVB{i}", [128, NDC, 128], BF)) for i in range(2)]
    KVT = [es4.enter_context(nc.sbuf_tensor(f"KVT{i}", [128, TOK], BF)) for i in range(2)]
    VTT = [es4.enter_context(nc.sbuf_tensor(f"VTT{i}", [128, 4, 128], BF)) for i in range(2)]
    KVP = [es4.enter_context(nc.psum_tensor(f"KVP{i}", [128, 512], F32)) for i in range(4)]
    VPT = [es4.enter_context(nc.psum_tensor(f"VPT{i}", [128, 4, 128], BF)) for i in range(2)]
    kwl = es4.enter_context(nc.semaphore("kwl"))
    kvm = es4.enter_context(nc.semaphore("kvm"))
    kte = es4.enter_context(nc.semaphore("kte"))
    ksk = es4.enter_context(nc.semaphore("ksk"))
    vst = es4.enter_context(nc.semaphore("vst"))
    vtm = es4.enter_context(nc.semaphore("vtm"))
    vte = es4.enter_context(nc.semaphore("vte"))
    with nc.Block() as blk:
        @blk.sync
        def _(sync):
            for hf in range(2):
                ln_sync(sync, lnb4, x_full, list(range(hf * 16, hf * 16 + 16)), hf * 16)
                for vc in range(NVC):
                    i = hf * NVC + vc
                    if vc < NQC:
                        sync.wait_ge(kte, i + 1)
                        sync.dma_start(out=k_sc[vc][:, hf * 2048:(hf + 1) * 2048],
                                       in_=KVT[i % 2][:]).then_inc(ksk, 16)
                    else:
                        gv = hf * 16 + (vc - NQC)
                        for bt in range(4):
                            gvb = gv * 4 + bt
                            sync.wait_ge(vte, gvb + 1)
                            for j in range(4):
                                tsub = bt * 4 + j
                                sync.dma_start(
                                    out=v_sc[hf * 16 + tsub][:, (vc - NQC) * 128:(vc - NQC) * 128 + 128],
                                    in_=VTT[gvb % 2][:, j, :]).then_inc(vst, 16)
            sync.wait_ge(ksk, 16 * 32)
            sync.wait_ge(vst, 16 * 4 * 128)

        @blk.gpsimd
        def _(g):
            for hf in range(2):
                for vc in range(NVC):
                    i = hf * NVC + vc
                    if i >= 2:
                        g.wait_ge(kvm, i - 1)
                    g.dma_start(out=WVB[i % 2][:], in_=wkv[vc]).then_inc(kwl, 16)

        @blk.scalar
        def _(act):
            for hf in range(2):
                ln_act(act, lnb4, 16, hf * 16)
                for vc in range(NQC, NVC):
                    gv = hf * 16 + (vc - NQC)
                    for bt in range(4):
                        gvb = gv * 4 + bt
                        act.wait_ge(vtm, gvb + 1)
                        if gvb >= 2:
                            act.wait_ge(vst, 16 * 4 * (gvb - 1))
                        act.copy(out=VTT[gvb % 2][:], in_=VPT[gvb % 2][:])
                        act.maybe_drain_then_inc((vte, 1))

        @blk.vector
        def _(v):
            v.wait_ge(ld2, 32)
            for hf in range(2):
                ln_dve(v, lnb4, 16, hf * 16)
                for vc in range(NVC):
                    i = hf * NVC + vc
                    v.wait_ge(kvm, i + 1)
                    if i >= 2:
                        sem, val = kfree_evt(i - 2)
                        v.wait_ge(ksk if sem == "ksk" else vtm, val)
                    for ts in range(4):
                        v.tensor_scalar(out=KVT[i % 2][:, ts * 512:(ts + 1) * 512],
                                        in0=KVP[ts][:], scalar1=BKV[:, vc:vc + 1],
                                        scalar2=None, op0=A.add)
                    v.maybe_drain_then_inc((kte, 1))

        @blk.tensor
        def _(t):
            for hf in range(2):
                ln_pe(t, lnb4, 16, hf * 16)
                for vc in range(NVC):
                    i = hf * NVC + vc
                    t.wait_ge(kwl, 16 * (i + 1))
                    if i >= 1:
                        t.wait_ge(kte, i)
                    for dc in range(NDC):
                        for ts in range(4):
                            t.matmul(KVP[ts][:], WVB[i % 2][:, dc, :],
                                     XN[:, dc, ts * 512:(ts + 1) * 512],
                                     start=(dc == 0), stop=(dc == NDC - 1))
                    t.maybe_drain_then_inc((kvm, 1))
                    if vc >= NQC:
                        gv = hf * 16 + (vc - NQC)
                        t.wait_ge(kte, i + 1)
                        for bt in range(4):
                            gvb = gv * 4 + bt
                            if gvb >= 2:
                                t.wait_ge(vte, gvb - 1)
                            for j in range(4):
                                tsub = bt * 4 + j
                                t.transpose(VPT[gvb % 2][:, j, :],
                                            KVT[i % 2][:, tsub * 128:(tsub + 1) * 128],
                                            idbf[:])
                            t.maybe_drain_then_inc((vtm, 1))
    es4.close()
    mid.close()
    if upto < 5:
        root.close(); return nc

    es5 = ExitStack()
    SALL = es5.enter_context(nc.sbuf_tensor("SALL", [128, NH, SFULL], F16))
    QS = [es5.enter_context(nc.sbuf_tensor(f"QS{i}", [128, NQC, 128], BF)) for i in range(2)]
    KBB = [es5.enter_context(nc.sbuf_tensor(f"KBB{i}", [128, NQC, 512], BF)) for i in range(2)]
    VBB = [es5.enter_context(nc.sbuf_tensor(f"VBB{i}", [128, 4, 2048], BF)) for i in range(2)]
    MS = [es5.enter_context(nc.sbuf_tensor(f"MS{i}", [128, 512], F32)) for i in range(2)]
    MBAL = es5.enter_context(nc.sbuf_tensor("MBAL", [128, NH * 8], F32))
    MMAX = es5.enter_context(nc.sbuf_tensor("MMAX", [128, NH], F32))
    NEGM = es5.enter_context(nc.sbuf_tensor("NEGM", [128, NH], F32))
    LSUM = es5.enter_context(nc.sbuf_tensor("LSUM", [128, NH], F32))
    RLV = es5.enter_context(nc.sbuf_tensor("RLV", [128, NH], F32))
    PTS = [es5.enter_context(nc.sbuf_tensor(f"PTS{i}", [128, 4, 128], F16)) for i in range(NH)]
    OSB = es5.enter_context(nc.sbuf_tensor("OSB", [128, NH * 256], F16))
    VALT = es5.enter_context(nc.sbuf_tensor("VALT", [128, NQC, 128], F16))
    SPS = [es5.enter_context(nc.psum_tensor(f"SPS{i}", [128, 512], F32)) for i in range(2)]
    PTP = [es5.enter_context(nc.psum_tensor(f"PTP{i}", [128, 4, 128], F16)) for i in range(2)]
    OPP = [es5.enter_context(nc.psum_tensor(f"OPP{i}", [128, 2, 256], F32)) for i in range(4)]
    qld = es5.enter_context(nc.semaphore("qld"))
    msl = es5.enter_context(nc.semaphore("msl"))
    msf = es5.enter_context(nc.semaphore("msf"))
    kld = es5.enter_context(nc.semaphore("kld"))
    vld = es5.enter_context(nc.semaphore("vld"))
    smm = es5.enter_context(nc.semaphore("smm"))
    sev = es5.enter_context(nc.semaphore("sev"))
    nmd = es5.enter_context(nc.semaphore("nmd"))
    expd = es5.enter_context(nc.semaphore("expd"))
    ptm = es5.enter_context(nc.semaphore("ptm"))
    ptsc = es5.enter_context(nc.semaphore("ptsc"))
    avm = es5.enter_context(nc.semaphore("avm"))
    osd = es5.enter_context(nc.semaphore("osd"))
    vst2 = es5.enter_context(nc.semaphore("vst2"))
    gd5 = es5.enter_context(nc.semaphore("gd5"))

    PTSC_AT_VALS = [8 * KB_PRE[s + 1] + 4 * (s + 1) for s in range(NSLOT)]
    with nc.Block() as blk:
        @blk.sync
        def _(sync):
            for s in range(NSLOT):
                kbc = KBCS[s]
                if s >= 2:
                    sync.wait_ge(smm, 8 * KB_PRE[s - 1])
                sync.dma_start(out=QS[s % 2][:],
                               in_=q_sc[:, :, s * 128:(s + 1) * 128].rearrange("q p t -> p q t")
                               ).then_inc(qld, 16)
                if s >= 2:
                    sync.wait_ge(msf, s - 1)
                sync.dma_start(out=MS[s % 2][:], in_=masks_in[s]).then_inc(msl, 16)
                for kb in range(kbc):
                    g = KB_PRE[s] + kb
                    if g >= 2:
                        sync.wait_ge(smm, 8 * (g - 1))
                    sync.dma_start(out=KBB[g % 2][:],
                                   in_=k_sc[:, :, kb * 512:(kb + 1) * 512].rearrange("c p t -> p c t")
                                   ).then_inc(kld, 16)
                for kb2 in range(kbc):
                    g2 = KB_PRE[s] + kb2
                    if g2 >= 2:
                        sync.wait_ge(avm, g2 - 1)
                    sync.dma_start(out=VBB[g2 % 2][:],
                                   in_=v_sc[kb2 * 4:kb2 * 4 + 4].rearrange("v p c -> p v c")
                                   ).then_inc(vld, 16)
                sync.wait_ge(ptsc, PTSC_AT_VALS[s])
                sync.dma_start(out=vals_sc[s],
                               in_=VALT[:].rearrange("p q t -> p (q t)")).then_inc(vst2, 16)
            sync.wait_ge(vst2, 16 * NSLOT)

        @blk.tensor
        def _(t):
            ptpu = [0]
            for s in range(NSLOT):
                kbc = KBCS[s]
                t.wait_ge(qld, 16 * (s + 1))
                for kb in range(kbc):
                    g = KB_PRE[s] + kb
                    t.wait_ge(kld, 16 * (g + 1))
                    for h in range(NH):
                        ghb = 8 * g + h
                        if ghb >= 2:
                            t.wait_ge(sev, ghb - 1)
                        for qi in range(2):
                            t.matmul(SPS[ghb % 2][:], QS[s % 2][:, 2 * h + qi, :],
                                     KBB[g % 2][:, 2 * h + qi, :],
                                     start=(qi == 0), stop=(qi == 1))
                        t.maybe_drain_then_inc((smm, 1))
                if s >= 1:
                    t.wait_ge(osd, s)
                for kb2 in range(kbc):
                    g2 = KB_PRE[s] + kb2
                    t.wait_ge(vld, 16 * (g2 + 1))
                    pend = []
                    for h in range(NH + 1):
                        if h < NH:
                            t.wait_ge(expd, 8 * s + h + 1)
                            if ptpu[0] >= 2:
                                t.wait_ge(ptsc, ptpu[0] - 1)
                            for j in range(4):
                                t.transpose(PTP[ptpu[0] % 2][:, j, :],
                                            SALL[:, h, (kb2 * 4 + j) * 128:(kb2 * 4 + j) * 128 + 128],
                                            idfp[:])
                            t.maybe_drain_then_inc((ptm, 1))
                            ptpu[0] += 1
                            pend.append((h, ptpu[0]))
                        if (h >= 1 and pend) or h == NH:
                            hh, pcnt = pend.pop(0)
                            h_save = h
                            h = hh
                            t.wait_ge(ptsc, pcnt)
                            for j in range(4):
                                t.matmul(OPP[h // 2][:, h % 2, :], PTS[h][:, j, :],
                                         VBB[g2 % 2][:, j, h * 256:(h + 1) * 256],
                                         start=(kb2 == 0 and j == 0 and h % 2 == 0),
                                         stop=(kb2 == kbc - 1 and j == 3))
                            h = h_save
                    t.maybe_drain_then_inc((avm, 1))
                t.wait_ge(osd, s + 1)
                for vb in range(4):
                    if ptpu[0] >= 2:
                        t.wait_ge(ptsc, ptpu[0] - 1)
                    for j in range(4):
                        cc = 4 * vb + j
                        t.transpose(PTP[ptpu[0] % 2][:, j, :],
                                    OSB[:, cc * 128:(cc + 1) * 128], idfp[:])
                    t.maybe_drain_then_inc((ptm, 1))
                    ptpu[0] += 1

        @blk.vector
        def _(v):
            g5 = [0]
            def G():
                g5[0] += 1
                v.maybe_drain_then_inc((gd5, 1))
                v.wait_ge(gd5, g5[0])
            for s in range(NSLOT):
                kbc = KBCS[s]
                diag = kbc - 1
                for kb in range(kbc):
                    g = KB_PRE[s] + kb
                    if kb == diag:
                        v.wait_ge(msl, 16 * (s + 1))
                    for h in range(NH):
                        ghb = 8 * g + h
                        v.wait_ge(smm, ghb + 1)
                        if kb == diag:
                            v.tensor_tensor(out=SPS[ghb % 2][:], in0=SPS[ghb % 2][:],
                                            in1=MS[s % 2][:], op=A.add)
                        v.tensor_scalar(out=SALL[:, h, kb * 512:(kb + 1) * 512],
                                        in0=SPS[ghb % 2][:], scalar1=0.0, scalar2=None,
                                        op0=A.add, op1=A.max,
                                        accum_out=MBAL[:, h * 8 + kb:h * 8 + kb + 1]
                                        ).then_inc(sev, 1)
                        if kb == diag:
                            G()
                            v.reduce_max(MMAX[:, h:h + 1], MBAL[:, h * 8:h * 8 + kbc], axis=AX)
                            G()
                            v.tensor_scalar(out=NEGM[:, h:h + 1], in0=MMAX[:, h:h + 1],
                                            scalar1=-1.0, scalar2=None, op0=A.mult)
                            v.maybe_drain_then_inc((nmd, 1))
                v.maybe_drain_then_inc((msf, 1))
                for h in range(NH):
                    v.wait_ge(expd, 8 * s + h + 1)
                    v.reciprocal(RLV[:, h:h + 1], LSUM[:, h:h + 1])
                G()
                v.wait_ge(avm, KB_PRE[s + 1])
                for h in range(NH):
                    v.tensor_scalar(out=OSB[:, h * 256:(h + 1) * 256],
                                    in0=OPP[h // 2][:, h % 2, :],
                                    scalar1=RLV[:, h:h + 1], scalar2=None, op0=A.mult)
                v.maybe_drain_then_inc((osd, 1))

        @blk.scalar
        def _(act):
            ptpu = [0]
            for s in range(NSLOT):
                kbc = KBCS[s]
                klen = kbc * 512
                for h in range(NH):
                    act.wait_ge(nmd, 8 * s + h + 1)
                    act.activation(out=SALL[:, h, 0:klen], in_=SALL[:, h, 0:klen],
                                   func=AF.Exp, bias=NEGM[:, h:h + 1], scale=1.0,
                                   accum_out=LSUM[:, h:h + 1])
                    act.maybe_drain_then_inc((expd, 1))
                for kb2 in range(kbc):
                    for h in range(NH):
                        act.wait_ge(ptm, ptpu[0] + 1)
                        act.copy(out=PTS[h][:], in_=PTP[ptpu[0] % 2][:])
                        act.maybe_drain_then_inc((ptsc, 1))
                        ptpu[0] += 1
                if s >= 1:
                    act.wait_ge(vst2, 16 * s)
                for vb in range(4):
                    act.wait_ge(ptm, ptpu[0] + 1)
                    act.copy(out=VALT[:, 4 * vb:4 * vb + 4, :], in_=PTP[ptpu[0] % 2][:])
                    act.maybe_drain_then_inc((ptsc, 1))
                    ptpu[0] += 1
    es5.close()
    if upto < 6:
        root.close(); return nc

    es6 = ExitStack()
    VALB = es6.enter_context(nc.sbuf_tensor("VALB", [128, NSLOT, NQC, 128], F16))
    WOB = [es6.enter_context(nc.sbuf_tensor(f"WOB{i}", [128, NQC, 1024], BF)) for i in range(2)]
    SOT = [es6.enter_context(nc.sbuf_tensor(f"SOT{i}", [128, 1024], F32)) for i in range(2)]
    SPO = [es6.enter_context(nc.psum_tensor(f"SPO{i}", [128, 512], F32)) for i in range(8)]
    vll = es6.enter_context(nc.semaphore("vll"))
    wol = es6.enter_context(nc.semaphore("wol"))
    wof = es6.enter_context(nc.semaphore("wof"))
    somm = es6.enter_context(nc.semaphore("somm"))
    soev = es6.enter_context(nc.semaphore("soev"))
    sst = es6.enter_context(nc.semaphore("sst"))
    with nc.Block() as blk:
        @blk.sync
        def _(sync):
            for sl in range(NSLOT):
                sync.dma_start(out=VALB[:, sl, :, :],
                               in_=vals_sc[sl].rearrange("p (q t) -> p q t", q=NQC)
                               ).then_inc(vll, 16)
            for jq in range(4):
                for grp in range(4):
                    gj = jq * 4 + grp
                    for sl in range(4):
                        gsl = gj * 4 + sl
                        sync.wait_ge(soev, gsl + 1)
                        sync.dma_start(out=s_sc[grp * 4 + sl][:, jq * 1024:(jq + 1) * 1024],
                                       in_=SOT[gsl % 2][:]).then_inc(sst, 16)
            sync.wait_ge(sst, 16 * 64)

        @blk.gpsimd
        def _(g):
            for jq in range(4):
                if jq >= 2:
                    g.wait_ge(wof, jq - 1)
                g.dma_start(out=WOB[jq % 2][:],
                            in_=wo[:, :, jq * 1024:(jq + 1) * 1024].rearrange("c p j -> p c j")
                            ).then_inc(wol, 16)

        @blk.tensor
        def _(t):
            t.wait_ge(vll, 16 * NSLOT)
            for jq in range(4):
                t.wait_ge(wol, 16 * (jq + 1))
                for grp in range(4):
                    gj = jq * 4 + grp
                    if gj >= 1:
                        t.wait_ge(soev, 4 * gj)
                    for cc in range(NQC):
                        for sl in range(4):
                            for jb in range(2):
                                mm = t.matmul(SPO[sl * 2 + jb][:],
                                              VALB[:, grp * 4 + sl, cc, :],
                                              WOB[jq % 2][:, cc, jb * 512:(jb + 1) * 512],
                                              start=(cc == 0), stop=(cc == NQC - 1))
                    t.maybe_drain_then_inc((somm, 1))
                t.nop().then_inc(wof, 1) if False else None
                t.maybe_drain_then_inc((wof, 1))

        @blk.vector
        def _(v):
            for jq in range(4):
                for grp in range(4):
                    gj = jq * 4 + grp
                    v.wait_ge(somm, gj + 1)
                    for sl in range(4):
                        gsl = gj * 4 + sl
                        if gsl >= 2:
                            v.wait_ge(sst, 16 * (gsl - 1))
                        for jb in range(2):
                            v.tensor_copy(SOT[gsl % 2][:, jb * 512:(jb + 1) * 512],
                                          SPO[sl * 2 + jb][:])
                        v.maybe_drain_then_inc((soev, 1))
    es6.close()
    if upto < 7:
        root.close(); return nc

    es7 = ExitStack()
    BOS = es7.enter_context(nc.sbuf_tensor("BOS", [1, D], BF))
    BOB = es7.enter_context(nc.sbuf_tensor("BOB", [128, D], BF))
    DT = es7.enter_context(nc.sbuf_tensor("DT", [128, NIC, 1024], BF))
    HTA = es7.enter_context(nc.sbuf_tensor("HTA", [128, 28, 512], BF))
    HTB = es7.enter_context(nc.sbuf_tensor("HTB", [128, 28, 512], BF))
    SS = es7.enter_context(nc.sbuf_tensor("SS", [128, 1024], F32))
    XS = es7.enter_context(nc.sbuf_tensor("XS", [128, 1024], F32))
    GG = [es7.enter_context(nc.sbuf_tensor(f"GG{i}", [128, 512], F32)) for i in range(2)]
    SFB = es7.enter_context(nc.sbuf_tensor("SFB", [128, 1024], F32))
    OUTB = [es7.enter_context(nc.sbuf_tensor(f"OUTB{i}", [128, 512], F32)) for i in range(2)]
    PSD = [es7.enter_context(nc.psum_tensor(f"PSD{i}", [128, 512], F32)) for i in range(8)]
    bld = es7.enter_context(nc.semaphore("bld"))
    bpm7 = es7.enter_context(nc.semaphore("bpm7"))
    bobd = es7.enter_context(nc.semaphore("bobd"))
    dld = es7.enter_context(nc.semaphore("dld"))
    hld = es7.enter_context(nc.semaphore("hld"))
    haf = es7.enter_context(nc.semaphore("haf"))
    hbf = es7.enter_context(nc.semaphore("hbf"))
    dmm = es7.enter_context(nc.semaphore("dmm"))
    gsd = es7.enter_context(nc.semaphore("gsd"))
    gfr = es7.enter_context(nc.semaphore("gfr"))
    sxl = es7.enter_context(nc.semaphore("sxl"))
    ssfre = es7.enter_context(nc.semaphore("ssfre"))
    outd = es7.enter_context(nc.semaphore("outd"))
    ost = es7.enter_context(nc.semaphore("ost"))
    with nc.Block() as blk:
        @blk.gpsimd
        def _(g):
            g.dma_start(out=BOS[:], in_=bo_in).then_inc(bld, 16)
            for jq in range(4):
                if jq >= 1:
                    g.wait_ge(dmm, 4 * jq)
                for pc in range(8):
                    g.dma_start(out=DT[:, 7 * pc:7 * pc + 7, :],
                                in_=wd[7 * pc:7 * pc + 7, :, jq * 1024:(jq + 1) * 1024]
                                .rearrange("i p j -> p i j")).then_inc(dld, 16)
                for tt in range(4):
                    gtt = 4 * jq + tt
                    if gtt >= 1:
                        g.wait_ge(haf, gtt)
                    for pc in range(4):
                        g.dma_start(out=HTA[:, 7 * pc:7 * pc + 7, :],
                                    in_=h_sc[7 * pc:7 * pc + 7, :, tt * 512:(tt + 1) * 512]
                                    .rearrange("i p t -> p i t")).then_inc(hld, 16)
                    if gtt >= 1:
                        g.wait_ge(hbf, gtt)
                    for pc in range(4):
                        g.dma_start(out=HTB[:, 7 * pc:7 * pc + 7, :],
                                    in_=h_sc[28 + 7 * pc:28 + 7 * pc + 7, :, tt * 512:(tt + 1) * 512]
                                    .rearrange("i p t -> p i t")).then_inc(hld, 16)

        @blk.tensor
        def _(t):
            t.wait_ge(bld, 16)
            for jb in range(8):
                t.matmul(PSD[jb][:], ones_bf[:], BOS[:, jb * 512:(jb + 1) * 512],
                         start=True, stop=True)
            t.maybe_drain_then_inc((bpm7, 1))
            for jq in range(4):
                for tt in range(4):
                    gtt = 4 * jq + tt
                    t.wait_ge(hld, 16 * 4 * (2 * gtt + 1))
                    if gtt == 0:
                        t.wait_ge(bobd, 1)
                    else:
                        t.wait_ge(gsd, 8 * gtt)
                    for ic in range(NIC):
                        if tt == 0 and ic % 7 == 0:
                            t.wait_ge(dld, 16 * (8 * jq + ic // 7 + 1))
                        if ic == 28:
                            t.wait_ge(hld, 16 * 4 * (2 * gtt + 2))
                        ht = HTA if ic < 28 else HTB
                        icl = ic if ic < 28 else ic - 28
                        for tsub in range(4):
                            for jb in range(2):
                                mm = t.matmul(PSD[tsub * 2 + jb][:],
                                              ht[:, icl, tsub * 128:(tsub + 1) * 128],
                                              DT[:, ic, jb * 512:(jb + 1) * 512],
                                              start=(ic == 0), stop=(ic == NIC - 1))
                                if tsub == 3 and jb == 1 and ic == 27:
                                    mm.then_inc(haf, 1)
                                if tsub == 3 and jb == 1 and ic == NIC - 1:
                                    mm.then_inc(hbf, 1)

                    t.maybe_drain_then_inc((dmm, 1))

        @blk.sync
        def _(sync):
            for jq in range(4):
                for tt in range(4):
                    for tsub in range(4):
                        ts7 = (4 * jq + tt) * 4 + tsub
                        slot = tt * 4 + tsub
                        if ts7 >= 1:
                            sync.wait_ge(ssfre, ts7)
                        sync.dma_start(out=SS[:],
                                       in_=s_sc[slot][:, jq * 1024:(jq + 1) * 1024]
                                       ).then_inc(sxl, 16)
                        sync.dma_start(out=XS[:],
                                       in_=x_mine[slot][:, jq * 1024:(jq + 1) * 1024]
                                       ).then_inc(sxl, 16)
                        for jb in range(2):
                            ko = ts7 * 2 + jb
                            sync.wait_ge(outd, ko + 1)
                            sync.dma_start(out=out_mine[slot][:, jq * 1024 + jb * 512:
                                                              jq * 1024 + jb * 512 + 512],
                                           in_=OUTB[ko % 2][:]).then_inc(ost, 16)
            sync.wait_ge(ost, 16 * 128)

        @blk.scalar
        def _(act):
            for jq in range(4):
                for tt in range(4):
                    gtt = 4 * jq + tt
                    act.wait_ge(dmm, gtt + 1)
                    for tsub in range(4):
                        for jb in range(2):
                            k = gtt * 8 + tsub * 2 + jb
                            if k >= 2:
                                act.wait_ge(gfr, k - 1)
                            act.activation(out=GG[k % 2][:], in_=PSD[tsub * 2 + jb][:],
                                           func=AF.Silu, bias=0.0, scale=1.0)
                            act.maybe_drain_then_inc((gsd, 1))

        @blk.vector
        def _(v):
            v.wait_ge(bpm7, 1)
            for jb in range(8):
                v.tensor_copy(BOB[:, jb * 512:(jb + 1) * 512], PSD[jb][:])
            v.maybe_drain_then_inc((bobd, 1))
            for jq in range(4):
                for tt in range(4):
                    for tsub in range(4):
                        ts7 = (4 * jq + tt) * 4 + tsub
                        v.wait_ge(sxl, 32 * (ts7 + 1))
                        v.tensor_tensor(out=SFB[:], in0=SS[:],
                                        in1=BOB[:, jq * 1024:(jq + 1) * 1024], op=A.add)
                        for jb in range(2):
                            k = (4 * jq + tt) * 8 + tsub * 2 + jb
                            ko = ts7 * 2 + jb
                            v.wait_ge(gsd, k + 1)
                            if ko >= 2:
                                v.wait_ge(ost, 16 * (ko - 1))
                            v.tensor_tensor(out=OUTB[ko % 2][:],
                                            in0=SFB[:, jb * 512:(jb + 1) * 512],
                                            in1=GG[k % 2][:], op=A.mult).then_inc(gfr, 1)
                            last = v.tensor_tensor(out=OUTB[ko % 2][:], in0=OUTB[ko % 2][:],
                                                   in1=XS[:, jb * 512:(jb + 1) * 512],
                                                   op=A.add)
                            if jb == 1:
                                last.then_inc(ssfre, 1)
                            v.maybe_drain_then_inc((outd, 1))
    es7.close()
    root.close()
    return nc



def _lay_w(wt):
    d_in, d_out = wt.shape
    ndc = d_in // 128
    noc = d_out // 128
    return np.ascontiguousarray(wt.reshape(ndc, 128, noc, 128).transpose(2, 1, 0, 3))


def _host_prepare(inputs):
    x = np.asarray(inputs["L0_attn_x"], np.float32)
    w_qkv = np.asarray(inputs["L0_attn_w_qkv"], np.float32)
    b_qkv = np.asarray(inputs["L0_attn_b_qkv"], np.float32)
    w_o = np.asarray(inputs["L0_attn_w_o"], np.float32)
    b_o = np.asarray(inputs["L0_attn_b_o"], np.float32)
    gate_w = np.asarray(inputs["L0_ffn_gate_w"], np.float32)
    u = np.asarray(inputs["L0_ffn_u"], np.float32)
    g = np.asarray(inputs["L0_ffn_g"], np.float32)
    d = np.asarray(inputs["L0_ffn_d"], np.float32)

    shared = {}
    shared["wq"] = _lay_w(np.ascontiguousarray(w_qkv[0:2048].T))
    shared["wkv"] = _lay_w(np.ascontiguousarray(w_qkv[2048:6144].T))
    shared["wu"] = _lay_w(np.ascontiguousarray(u.T))
    shared["wg"] = _lay_w(np.ascontiguousarray(g.T))
    shared["wd"] = np.ascontiguousarray(d.T.reshape(NIC, 128, D))
    shared["wo"] = np.ascontiguousarray(w_o.T.reshape(NQC, 128, D))
    shared["gwt"] = np.ascontiguousarray(gate_w.T.reshape(NDC, 128, NE).transpose(1, 0, 2))
    shared["bq_in"] = np.ascontiguousarray(b_qkv[0:2048].reshape(NQC, 128).T)
    shared["bkv_in"] = np.ascontiguousarray(b_qkv[2048:6144].reshape(NVC, 128).T)
    shared["bo_in"] = np.ascontiguousarray(b_o.reshape(1, D))
    shared["eye_in"] = np.eye(128, dtype=np.float32)
    shared["ones_in"] = np.ones((1, 128), np.float32)
    smat = np.zeros((NE, NIC * 128), np.float32)
    rows = np.arange(NIC * 128)
    smat[rows // 448, rows] = 1.0
    shared["smat_in"] = smat

    def masks_for(parity):
        m = np.zeros((NSLOT, 128, 512), np.float32)
        for s in range(NSLOT):
            qt = 2 * s + parity
            base = (KBCS[s] - 1) * 512
            qpos = qt * 128 + np.arange(128)[:, None]
            kpos = base + np.arange(512)[None, :]
            m[s] = np.where(kpos <= qpos, 0.0, -30000.0).astype(np.float32)
        return m

    masks = [masks_for(0), masks_for(1)]

    in_maps = []
    for c in range(8):
        b, p = c // 2, c % 2
        xb = x[b].reshape(32, 128, D)
        im = dict(shared)
        im["x_full"] = np.ascontiguousarray(xb)
        im["x_mine"] = np.ascontiguousarray(xb[p::2])
        im["masks_in"] = masks[p]
        for kk in im:
            assert im[kk].dtype == np.float32, (kk, im[kk].dtype)
        in_maps.append(im)
    return in_maps


def kernel(**inputs):
    in_maps = _host_prepare(inputs)
    nc = build_nc(dbg=False)
    res = run_bass_kernel_spmd(nc, in_maps, core_ids=list(range(8)))
    out = np.empty((4, SFULL, D), np.float32)
    for c in range(8):
        b, p = c // 2, c % 2
        om = np.asarray(res.results[c]["out_mine"])
        ov = out[b].reshape(32, 128, D)
        ov[p::2] = om
    return out


# version 05
# speedup vs baseline: 1.1889x; 1.0501x over previous
import numpy as np
import ml_dtypes
import concourse.bass as bass
import concourse.mybir as mybir
from concourse.alu_op_type import AluOpType as A
from concourse.bass_utils import run_bass_kernel_spmd
from contextlib import ExitStack

F32 = mybir.dt.float32
BF = mybir.dt.bfloat16
F16 = mybir.dt.float16
AF = mybir.ActivationFunctionType
AX = mybir.AxisListType.X

D = 4096
NDC = 32
TOK = 2048
NSLOT = 16
SFULL = 4096
NQC = 16
NVC = 32
NIC = 56
NH = 8
HD = 256
NE = 16
EPS = 1e-5

KBCS = [(s + 2) // 2 for s in range(NSLOT)]
KB_PRE = [0]
for s in range(NSLOT):
    KB_PRE.append(KB_PRE[-1] + KBCS[s])
KB_TOT = KB_PRE[-1]


def build_nc(dbg=False, upto=99):
    nc = bass.Bass("TRN2", target_bir_lowering=False)

    x_mine = nc.dram_tensor("x_mine", [NSLOT, 128, D], F32, kind="ExternalInput").ap()
    wq = nc.dram_tensor("wq", [NQC, 128, NDC, 128], F32, kind="ExternalInput").ap()
    wkv = nc.dram_tensor("wkv", [NVC, 128, NDC, 128], F32, kind="ExternalInput").ap()
    wu = nc.dram_tensor("wu", [NIC, 128, NDC, 128], F32, kind="ExternalInput").ap()
    wg = nc.dram_tensor("wg", [NIC, 128, NDC, 128], F32, kind="ExternalInput").ap()
    wd = nc.dram_tensor("wd", [NIC, 128, D], F32, kind="ExternalInput").ap()
    wo = nc.dram_tensor("wo", [NQC, 128, D], F32, kind="ExternalInput").ap()
    gwt = nc.dram_tensor("gwt", [128, NDC, NE], F32, kind="ExternalInput").ap()
    bq_in = nc.dram_tensor("bq_in", [128, NQC], F32, kind="ExternalInput").ap()
    bkv_in = nc.dram_tensor("bkv_in", [128, NVC], F32, kind="ExternalInput").ap()
    bo_in = nc.dram_tensor("bo_in", [1, D], F32, kind="ExternalInput").ap()
    eye_in = nc.dram_tensor("eye_in", [128, 128], F32, kind="ExternalInput").ap()
    ones_in = nc.dram_tensor("ones_in", [1, 128], F32, kind="ExternalInput").ap()
    smat_in = nc.dram_tensor("smat_in", [NE, NIC * 128], F32, kind="ExternalInput").ap()
    masks_in = nc.dram_tensor("masks_in", [NSLOT, 128, 512], F32, kind="ExternalInput").ap()

    out_mine = nc.dram_tensor("out_mine", [NSLOT, 128, D], F32, kind="ExternalOutput").ap()

    sk = "ExternalOutput" if dbg else "Internal"
    q_sc = nc.dram_tensor("q_sc", [NQC, 128, TOK], BF, kind=sk).ap()
    k_own = nc.dram_tensor("k_own", [2048, 2048], BF, kind="Internal").ap()
    v_own = nc.dram_tensor("v_own", [2048, 2048], BF, kind="Internal").ap()
    k_gath = nc.dram_tensor("k_gath", [4096, 2048], BF, kind=sk).ap()
    v_gath = nc.dram_tensor("v_gath", [4096, 2048], BF, kind=sk).ap()
    h_sc = nc.dram_tensor("h_sc", [NIC, 128, TOK], BF, kind=sk).ap()
    s_sc = nc.dram_tensor("s_sc", [NSLOT, 128, D], F32, kind=sk).ap()
    vals_sc = nc.dram_tensor("vals_sc", [NSLOT, 128, TOK], F16, kind=sk).ap()
    if dbg:
        xn_dbg = nc.dram_tensor("xn_dbg", [128, NDC, TOK], BF, kind="ExternalOutput").ap()
        wtb_dbg = nc.dram_tensor("wtb_dbg", [NE, TOK], BF, kind="ExternalOutput").ap()

    root = ExitStack()
    idbf = root.enter_context(nc.sbuf_tensor("idbf", [128, 128], BF))
    idfp = root.enter_context(nc.sbuf_tensor("idfp", [128, 128], F16))
    ones_bf = root.enter_context(nc.sbuf_tensor("ones_bf", [1, 128], BF))

    BQ = root.enter_context(nc.sbuf_tensor("BQ", [128, NQC], F32))
    BKV = root.enter_context(nc.sbuf_tensor("BKV", [128, NVC], F32))
    ld1 = root.enter_context(nc.semaphore("ld1"))
    ld2 = root.enter_context(nc.semaphore("ld2"))

    mid = ExitStack()
    XN = mid.enter_context(nc.sbuf_tensor("XN", [128, NDC, TOK], BF))
    SQD = mid.enter_context(nc.sbuf_tensor("SQD", [128, D], BF))
    mid2 = ExitStack()
    WTB = mid2.enter_context(nc.sbuf_tensor("WTB", [NE, TOK], BF))
    S_sb = mid2.enter_context(nc.sbuf_tensor("S_sb", [NE, NIC * 128], BF))
    GWS = mid2.enter_context(nc.sbuf_tensor("GWS", [128, NDC, NE], BF))

    def ln_alloc(es, tag):
        b = {}
        b["xsb"] = [es.enter_context(nc.sbuf_tensor(f"{tag}xsb{i}", [128, D], F32)) for i in range(2)]
        b["xnt"] = es.enter_context(nc.sbuf_tensor(f"{tag}xnt", [128, D], BF))
        b["ssq"] = [es.enter_context(nc.sbuf_tensor(f"{tag}ssq{i}", [128, 1], F32)) for i in range(2)]
        b["sm"] = es.enter_context(nc.sbuf_tensor(f"{tag}sm", [128, 1], F32))
        b["mn"] = [es.enter_context(nc.sbuf_tensor(f"{tag}mn{i}", [128, 1], F32)) for i in range(2)]
        b["ex2"] = es.enter_context(nc.sbuf_tensor(f"{tag}ex2", [128, 1], F32))
        b["msq"] = es.enter_context(nc.sbuf_tensor(f"{tag}msq", [128, 1], F32))
        b["var"] = es.enter_context(nc.sbuf_tensor(f"{tag}var", [128, 1], F32))
        b["ve"] = es.enter_context(nc.sbuf_tensor(f"{tag}ve", [128, 1], F32))
        b["rv"] = [es.enter_context(nc.sbuf_tensor(f"{tag}rv{i}", [128, 1], F32)) for i in range(2)]
        b["rsd"] = [es.enter_context(nc.sbuf_tensor(f"{tag}rsd{i}", [128, 1], F32)) for i in range(2)]
        b["pt"] = [es.enter_context(nc.psum_tensor(f"{tag}pt{i}", [128, 4, 128], BF)) for i in range(2)]
        for s in ["ldx", "fre", "sqs", "rvd", "rsb", "xnr", "ptd", "evd", "gd"]:
            b[s] = es.enter_context(nc.semaphore(f"{tag}{s}"))
        b["gdn"] = [0]
        return b

    def ln_sync(sync, b, x_ap, tiles, g0):
        for i, t in enumerate(tiles):
            gs = g0 + i
            if gs >= 2:
                sync.wait_ge(b["fre"], 2 * (gs - 1))
            sync.dma_start(out=b["xsb"][gs % 2][:], in_=x_ap[t]).then_inc(b["ldx"], 16)

    def ln_act(act, b, nsub, g0):
        for i in range(nsub):
            gs = g0 + i
            act.wait_ge(b["ldx"], 16 * (gs + 1))
            act.activation(out=SQD[:], in_=b["xsb"][gs % 2][:], func=AF.Square,
                           bias=0.0, scale=1.0,
                           accum_out=b["ssq"][gs % 2][:]).then_inc(b["fre"], 1)
            act.maybe_drain_then_inc((b["sqs"], 1))
            act.wait_ge(b["rvd"], gs + 1)
            act.activation(out=b["rsd"][gs % 2][:], in_=b["rv"][gs % 2][:], func=AF.Sqrt,
                           bias=0.0, scale=1.0)
            act.maybe_drain_then_inc((b["rsb"], 1))
            for bt in range(8):
                gb = 8 * gs + bt
                act.wait_ge(b["ptd"], gb + 1)
                act.copy(out=XN[:, 4 * bt:4 * bt + 4, i * 128:(i + 1) * 128],
                         in_=b["pt"][bt % 2][:]).then_inc(b["evd"], 1)

    def ln_dve(v, b, nsub, g0):
        def G():
            b["gdn"][0] += 1
            v.maybe_drain_then_inc((b["gd"], 1))
            v.wait_ge(b["gd"], b["gdn"][0])
        for i in range(nsub):
            gs = g0 + i
            v.wait_ge(b["ldx"], 16 * (gs + 1))
            v.reduce_sum(b["sm"][:], b["xsb"][gs % 2][:], axis=AX)
            G()
            v.tensor_scalar(out=b["mn"][gs % 2][:], in0=b["sm"][:], scalar1=1.0 / D,
                            scalar2=None, op0=A.mult)
            v.wait_ge(b["sqs"], gs + 1)
            v.tensor_scalar(out=b["ex2"][:], in0=b["ssq"][gs % 2][:], scalar1=1.0 / D,
                            scalar2=None, op0=A.mult)
            G()
            v.tensor_tensor(out=b["msq"][:], in0=b["mn"][gs % 2][:], in1=b["mn"][gs % 2][:], op=A.mult)
            G()
            v.tensor_tensor(out=b["var"][:], in0=b["ex2"][:], in1=b["msq"][:], op=A.subtract)
            G()
            v.tensor_scalar(out=b["ve"][:], in0=b["var"][:], scalar1=EPS, scalar2=None, op0=A.add)
            G()
            v.reciprocal(b["rv"][gs % 2][:], b["ve"][:])
            v.maybe_drain_then_inc((b["rvd"], 1))
            v.wait_ge(b["rsb"], gs + 1)
            if gs >= 1:
                v.wait_ge(b["ptd"], 8 * gs)
            v.tensor_scalar(out=b["xnt"][:], in0=b["xsb"][gs % 2][:],
                            scalar1=b["mn"][gs % 2][:], scalar2=b["rsd"][gs % 2][:],
                            op0=A.subtract, op1=A.mult).then_inc(b["fre"], 1)
            v.maybe_drain_then_inc((b["xnr"], 1))

    def ln_pe(t, b, nsub, g0):
        for i in range(nsub):
            gs = g0 + i
            t.wait_ge(b["xnr"], gs + 1)
            for bt in range(8):
                gb = 8 * gs + bt
                if gb >= 2:
                    t.wait_ge(b["evd"], gb - 1)
                for j in range(4):
                    t.transpose(b["pt"][bt % 2][:, j, :],
                                b["xnt"][:, (4 * bt + j) * 128:(4 * bt + j) * 128 + 128],
                                idbf[:])
                t.maybe_drain_then_inc((b["ptd"], 1))

    es0 = ExitStack()
    lnb = ln_alloc(es0, "l0")
    with nc.Block() as blk:
        @blk.gpsimd
        def _(g):
            g.dma_start(out=idbf[:], in_=eye_in).then_inc(ld1, 16)
            g.dma_start(out=idfp[:], in_=eye_in).then_inc(ld1, 16)
            g.dma_start(out=ones_bf[:], in_=ones_in).then_inc(ld1, 16)
            g.dma_start(out=GWS[:], in_=gwt).then_inc(ld1, 16)
            g.dma_start(out=S_sb[:], in_=smat_in).then_inc(ld1, 16)

        @blk.sync
        def _(sync):
            sync.dma_start(out=BQ[:], in_=bq_in).then_inc(ld2, 16)
            sync.dma_start(out=BKV[:], in_=bkv_in).then_inc(ld2, 16)
            ln_sync(sync, lnb, x_mine, list(range(16)), 0)

        @blk.scalar
        def _(act):
            ln_act(act, lnb, 16, 0)

        @blk.vector
        def _(v):
            ln_dve(v, lnb, 16, 0)

        @blk.tensor
        def _(t):
            t.wait_ge(ld1, 16)
            ln_pe(t, lnb, 16, 0)
    es0.close()
    if upto < 1:
        mid2.close(); mid.close(); root.close(); return nc

    es1 = ExitStack()
    WQB = [es1.enter_context(nc.sbuf_tensor(f"WQB{i}", [128, NDC, 128], BF)) for i in range(2)]
    QTILE = [es1.enter_context(nc.sbuf_tensor(f"QTILE{i}", [128, TOK], BF)) for i in range(2)]
    QP = [es1.enter_context(nc.psum_tensor(f"QP{i}", [128, 512], F32)) for i in range(8)]
    wld = es1.enter_context(nc.semaphore("wld"))
    qmm = es1.enter_context(nc.semaphore("qmm"))
    qtl = es1.enter_context(nc.semaphore("qtl"))
    qst = es1.enter_context(nc.semaphore("qst"))
    with nc.Block() as blk:
        @blk.gpsimd
        def _(g):
            for qc in range(NQC):
                if qc >= 2:
                    g.wait_ge(qmm, qc - 1)
                g.dma_start(out=WQB[qc % 2][:], in_=wq[qc]).then_inc(wld, 16)

        @blk.tensor
        def _(t):
            for qc in range(NQC):
                t.wait_ge(wld, 16 * (qc + 1))
                if qc >= 2:
                    t.wait_ge(qtl, qc - 1)
                bs = (qc % 2) * 4
                for dc in range(NDC):
                    for ts in range(4):
                        t.matmul(QP[bs + ts][:], WQB[qc % 2][:, dc, :],
                                 XN[:, dc, ts * 512:(ts + 1) * 512],
                                 start=(dc == 0), stop=(dc == NDC - 1))
                t.maybe_drain_then_inc((qmm, 1))

        @blk.vector
        def _(v):
            v.wait_ge(ld2, 16)
            for qc in range(NQC):
                v.wait_ge(qmm, qc + 1)
                if qc >= 2:
                    v.wait_ge(qst, 16 * (qc - 1))
                bs = (qc % 2) * 4
                for ts in range(4):
                    v.tensor_scalar(out=QTILE[qc % 2][:, ts * 512:(ts + 1) * 512],
                                    in0=QP[bs + ts][:], scalar1=BQ[:, qc:qc + 1],
                                    scalar2=None, op0=A.add)
                v.maybe_drain_then_inc((qtl, 1))

        @blk.sync
        def _(sync):
            for qc in range(NQC):
                sync.wait_ge(qtl, qc + 1)
                sync.dma_start(out=q_sc[qc], in_=QTILE[qc % 2][:]).then_inc(qst, 16)
            sync.wait_ge(qst, 16 * NQC)
            if dbg and upto == 1:
                sync.dma_start(out=xn_dbg, in_=XN[:]).then_inc(qst, 16)
                sync.wait_ge(qst, 16 * NQC + 16)
    es1.close()
    if upto < 2:
        mid2.close(); mid.close(); root.close(); return nc

    es2 = ExitStack()
    RPB = es2.enter_context(nc.psum_tensor("RPB", [128, 256], F32))
    TPW = [es2.enter_context(nc.psum_tensor(f"TPW{i}", [16, 128], BF)) for i in range(2)]
    WBF = es2.enter_context(nc.sbuf_tensor("WBF", [128, 256], BF))
    rlg = es2.enter_context(nc.sbuf_tensor("rlg", [128, NE], F32))
    rm1 = es2.enter_context(nc.sbuf_tensor("rm1", [128, 1], F32))
    rnm = es2.enter_context(nc.sbuf_tensor("rnm", [128, 1], F32))
    rmsk = es2.enter_context(nc.sbuf_tensor("rmsk", [128, NE], F32))
    rlm = es2.enter_context(nc.sbuf_tensor("rlm", [128, NE], F32))
    rm2 = es2.enter_context(nc.sbuf_tensor("rm2", [128, 1], F32))
    rsel = es2.enter_context(nc.sbuf_tensor("rsel", [128, NE], F32))
    rel = es2.enter_context(nc.sbuf_tensor("rel", [128, NE], F32))
    rwn = es2.enter_context(nc.sbuf_tensor("rwn", [128, NE], F32))
    rden = es2.enter_context(nc.sbuf_tensor("rden", [128, 1], F32))
    rrd = es2.enter_context(nc.sbuf_tensor("rrd", [128, 1], F32))
    rwf = es2.enter_context(nc.sbuf_tensor("rwf", [128, NE], F32))
    rmm = es2.enter_context(nc.semaphore("rmm"))
    rwb = es2.enter_context(nc.semaphore("rwb"))
    rtw = es2.enter_context(nc.semaphore("rtw"))
    rwt = es2.enter_context(nc.semaphore("rwt"))
    rex = es2.enter_context(nc.semaphore("rex"))
    rnd = es2.enter_context(nc.semaphore("rnd"))
    rgd = es2.enter_context(nc.semaphore("rgd"))
    with nc.Block() as blk:
        @blk.tensor
        def _(t):
            t.wait_ge(ld1, 80)
            for sub in range(16):
                for dc in range(NDC):
                    t.matmul(RPB[:, sub * 16:sub * 16 + 16],
                             XN[:, dc, sub * 128:sub * 128 + 128],
                             GWS[:, dc, :], start=(dc == 0), stop=(dc == NDC - 1))
            t.maybe_drain_then_inc((rmm, 1))
            for sub in range(16):
                t.wait_ge(rwb, sub + 1)
                if sub >= 2:
                    t.wait_ge(rwt, sub - 1)
                t.transpose(TPW[sub % 2][:], WBF[:, sub * 16:sub * 16 + 16], idbf[:])
                t.maybe_drain_then_inc((rtw, 1))

        @blk.vector
        def _(v):
            g = [0]
            def G():
                g[0] += 1
                v.maybe_drain_then_inc((rgd, 1))
                v.wait_ge(rgd, g[0])
            v.wait_ge(rmm, 1)
            for sub in range(16):
                v.tensor_copy(rlg[:], RPB[:, sub * 16:sub * 16 + 16])
                G()
                v.reduce_max(rm1[:], rlg[:], axis=AX)
                G()
                v.tensor_scalar(out=rnm[:], in0=rm1[:], scalar1=-1.0, scalar2=None, op0=A.mult)
                v.tensor_scalar(out=rmsk[:], in0=rlg[:], scalar1=rm1[:], scalar2=None, op0=A.is_ge)
                G()
                v.scalar_tensor_tensor(out=rlm[:], in0=rmsk[:], scalar=-30000.0, in1=rlg[:],
                                       op0=A.mult, op1=A.add)
                G()
                v.reduce_max(rm2[:], rlm[:], axis=AX)
                G()
                v.tensor_scalar(out=rsel[:], in0=rlg[:], scalar1=rm2[:], scalar2=None, op0=A.is_ge)
                v.maybe_drain_then_inc((rnd, 1))
                v.wait_ge(rex, sub + 1)
                v.tensor_tensor(out=rwn[:], in0=rel[:], in1=rsel[:], op=A.mult)
                G()
                v.reduce_sum(rden[:], rwn[:], axis=AX)
                G()
                v.reciprocal(rrd[:], rden[:])
                G()
                v.tensor_scalar(out=rwf[:], in0=rwn[:], scalar1=rrd[:], scalar2=None, op0=A.mult)
                G()
                v.tensor_copy(WBF[:, sub * 16:sub * 16 + 16], rwf[:])
                v.maybe_drain_then_inc((rwb, 1))
            for sub in range(16):
                v.wait_ge(rtw, sub + 1)
                v.tensor_copy(WTB[:, sub * 128:(sub + 1) * 128], TPW[sub % 2][:])
                v.maybe_drain_then_inc((rwt, 1))

        @blk.scalar
        def _(act):
            for sub in range(16):
                act.wait_ge(rnd, sub + 1)
                act.activation(out=rel[:], in_=rlg[:], func=AF.Exp, bias=rnm[:], scale=1.0)
                act.maybe_drain_then_inc((rex, 1))
    es2.close()
    if upto < 3:
        mid2.close(); mid.close(); root.close(); return nc

    es3 = ExitStack()
    UB = [es3.enter_context(nc.sbuf_tensor(f"UB{i}", [128, NDC, 128], BF)) for i in range(2)]
    GB = [es3.enter_context(nc.sbuf_tensor(f"GB{i}", [128, NDC, 128], BF)) for i in range(2)]
    SG = es3.enter_context(nc.sbuf_tensor("SG", [128, TOK], BF))
    H1 = es3.enter_context(nc.sbuf_tensor("H1", [128, TOK], BF))
    HW = [es3.enter_context(nc.sbuf_tensor(f"HW{i}", [128, TOK], BF)) for i in range(2)]
    UP = [es3.enter_context(nc.psum_tensor(f"UP{i}", [128, 512], F32)) for i in range(4)]
    GP = [es3.enter_context(nc.psum_tensor(f"GP{i}", [128, 512], F32)) for i in range(4)]
    uld = es3.enter_context(nc.semaphore("uld"))
    ugm = es3.enter_context(nc.semaphore("ugm"))
    wpm = es3.enter_context(nc.semaphore("wpm"))
    sgd = es3.enter_context(nc.semaphore("sgd"))
    h1d = es3.enter_context(nc.semaphore("h1d"))
    hwd = es3.enter_context(nc.semaphore("hwd"))
    hst = es3.enter_context(nc.semaphore("hst"))
    with nc.Block() as blk:
        @blk.gpsimd
        def _(g):
            for ic in range(NIC):
                if ic >= 2:
                    g.wait_ge(ugm, ic - 1)
                g.dma_start(out=UB[ic % 2][:], in_=wu[ic]).then_inc(uld, 16)
                g.dma_start(out=GB[ic % 2][:], in_=wg[ic]).then_inc(uld, 16)

        @blk.tensor
        def _(t):
            for ic in range(NIC):
                t.wait_ge(uld, 32 * (ic + 1))
                if ic >= 1:
                    t.wait_ge(hwd, ic)
                    t.wait_ge(sgd, ic)
                for dc in range(NDC):
                    for ts in range(4):
                        t.matmul(UP[ts][:], UB[ic % 2][:, dc, :],
                                 XN[:, dc, ts * 512:(ts + 1) * 512],
                                 start=(dc == 0), stop=(dc == NDC - 1))
                    for ts in range(4):
                        t.matmul(GP[ts][:], GB[ic % 2][:, dc, :],
                                 XN[:, dc, ts * 512:(ts + 1) * 512],
                                 start=(dc == 0), stop=(dc == NDC - 1))
                t.maybe_drain_then_inc((ugm, 1))
                t.wait_ge(h1d, ic + 1)
                for ts in range(4):
                    t.matmul(UP[ts][:], S_sb[:, ic * 128:(ic + 1) * 128],
                             WTB[:, ts * 512:(ts + 1) * 512], start=True, stop=True)
                t.maybe_drain_then_inc((wpm, 1))

        @blk.scalar
        def _(act):
            for ic in range(NIC):
                act.wait_ge(ugm, ic + 1)
                for ts in range(4):
                    act.activation(out=SG[:, ts * 512:(ts + 1) * 512], in_=GP[ts][:],
                                   func=AF.Silu, bias=0.0, scale=1.0)
                act.maybe_drain_then_inc((sgd, 1))

        @blk.vector
        def _(v):
            for ic in range(NIC):
                v.wait_ge(sgd, ic + 1)
                for ts in range(4):
                    v.tensor_tensor(out=H1[:, ts * 512:(ts + 1) * 512], in0=UP[ts][:],
                                    in1=SG[:, ts * 512:(ts + 1) * 512], op=A.mult)
                v.maybe_drain_then_inc((h1d, 1))
                v.wait_ge(wpm, ic + 1)
                if ic >= 2:
                    v.wait_ge(hst, 16 * (ic - 1))
                for ts in range(4):
                    v.tensor_tensor(out=HW[ic % 2][:, ts * 512:(ts + 1) * 512], in0=UP[ts][:],
                                    in1=H1[:, ts * 512:(ts + 1) * 512], op=A.mult)
                v.maybe_drain_then_inc((hwd, 1))

        @blk.sync
        def _(sync):
            for ic in range(NIC):
                sync.wait_ge(hwd, ic + 1)
                sync.dma_start(out=h_sc[ic], in_=HW[ic % 2][:]).then_inc(hst, 16)
            sync.wait_ge(hst, 16 * NIC)
            if dbg:
                sync.dma_start(out=xn_dbg, in_=XN[:]).then_inc(hst, 16)
                sync.dma_start(out=wtb_dbg, in_=WTB[:]).then_inc(hst, 16)
                sync.wait_ge(hst, 16 * NIC + 32)
    es3.close()
    mid2.close()
    if upto < 4:
        mid.close(); root.close(); return nc

    es4 = ExitStack()
    WVB = [es4.enter_context(nc.sbuf_tensor(f"WVB{i}", [128, NDC, 128], BF)) for i in range(2)]
    KVT = [es4.enter_context(nc.sbuf_tensor(f"KVT{i}", [128, TOK], BF)) for i in range(2)]
    VTT = [es4.enter_context(nc.sbuf_tensor(f"VTT{i}", [128, 4, 128], BF)) for i in range(2)]
    KVP = [es4.enter_context(nc.psum_tensor(f"KVP{i}", [128, 512], F32)) for i in range(4)]
    VPT = [es4.enter_context(nc.psum_tensor(f"VPT{i}", [128, 4, 128], BF)) for i in range(2)]
    kwl = es4.enter_context(nc.semaphore("kwl"))
    kvm = es4.enter_context(nc.semaphore("kvm"))
    kte = es4.enter_context(nc.semaphore("kte"))
    ksk = es4.enter_context(nc.semaphore("ksk"))
    vst = es4.enter_context(nc.semaphore("vst"))
    vtm = es4.enter_context(nc.semaphore("vtm"))
    vte = es4.enter_context(nc.semaphore("vte"))
    ccs = es4.enter_context(nc.semaphore("ccs"))

    def kfree_evt(i):
        if i < NQC:
            return ("ksk", 16 * (i + 1))
        gv_ = i - NQC
        return ("vtm", 4 * (gv_ + 1))

    with nc.Block() as blk:
        @blk.sync
        def _(sync):
            for vc in range(NVC):
                if vc < NQC:
                    sync.wait_ge(kte, vc + 1)
                    sync.dma_start(out=k_own.rearrange("(c p) t -> c p t", p=128)[vc],
                                   in_=KVT[vc % 2][:]).then_inc(ksk, 16)
                else:
                    gv = vc - NQC
                    for bt in range(4):
                        gvb = gv * 4 + bt
                        sync.wait_ge(vte, gvb + 1)
                        for j in range(4):
                            tsub = bt * 4 + j
                            sync.dma_start(
                                out=v_own.rearrange("(s t) c -> s t c", t=128)[tsub][:, gv * 128:gv * 128 + 128],
                                in_=VTT[gvb % 2][:, j, :]).then_inc(vst, 16)
            sync.wait_ge(ksk, 16 * NQC)
            sync.wait_ge(vst, 16 * 4 * 64)

        @blk.gpsimd
        def _(g):
            for vc in range(NVC):
                if vc >= 2:
                    g.wait_ge(kvm, vc - 1)
                g.dma_start(out=WVB[vc % 2][:], in_=wkv[vc]).then_inc(kwl, 16)
            g.wait_ge(ksk, 16 * NQC)
            g.wait_ge(vst, 16 * 4 * 64)
            g.collective_compute(
                "AllGather", A.bypass,
                replica_groups=[[0, 1], [2, 3], [4, 5], [6, 7]],
                ins=[k_own.opt()],
                outs=[k_gath.opt()],
            ).then_inc(ccs, 1)
            g.collective_compute(
                "AllGather", A.bypass,
                replica_groups=[[0, 1], [2, 3], [4, 5], [6, 7]],
                ins=[v_own.opt()],
                outs=[v_gath.opt()],
            ).then_inc(ccs, 1)
            g.wait_ge(ccs, 2)

        @blk.scalar
        def _(act):
            for vc in range(NQC, NVC):
                gv = vc - NQC
                for bt in range(4):
                    gvb = gv * 4 + bt
                    act.wait_ge(vtm, gvb + 1)
                    if gvb >= 2:
                        act.wait_ge(vst, 16 * 4 * (gvb - 1))
                    act.copy(out=VTT[gvb % 2][:], in_=VPT[gvb % 2][:])
                    act.maybe_drain_then_inc((vte, 1))

        @blk.vector
        def _(v):
            v.wait_ge(ld2, 32)
            for vc in range(NVC):
                v.wait_ge(kvm, vc + 1)
                if vc >= 2:
                    sem, val = kfree_evt(vc - 2)
                    v.wait_ge(ksk if sem == "ksk" else vtm, val)
                for ts in range(4):
                    v.tensor_scalar(out=KVT[vc % 2][:, ts * 512:(ts + 1) * 512],
                                    in0=KVP[ts][:], scalar1=BKV[:, vc:vc + 1],
                                    scalar2=None, op0=A.add)
                v.maybe_drain_then_inc((kte, 1))

        @blk.tensor
        def _(t):
            for vc in range(NVC):
                t.wait_ge(kwl, 16 * (vc + 1))
                if vc >= 1:
                    t.wait_ge(kte, vc)
                for dc in range(NDC):
                    for ts in range(4):
                        t.matmul(KVP[ts][:], WVB[vc % 2][:, dc, :],
                                 XN[:, dc, ts * 512:(ts + 1) * 512],
                                 start=(dc == 0), stop=(dc == NDC - 1))
                t.maybe_drain_then_inc((kvm, 1))
                if vc >= NQC:
                    gv = vc - NQC
                    t.wait_ge(kte, vc + 1)
                    for bt in range(4):
                        gvb = gv * 4 + bt
                        if gvb >= 2:
                            t.wait_ge(vte, gvb - 1)
                        for j in range(4):
                            tsub = bt * 4 + j
                            t.transpose(VPT[gvb % 2][:, j, :],
                                        KVT[vc % 2][:, tsub * 128:(tsub + 1) * 128],
                                        idbf[:])
                        t.maybe_drain_then_inc((vtm, 1))
    es4.close()
    mid.close()
    if upto < 5:
        root.close(); return nc

    es5 = ExitStack()
    SALL = es5.enter_context(nc.sbuf_tensor("SALL", [128, NH, SFULL], F16))
    QS = [es5.enter_context(nc.sbuf_tensor(f"QS{i}", [128, NQC, 128], BF)) for i in range(2)]
    KBB = [es5.enter_context(nc.sbuf_tensor(f"KBB{i}", [128, NQC, 512], BF)) for i in range(2)]
    VBB = [es5.enter_context(nc.sbuf_tensor(f"VBB{i}", [128, 4, 2048], BF)) for i in range(2)]
    MS = [es5.enter_context(nc.sbuf_tensor(f"MS{i}", [128, 512], F32)) for i in range(2)]
    MBAL = es5.enter_context(nc.sbuf_tensor("MBAL", [128, NH * 8], F32))
    MMAX = es5.enter_context(nc.sbuf_tensor("MMAX", [128, NH], F32))
    NEGM = es5.enter_context(nc.sbuf_tensor("NEGM", [128, NH], F32))
    LSUM = es5.enter_context(nc.sbuf_tensor("LSUM", [128, NH], F32))
    RLV = es5.enter_context(nc.sbuf_tensor("RLV", [128, NH], F32))
    PTS = [es5.enter_context(nc.sbuf_tensor(f"PTS{i}", [128, 4, 128], F16)) for i in range(NH)]
    OSB = es5.enter_context(nc.sbuf_tensor("OSB", [128, NH * 256], F16))
    VALT = es5.enter_context(nc.sbuf_tensor("VALT", [128, NQC, 128], F16))
    SPS = [es5.enter_context(nc.psum_tensor(f"SPS{i}", [128, 512], F32)) for i in range(2)]
    PTP = [es5.enter_context(nc.psum_tensor(f"PTP{i}", [128, 4, 128], F16)) for i in range(2)]
    OPP = [es5.enter_context(nc.psum_tensor(f"OPP{i}", [128, 2, 256], F32)) for i in range(4)]
    qld = es5.enter_context(nc.semaphore("qld"))
    msl = es5.enter_context(nc.semaphore("msl"))
    msf = es5.enter_context(nc.semaphore("msf"))
    kld = es5.enter_context(nc.semaphore("kld"))
    vld = es5.enter_context(nc.semaphore("vld"))
    smm = es5.enter_context(nc.semaphore("smm"))
    sev = es5.enter_context(nc.semaphore("sev"))
    nmd = es5.enter_context(nc.semaphore("nmd"))
    expd = es5.enter_context(nc.semaphore("expd"))
    ptm = es5.enter_context(nc.semaphore("ptm"))
    ptsc = es5.enter_context(nc.semaphore("ptsc"))
    avm = es5.enter_context(nc.semaphore("avm"))
    osd = es5.enter_context(nc.semaphore("osd"))
    vst2 = es5.enter_context(nc.semaphore("vst2"))
    gd5 = es5.enter_context(nc.semaphore("gd5"))

    PTSC_AT_VALS = [8 * KB_PRE[s + 1] + 4 * (s + 1) for s in range(NSLOT)]
    with nc.Block() as blk:
        @blk.sync
        def _(sync):
            for s in range(NSLOT):
                kbc = KBCS[s]
                if s >= 2:
                    sync.wait_ge(smm, 8 * KB_PRE[s - 1])
                sync.dma_start(out=QS[s % 2][:],
                               in_=q_sc[:, :, s * 128:(s + 1) * 128].rearrange("q p t -> p q t")
                               ).then_inc(qld, 16)
                if s >= 2:
                    sync.wait_ge(msf, s - 1)
                sync.dma_start(out=MS[s % 2][:], in_=masks_in[s]).then_inc(msl, 16)
                for kb in range(kbc):
                    g = KB_PRE[s] + kb
                    if g >= 2:
                        sync.wait_ge(smm, 8 * (g - 1))
                    for i in range(4):
                        r, lt = i % 2, 2 * kb + i // 2
                        sync.dma_start(
                            out=KBB[g % 2][:, :, i * 128:(i + 1) * 128],
                            in_=k_gath[r * 2048:(r + 1) * 2048, lt * 128:(lt + 1) * 128]
                            .rearrange("(c p) t -> p c t", p=128)
                        ).then_inc(kld, 16)
                for kb2 in range(kbc):
                    g2 = KB_PRE[s] + kb2
                    if g2 >= 2:
                        sync.wait_ge(avm, g2 - 1)
                    for i in range(4):
                        r, lt = i % 2, 2 * kb2 + i // 2
                        sync.dma_start(
                            out=VBB[g2 % 2][:, i, :],
                            in_=v_gath[r * 2048 + lt * 128: r * 2048 + (lt + 1) * 128, :]
                        ).then_inc(vld, 16)
                sync.wait_ge(ptsc, PTSC_AT_VALS[s])
                sync.dma_start(out=vals_sc[s],
                               in_=VALT[:].rearrange("p q t -> p (q t)")).then_inc(vst2, 16)
            sync.wait_ge(vst2, 16 * NSLOT)

        @blk.tensor
        def _(t):
            ptpu = [0]
            for s in range(NSLOT):
                kbc = KBCS[s]
                t.wait_ge(qld, 16 * (s + 1))
                for kb in range(kbc):
                    g = KB_PRE[s] + kb
                    t.wait_ge(kld, 64 * (g + 1))
                    for h in range(NH):
                        ghb = 8 * g + h
                        if ghb >= 2:
                            t.wait_ge(sev, ghb - 1)
                        for qi in range(2):
                            t.matmul(SPS[ghb % 2][:], QS[s % 2][:, 2 * h + qi, :],
                                     KBB[g % 2][:, 2 * h + qi, :],
                                     start=(qi == 0), stop=(qi == 1))
                        t.maybe_drain_then_inc((smm, 1))
                if s >= 1:
                    t.wait_ge(osd, s)
                for kb2 in range(kbc):
                    g2 = KB_PRE[s] + kb2
                    t.wait_ge(vld, 64 * (g2 + 1))
                    pend = []
                    for h in range(NH + 1):
                        if h < NH:
                            t.wait_ge(expd, 8 * s + h + 1)
                            if ptpu[0] >= 2:
                                t.wait_ge(ptsc, ptpu[0] - 1)
                            for j in range(4):
                                t.transpose(PTP[ptpu[0] % 2][:, j, :],
                                            SALL[:, h, (kb2 * 4 + j) * 128:(kb2 * 4 + j) * 128 + 128],
                                            idfp[:])
                            t.maybe_drain_then_inc((ptm, 1))
                            ptpu[0] += 1
                            pend.append((h, ptpu[0]))
                        if (h >= 1 and pend) or h == NH:
                            hh, pcnt = pend.pop(0)
                            h_save = h
                            h = hh
                            t.wait_ge(ptsc, pcnt)
                            for j in range(4):
                                t.matmul(OPP[h // 2][:, h % 2, :], PTS[h][:, j, :],
                                         VBB[g2 % 2][:, j, h * 256:(h + 1) * 256],
                                         start=(kb2 == 0 and j == 0 and h % 2 == 0),
                                         stop=(kb2 == kbc - 1 and j == 3))
                            h = h_save
                    t.maybe_drain_then_inc((avm, 1))
                t.wait_ge(osd, s + 1)
                for vb in range(4):
                    if ptpu[0] >= 2:
                        t.wait_ge(ptsc, ptpu[0] - 1)
                    for j in range(4):
                        cc = 4 * vb + j
                        t.transpose(PTP[ptpu[0] % 2][:, j, :],
                                    OSB[:, cc * 128:(cc + 1) * 128], idfp[:])
                    t.maybe_drain_then_inc((ptm, 1))
                    ptpu[0] += 1

        @blk.vector
        def _(v):
            g5 = [0]
            def G():
                g5[0] += 1
                v.maybe_drain_then_inc((gd5, 1))
                v.wait_ge(gd5, g5[0])
            for s in range(NSLOT):
                kbc = KBCS[s]
                diag = kbc - 1
                for kb in range(kbc):
                    g = KB_PRE[s] + kb
                    if kb == diag:
                        v.wait_ge(msl, 16 * (s + 1))
                    for h in range(NH):
                        ghb = 8 * g + h
                        v.wait_ge(smm, ghb + 1)
                        if kb == diag:
                            v.tensor_tensor(out=SPS[ghb % 2][:], in0=SPS[ghb % 2][:],
                                            in1=MS[s % 2][:], op=A.add)
                        v.tensor_scalar(out=SALL[:, h, kb * 512:(kb + 1) * 512],
                                        in0=SPS[ghb % 2][:], scalar1=0.0, scalar2=None,
                                        op0=A.add, op1=A.max,
                                        accum_out=MBAL[:, h * 8 + kb:h * 8 + kb + 1]
                                        ).then_inc(sev, 1)
                        if kb == diag:
                            G()
                            v.reduce_max(MMAX[:, h:h + 1], MBAL[:, h * 8:h * 8 + kbc], axis=AX)
                            G()
                            v.tensor_scalar(out=NEGM[:, h:h + 1], in0=MMAX[:, h:h + 1],
                                            scalar1=-1.0, scalar2=None, op0=A.mult)
                            v.maybe_drain_then_inc((nmd, 1))
                v.maybe_drain_then_inc((msf, 1))
                for h in range(NH):
                    v.wait_ge(expd, 8 * s + h + 1)
                    v.reciprocal(RLV[:, h:h + 1], LSUM[:, h:h + 1])
                G()
                v.wait_ge(avm, KB_PRE[s + 1])
                for h in range(NH):
                    v.tensor_scalar(out=OSB[:, h * 256:(h + 1) * 256],
                                    in0=OPP[h // 2][:, h % 2, :],
                                    scalar1=RLV[:, h:h + 1], scalar2=None, op0=A.mult)
                v.maybe_drain_then_inc((osd, 1))

        @blk.scalar
        def _(act):
            ptpu = [0]
            for s in range(NSLOT):
                kbc = KBCS[s]
                klen = kbc * 512
                for h in range(NH):
                    act.wait_ge(nmd, 8 * s + h + 1)
                    act.activation(out=SALL[:, h, 0:klen], in_=SALL[:, h, 0:klen],
                                   func=AF.Exp, bias=NEGM[:, h:h + 1], scale=1.0,
                                   accum_out=LSUM[:, h:h + 1])
                    act.maybe_drain_then_inc((expd, 1))
                for kb2 in range(kbc):
                    for h in range(NH):
                        act.wait_ge(ptm, ptpu[0] + 1)
                        act.copy(out=PTS[h][:], in_=PTP[ptpu[0] % 2][:])
                        act.maybe_drain_then_inc((ptsc, 1))
                        ptpu[0] += 1
                if s >= 1:
                    act.wait_ge(vst2, 16 * s)
                for vb in range(4):
                    act.wait_ge(ptm, ptpu[0] + 1)
                    act.copy(out=VALT[:, 4 * vb:4 * vb + 4, :], in_=PTP[ptpu[0] % 2][:])
                    act.maybe_drain_then_inc((ptsc, 1))
                    ptpu[0] += 1
    es5.close()
    if upto < 6:
        root.close(); return nc

    es6 = ExitStack()
    VALB = es6.enter_context(nc.sbuf_tensor("VALB", [128, NSLOT, NQC, 128], F16))
    WOB = [es6.enter_context(nc.sbuf_tensor(f"WOB{i}", [128, NQC, 1024], BF)) for i in range(2)]
    SOT = [es6.enter_context(nc.sbuf_tensor(f"SOT{i}", [128, 1024], F32)) for i in range(2)]
    SPO = [es6.enter_context(nc.psum_tensor(f"SPO{i}", [128, 512], F32)) for i in range(8)]
    vll = es6.enter_context(nc.semaphore("vll"))
    wol = es6.enter_context(nc.semaphore("wol"))
    wof = es6.enter_context(nc.semaphore("wof"))
    somm = es6.enter_context(nc.semaphore("somm"))
    soev = es6.enter_context(nc.semaphore("soev"))
    sst = es6.enter_context(nc.semaphore("sst"))
    with nc.Block() as blk:
        @blk.sync
        def _(sync):
            for sl in range(NSLOT):
                sync.dma_start(out=VALB[:, sl, :, :],
                               in_=vals_sc[sl].rearrange("p (q t) -> p q t", q=NQC)
                               ).then_inc(vll, 16)
            for jq in range(4):
                for grp in range(4):
                    gj = jq * 4 + grp
                    for sl in range(4):
                        gsl = gj * 4 + sl
                        sync.wait_ge(soev, gsl + 1)
                        sync.dma_start(out=s_sc[grp * 4 + sl][:, jq * 1024:(jq + 1) * 1024],
                                       in_=SOT[gsl % 2][:]).then_inc(sst, 16)
            sync.wait_ge(sst, 16 * 64)

        @blk.gpsimd
        def _(g):
            for jq in range(4):
                if jq >= 2:
                    g.wait_ge(wof, jq - 1)
                g.dma_start(out=WOB[jq % 2][:],
                            in_=wo[:, :, jq * 1024:(jq + 1) * 1024].rearrange("c p j -> p c j")
                            ).then_inc(wol, 16)

        @blk.tensor
        def _(t):
            t.wait_ge(vll, 16 * NSLOT)
            for jq in range(4):
                t.wait_ge(wol, 16 * (jq + 1))
                for grp in range(4):
                    gj = jq * 4 + grp
                    if gj >= 1:
                        t.wait_ge(soev, 4 * gj)
                    for cc in range(NQC):
                        for sl in range(4):
                            for jb in range(2):
                                mm = t.matmul(SPO[sl * 2 + jb][:],
                                              VALB[:, grp * 4 + sl, cc, :],
                                              WOB[jq % 2][:, cc, jb * 512:(jb + 1) * 512],
                                              start=(cc == 0), stop=(cc == NQC - 1))
                    t.maybe_drain_then_inc((somm, 1))
                t.nop().then_inc(wof, 1) if False else None
                t.maybe_drain_then_inc((wof, 1))

        @blk.vector
        def _(v):
            for jq in range(4):
                for grp in range(4):
                    gj = jq * 4 + grp
                    v.wait_ge(somm, gj + 1)
                    for sl in range(4):
                        gsl = gj * 4 + sl
                        if gsl >= 2:
                            v.wait_ge(sst, 16 * (gsl - 1))
                        for jb in range(2):
                            v.tensor_copy(SOT[gsl % 2][:, jb * 512:(jb + 1) * 512],
                                          SPO[sl * 2 + jb][:])
                        v.maybe_drain_then_inc((soev, 1))
    es6.close()
    if upto < 7:
        root.close(); return nc

    es7 = ExitStack()
    BOS = es7.enter_context(nc.sbuf_tensor("BOS", [1, D], BF))
    BOB = es7.enter_context(nc.sbuf_tensor("BOB", [128, D], BF))
    DT = es7.enter_context(nc.sbuf_tensor("DT", [128, NIC, 1024], BF))
    HTA = es7.enter_context(nc.sbuf_tensor("HTA", [128, 28, 512], BF))
    HTB = es7.enter_context(nc.sbuf_tensor("HTB", [128, 28, 512], BF))
    SS = es7.enter_context(nc.sbuf_tensor("SS", [128, 1024], F32))
    XS = es7.enter_context(nc.sbuf_tensor("XS", [128, 1024], F32))
    GG = [es7.enter_context(nc.sbuf_tensor(f"GG{i}", [128, 512], F32)) for i in range(2)]
    SFB = es7.enter_context(nc.sbuf_tensor("SFB", [128, 1024], F32))
    OUTB = [es7.enter_context(nc.sbuf_tensor(f"OUTB{i}", [128, 512], F32)) for i in range(2)]
    PSD = [es7.enter_context(nc.psum_tensor(f"PSD{i}", [128, 512], F32)) for i in range(8)]
    bld = es7.enter_context(nc.semaphore("bld"))
    bpm7 = es7.enter_context(nc.semaphore("bpm7"))
    bobd = es7.enter_context(nc.semaphore("bobd"))
    dld = es7.enter_context(nc.semaphore("dld"))
    hld = es7.enter_context(nc.semaphore("hld"))
    haf = es7.enter_context(nc.semaphore("haf"))
    hbf = es7.enter_context(nc.semaphore("hbf"))
    dmm = es7.enter_context(nc.semaphore("dmm"))
    gsd = es7.enter_context(nc.semaphore("gsd"))
    gfr = es7.enter_context(nc.semaphore("gfr"))
    sxl = es7.enter_context(nc.semaphore("sxl"))
    ssfre = es7.enter_context(nc.semaphore("ssfre"))
    outd = es7.enter_context(nc.semaphore("outd"))
    ost = es7.enter_context(nc.semaphore("ost"))
    with nc.Block() as blk:
        @blk.gpsimd
        def _(g):
            g.dma_start(out=BOS[:], in_=bo_in).then_inc(bld, 16)
            for jq in range(4):
                if jq >= 1:
                    g.wait_ge(dmm, 4 * jq)
                for pc in range(8):
                    g.dma_start(out=DT[:, 7 * pc:7 * pc + 7, :],
                                in_=wd[7 * pc:7 * pc + 7, :, jq * 1024:(jq + 1) * 1024]
                                .rearrange("i p j -> p i j")).then_inc(dld, 16)
                for tt in range(4):
                    gtt = 4 * jq + tt
                    if gtt >= 1:
                        g.wait_ge(haf, gtt)
                    for pc in range(4):
                        g.dma_start(out=HTA[:, 7 * pc:7 * pc + 7, :],
                                    in_=h_sc[7 * pc:7 * pc + 7, :, tt * 512:(tt + 1) * 512]
                                    .rearrange("i p t -> p i t")).then_inc(hld, 16)
                    if gtt >= 1:
                        g.wait_ge(hbf, gtt)
                    for pc in range(4):
                        g.dma_start(out=HTB[:, 7 * pc:7 * pc + 7, :],
                                    in_=h_sc[28 + 7 * pc:28 + 7 * pc + 7, :, tt * 512:(tt + 1) * 512]
                                    .rearrange("i p t -> p i t")).then_inc(hld, 16)

        @blk.tensor
        def _(t):
            t.wait_ge(bld, 16)
            for jb in range(8):
                t.matmul(PSD[jb][:], ones_bf[:], BOS[:, jb * 512:(jb + 1) * 512],
                         start=True, stop=True)
            t.maybe_drain_then_inc((bpm7, 1))
            for jq in range(4):
                for tt in range(4):
                    gtt = 4 * jq + tt
                    t.wait_ge(hld, 16 * 4 * (2 * gtt + 1))
                    if gtt == 0:
                        t.wait_ge(bobd, 1)
                    else:
                        t.wait_ge(gsd, 8 * gtt)
                    for ic in range(NIC):
                        if tt == 0 and ic % 7 == 0:
                            t.wait_ge(dld, 16 * (8 * jq + ic // 7 + 1))
                        if ic == 28:
                            t.wait_ge(hld, 16 * 4 * (2 * gtt + 2))
                        ht = HTA if ic < 28 else HTB
                        icl = ic if ic < 28 else ic - 28
                        for tsub in range(4):
                            for jb in range(2):
                                mm = t.matmul(PSD[tsub * 2 + jb][:],
                                              ht[:, icl, tsub * 128:(tsub + 1) * 128],
                                              DT[:, ic, jb * 512:(jb + 1) * 512],
                                              start=(ic == 0), stop=(ic == NIC - 1))
                                if tsub == 3 and jb == 1 and ic == 27:
                                    mm.then_inc(haf, 1)
                                if tsub == 3 and jb == 1 and ic == NIC - 1:
                                    mm.then_inc(hbf, 1)

                    t.maybe_drain_then_inc((dmm, 1))

        @blk.sync
        def _(sync):
            for jq in range(4):
                for tt in range(4):
                    for tsub in range(4):
                        ts7 = (4 * jq + tt) * 4 + tsub
                        slot = tt * 4 + tsub
                        if ts7 >= 1:
                            sync.wait_ge(ssfre, ts7)
                        sync.dma_start(out=SS[:],
                                       in_=s_sc[slot][:, jq * 1024:(jq + 1) * 1024]
                                       ).then_inc(sxl, 16)
                        sync.dma_start(out=XS[:],
                                       in_=x_mine[slot][:, jq * 1024:(jq + 1) * 1024]
                                       ).then_inc(sxl, 16)
                        for jb in range(2):
                            ko = ts7 * 2 + jb
                            sync.wait_ge(outd, ko + 1)
                            sync.dma_start(out=out_mine[slot][:, jq * 1024 + jb * 512:
                                                              jq * 1024 + jb * 512 + 512],
                                           in_=OUTB[ko % 2][:]).then_inc(ost, 16)
            sync.wait_ge(ost, 16 * 128)

        @blk.scalar
        def _(act):
            for jq in range(4):
                for tt in range(4):
                    gtt = 4 * jq + tt
                    act.wait_ge(dmm, gtt + 1)
                    for tsub in range(4):
                        for jb in range(2):
                            k = gtt * 8 + tsub * 2 + jb
                            if k >= 2:
                                act.wait_ge(gfr, k - 1)
                            act.activation(out=GG[k % 2][:], in_=PSD[tsub * 2 + jb][:],
                                           func=AF.Silu, bias=0.0, scale=1.0)
                            act.maybe_drain_then_inc((gsd, 1))

        @blk.vector
        def _(v):
            v.wait_ge(bpm7, 1)
            for jb in range(8):
                v.tensor_copy(BOB[:, jb * 512:(jb + 1) * 512], PSD[jb][:])
            v.maybe_drain_then_inc((bobd, 1))
            for jq in range(4):
                for tt in range(4):
                    for tsub in range(4):
                        ts7 = (4 * jq + tt) * 4 + tsub
                        v.wait_ge(sxl, 32 * (ts7 + 1))
                        v.tensor_tensor(out=SFB[:], in0=SS[:],
                                        in1=BOB[:, jq * 1024:(jq + 1) * 1024], op=A.add)
                        for jb in range(2):
                            k = (4 * jq + tt) * 8 + tsub * 2 + jb
                            ko = ts7 * 2 + jb
                            v.wait_ge(gsd, k + 1)
                            if ko >= 2:
                                v.wait_ge(ost, 16 * (ko - 1))
                            v.tensor_tensor(out=OUTB[ko % 2][:],
                                            in0=SFB[:, jb * 512:(jb + 1) * 512],
                                            in1=GG[k % 2][:], op=A.mult).then_inc(gfr, 1)
                            last = v.tensor_tensor(out=OUTB[ko % 2][:], in0=OUTB[ko % 2][:],
                                                   in1=XS[:, jb * 512:(jb + 1) * 512],
                                                   op=A.add)
                            if jb == 1:
                                last.then_inc(ssfre, 1)
                            v.maybe_drain_then_inc((outd, 1))
    es7.close()
    root.close()
    return nc



def _lay_w(wt):
    d_in, d_out = wt.shape
    ndc = d_in // 128
    noc = d_out // 128
    return np.ascontiguousarray(wt.reshape(ndc, 128, noc, 128).transpose(2, 1, 0, 3))


def _host_prepare(inputs):
    x = np.asarray(inputs["L0_attn_x"], np.float32)
    w_qkv = np.asarray(inputs["L0_attn_w_qkv"], np.float32)
    b_qkv = np.asarray(inputs["L0_attn_b_qkv"], np.float32)
    w_o = np.asarray(inputs["L0_attn_w_o"], np.float32)
    b_o = np.asarray(inputs["L0_attn_b_o"], np.float32)
    gate_w = np.asarray(inputs["L0_ffn_gate_w"], np.float32)
    u = np.asarray(inputs["L0_ffn_u"], np.float32)
    g = np.asarray(inputs["L0_ffn_g"], np.float32)
    d = np.asarray(inputs["L0_ffn_d"], np.float32)

    shared = {}
    shared["wq"] = _lay_w(np.ascontiguousarray(w_qkv[0:2048].T))
    shared["wkv"] = _lay_w(np.ascontiguousarray(w_qkv[2048:6144].T))
    shared["wu"] = _lay_w(np.ascontiguousarray(u.T))
    shared["wg"] = _lay_w(np.ascontiguousarray(g.T))
    shared["wd"] = np.ascontiguousarray(d.T.reshape(NIC, 128, D))
    shared["wo"] = np.ascontiguousarray(w_o.T.reshape(NQC, 128, D))
    shared["gwt"] = np.ascontiguousarray(gate_w.T.reshape(NDC, 128, NE).transpose(1, 0, 2))
    shared["bq_in"] = np.ascontiguousarray(b_qkv[0:2048].reshape(NQC, 128).T)
    shared["bkv_in"] = np.ascontiguousarray(b_qkv[2048:6144].reshape(NVC, 128).T)
    shared["bo_in"] = np.ascontiguousarray(b_o.reshape(1, D))
    shared["eye_in"] = np.eye(128, dtype=np.float32)
    shared["ones_in"] = np.ones((1, 128), np.float32)
    smat = np.zeros((NE, NIC * 128), np.float32)
    rows = np.arange(NIC * 128)
    smat[rows // 448, rows] = 1.0
    shared["smat_in"] = smat

    def masks_for(parity):
        m = np.zeros((NSLOT, 128, 512), np.float32)
        for s in range(NSLOT):
            qt = 2 * s + parity
            base = (KBCS[s] - 1) * 512
            qpos = qt * 128 + np.arange(128)[:, None]
            kpos = base + np.arange(512)[None, :]
            m[s] = np.where(kpos <= qpos, 0.0, -30000.0).astype(np.float32)
        return m

    masks = [masks_for(0), masks_for(1)]

    in_maps = []
    for c in range(8):
        b, p = c // 2, c % 2
        xb = x[b].reshape(32, 128, D)
        im = dict(shared)
        im["x_mine"] = np.ascontiguousarray(xb[p::2])
        im["masks_in"] = masks[p]
        for kk in im:
            assert im[kk].dtype == np.float32, (kk, im[kk].dtype)
        in_maps.append(im)
    return in_maps


def kernel(**inputs):
    in_maps = _host_prepare(inputs)
    nc = build_nc(dbg=False)
    res = run_bass_kernel_spmd(nc, in_maps, core_ids=list(range(8)))
    out = np.empty((4, SFULL, D), np.float32)
    for c in range(8):
        b, p = c // 2, c % 2
        om = np.asarray(res.results[c]["out_mine"])
        ov = out[b].reshape(32, 128, D)
        ov[p::2] = om
    return out


# version 14
# speedup vs baseline: 1.2561x; 1.0566x over previous
import numpy as np
import ml_dtypes
import concourse.bass as bass
import concourse.mybir as mybir
from concourse.alu_op_type import AluOpType as A
from concourse.bass_utils import run_bass_kernel_spmd
from contextlib import ExitStack

F32 = mybir.dt.float32
BF = mybir.dt.bfloat16
F16 = mybir.dt.float16
AF = mybir.ActivationFunctionType
AX = mybir.AxisListType.X

D = 4096
NDC = 32
TOK = 2048
NSLOT = 16
SFULL = 4096
NQC = 16
NVC = 32
NIC = 56
NH = 8
HD = 256
NE = 16
EPS = 1e-5

KBCS = [(s + 2) // 2 for s in range(NSLOT)]
KB_PRE = [0]
for s in range(NSLOT):
    KB_PRE.append(KB_PRE[-1] + KBCS[s])
KB_TOT = KB_PRE[-1]


def build_nc(dbg=False, upto=99):
    nc = bass.Bass("TRN2", target_bir_lowering=False)

    x_mine = nc.dram_tensor("x_mine", [NSLOT, 128, D], F32, kind="ExternalInput").ap()
    wq = nc.dram_tensor("wq", [NQC, 128, NDC, 128], F32, kind="ExternalInput").ap()
    wkv = nc.dram_tensor("wkv", [NVC, 128, NDC, 128], F32, kind="ExternalInput").ap()
    wu = nc.dram_tensor("wu", [NIC, 128, NDC, 128], F32, kind="ExternalInput").ap()
    wg = nc.dram_tensor("wg", [NIC, 128, NDC, 128], F32, kind="ExternalInput").ap()
    wd = nc.dram_tensor("wd", [NIC, 128, D], F32, kind="ExternalInput").ap()
    wo = nc.dram_tensor("wo", [NQC, 128, D], F32, kind="ExternalInput").ap()
    gwt = nc.dram_tensor("gwt", [128, NDC, NE], F32, kind="ExternalInput").ap()
    bq_in = nc.dram_tensor("bq_in", [128, NQC], F32, kind="ExternalInput").ap()
    bkv_in = nc.dram_tensor("bkv_in", [128, NVC], F32, kind="ExternalInput").ap()
    bo_in = nc.dram_tensor("bo_in", [1, D], F32, kind="ExternalInput").ap()
    eye_in = nc.dram_tensor("eye_in", [128, 128], F32, kind="ExternalInput").ap()
    ones_in = nc.dram_tensor("ones_in", [1, 128], F32, kind="ExternalInput").ap()
    smat_in = nc.dram_tensor("smat_in", [NE, NIC * 128], F32, kind="ExternalInput").ap()
    masks_in = nc.dram_tensor("masks_in", [NSLOT, 128, 512], F32, kind="ExternalInput").ap()

    out_mine = nc.dram_tensor("out_mine", [NSLOT, 128, D], F32, kind="ExternalOutput").ap()

    sk = "ExternalOutput" if dbg else "Internal"
    q_sc = nc.dram_tensor("q_sc", [NQC, 128, TOK], BF, kind=sk).ap()
    k_own = nc.dram_tensor("k_own", [2048, 2048], BF, kind="Internal").ap()
    v_own = nc.dram_tensor("v_own", [2048, 2048], BF, kind="Internal").ap()
    k_gath = nc.dram_tensor("k_gath", [4096, 2048], BF, kind="Internal").ap()
    v_gath = nc.dram_tensor("v_gath", [4096, 2048], BF, kind="Internal").ap()
    if dbg:
        kg_dbg = nc.dram_tensor("kg_dbg", [4096, 2048], BF, kind="ExternalOutput").ap()
        vg_dbg = nc.dram_tensor("vg_dbg", [4096, 2048], BF, kind="ExternalOutput").ap()
    h_sc = nc.dram_tensor("h_sc", [NIC, 128, TOK], BF, kind=sk).ap()
    s_sc = nc.dram_tensor("s_sc", [NSLOT, 128, D], F32, kind=sk).ap()
    vals_sc = nc.dram_tensor("vals_sc", [NSLOT, 128, TOK], F16, kind=sk).ap()
    if dbg:
        xn_dbg = nc.dram_tensor("xn_dbg", [128, NDC, TOK], BF, kind="ExternalOutput").ap()
        wtb_dbg = nc.dram_tensor("wtb_dbg", [NE, TOK], BF, kind="ExternalOutput").ap()

    root = ExitStack()
    idbf = root.enter_context(nc.sbuf_tensor("idbf", [128, 128], BF))
    idfp = root.enter_context(nc.sbuf_tensor("idfp", [128, 128], F16))
    ones_bf = root.enter_context(nc.sbuf_tensor("ones_bf", [1, 128], BF))

    BQ = root.enter_context(nc.sbuf_tensor("BQ", [128, NQC], F32))
    BKV = root.enter_context(nc.sbuf_tensor("BKV", [128, NVC], F32))
    ld1 = root.enter_context(nc.semaphore("ld1"))
    ld2 = root.enter_context(nc.semaphore("ld2"))

    mid = ExitStack()
    XN = mid.enter_context(nc.sbuf_tensor("XN", [128, NDC, TOK], BF))
    SQD = mid.enter_context(nc.sbuf_tensor("SQD", [128, D], BF))
    mid2 = ExitStack()
    WTB = mid2.enter_context(nc.sbuf_tensor("WTB", [NE, TOK], BF))
    S_sb = mid2.enter_context(nc.sbuf_tensor("S_sb", [NE, NIC * 128], BF))
    GWS = mid2.enter_context(nc.sbuf_tensor("GWS", [128, NDC, NE], BF))

    def ln_alloc(es, tag):
        b = {}
        b["xsb"] = [es.enter_context(nc.sbuf_tensor(f"{tag}xsb{i}", [128, D], F32)) for i in range(2)]
        b["xnt"] = es.enter_context(nc.sbuf_tensor(f"{tag}xnt", [128, D], BF))
        b["ssq"] = [es.enter_context(nc.sbuf_tensor(f"{tag}ssq{i}", [128, 1], F32)) for i in range(2)]
        b["sm"] = es.enter_context(nc.sbuf_tensor(f"{tag}sm", [128, 1], F32))
        b["mn"] = [es.enter_context(nc.sbuf_tensor(f"{tag}mn{i}", [128, 1], F32)) for i in range(2)]
        b["ex2"] = es.enter_context(nc.sbuf_tensor(f"{tag}ex2", [128, 1], F32))
        b["msq"] = es.enter_context(nc.sbuf_tensor(f"{tag}msq", [128, 1], F32))
        b["var"] = es.enter_context(nc.sbuf_tensor(f"{tag}var", [128, 1], F32))
        b["ve"] = es.enter_context(nc.sbuf_tensor(f"{tag}ve", [128, 1], F32))
        b["rv"] = [es.enter_context(nc.sbuf_tensor(f"{tag}rv{i}", [128, 1], F32)) for i in range(2)]
        b["rsd"] = [es.enter_context(nc.sbuf_tensor(f"{tag}rsd{i}", [128, 1], F32)) for i in range(2)]
        b["pt"] = [es.enter_context(nc.psum_tensor(f"{tag}pt{i}", [128, 4, 128], BF)) for i in range(2)]
        for s in ["ldx", "fre", "sqs", "rvd", "rsb", "xnr", "ptd", "evd", "gd"]:
            b[s] = es.enter_context(nc.semaphore(f"{tag}{s}"))
        b["gdn"] = [0]
        return b

    def ln_sync(sync, b, x_ap, tiles, g0):
        for i, t in enumerate(tiles):
            gs = g0 + i
            if gs >= 2:
                sync.wait_ge(b["fre"], 2 * (gs - 1))
            sync.dma_start(out=b["xsb"][gs % 2][:], in_=x_ap[t]).then_inc(b["ldx"], 16)

    def ln_act(act, b, nsub, g0):
        for i in range(nsub):
            gs = g0 + i
            act.wait_ge(b["ldx"], 16 * (gs + 1))
            act.activation(out=SQD[:], in_=b["xsb"][gs % 2][:], func=AF.Square,
                           bias=0.0, scale=1.0,
                           accum_out=b["ssq"][gs % 2][:]).then_inc(b["fre"], 1)
            act.maybe_drain_then_inc((b["sqs"], 1))
            act.wait_ge(b["rvd"], gs + 1)
            act.activation(out=b["rsd"][gs % 2][:], in_=b["rv"][gs % 2][:], func=AF.Sqrt,
                           bias=0.0, scale=1.0)
            act.maybe_drain_then_inc((b["rsb"], 1))
            for bt in range(8):
                gb = 8 * gs + bt
                act.wait_ge(b["ptd"], gb + 1)
                act.copy(out=XN[:, 4 * bt:4 * bt + 4, i * 128:(i + 1) * 128],
                         in_=b["pt"][bt % 2][:]).then_inc(b["evd"], 1)

    def ln_dve(v, b, nsub, g0):
        def G():
            b["gdn"][0] += 1
            v.maybe_drain_then_inc((b["gd"], 1))
            v.wait_ge(b["gd"], b["gdn"][0])
        for i in range(nsub):
            gs = g0 + i
            v.wait_ge(b["ldx"], 16 * (gs + 1))
            v.reduce_sum(b["sm"][:], b["xsb"][gs % 2][:], axis=AX)
            G()
            v.tensor_scalar(out=b["mn"][gs % 2][:], in0=b["sm"][:], scalar1=1.0 / D,
                            scalar2=None, op0=A.mult)
            v.wait_ge(b["sqs"], gs + 1)
            v.tensor_scalar(out=b["ex2"][:], in0=b["ssq"][gs % 2][:], scalar1=1.0 / D,
                            scalar2=None, op0=A.mult)
            G()
            v.tensor_tensor(out=b["msq"][:], in0=b["mn"][gs % 2][:], in1=b["mn"][gs % 2][:], op=A.mult)
            G()
            v.tensor_tensor(out=b["var"][:], in0=b["ex2"][:], in1=b["msq"][:], op=A.subtract)
            G()
            v.tensor_scalar(out=b["ve"][:], in0=b["var"][:], scalar1=EPS, scalar2=None, op0=A.add)
            G()
            v.reciprocal(b["rv"][gs % 2][:], b["ve"][:])
            v.maybe_drain_then_inc((b["rvd"], 1))
            v.wait_ge(b["rsb"], gs + 1)
            if gs >= 1:
                v.wait_ge(b["ptd"], 8 * gs)
            v.tensor_scalar(out=b["xnt"][:], in0=b["xsb"][gs % 2][:],
                            scalar1=b["mn"][gs % 2][:], scalar2=b["rsd"][gs % 2][:],
                            op0=A.subtract, op1=A.mult).then_inc(b["fre"], 1)
            v.maybe_drain_then_inc((b["xnr"], 1))

    def ln_pe(t, b, nsub, g0):
        for i in range(nsub):
            gs = g0 + i
            t.wait_ge(b["xnr"], gs + 1)
            for bt in range(8):
                gb = 8 * gs + bt
                if gb >= 2:
                    t.wait_ge(b["evd"], gb - 1)
                for j in range(4):
                    t.transpose(b["pt"][bt % 2][:, j, :],
                                b["xnt"][:, (4 * bt + j) * 128:(4 * bt + j) * 128 + 128],
                                idbf[:])
                t.maybe_drain_then_inc((b["ptd"], 1))

    es0 = ExitStack()
    lnb = ln_alloc(es0, "l0")
    with nc.Block() as blk:
        @blk.gpsimd
        def _(g):
            g.dma_start(out=idbf[:], in_=eye_in).then_inc(ld1, 16)
            g.dma_start(out=idfp[:], in_=eye_in).then_inc(ld1, 16)
            g.dma_start(out=ones_bf[:], in_=ones_in).then_inc(ld1, 16)
            g.dma_start(out=GWS[:], in_=gwt).then_inc(ld1, 16)
            g.dma_start(out=S_sb[:], in_=smat_in).then_inc(ld1, 16)

        @blk.sync
        def _(sync):
            sync.dma_start(out=BQ[:], in_=bq_in).then_inc(ld2, 16)
            sync.dma_start(out=BKV[:], in_=bkv_in).then_inc(ld2, 16)
            ln_sync(sync, lnb, x_mine, list(range(16)), 0)

        @blk.scalar
        def _(act):
            ln_act(act, lnb, 16, 0)

        @blk.vector
        def _(v):
            ln_dve(v, lnb, 16, 0)

        @blk.tensor
        def _(t):
            t.wait_ge(ld1, 16)
            ln_pe(t, lnb, 16, 0)
    es0.close()
    if upto < 1:
        mid2.close(); mid.close(); root.close(); return nc

    es1 = ExitStack()
    WQB = [es1.enter_context(nc.sbuf_tensor(f"WQB{i}", [128, NDC, 128], BF)) for i in range(2)]
    QTILE = [es1.enter_context(nc.sbuf_tensor(f"QTILE{i}", [128, TOK], BF)) for i in range(2)]
    QP = [es1.enter_context(nc.psum_tensor(f"QP{i}", [128, 512], F32)) for i in range(8)]
    wld = es1.enter_context(nc.semaphore("wld"))
    qmm = es1.enter_context(nc.semaphore("qmm"))
    qtl = es1.enter_context(nc.semaphore("qtl"))
    qst = es1.enter_context(nc.semaphore("qst"))
    with nc.Block() as blk:
        @blk.gpsimd
        def _(g):
            for qc in range(NQC):
                if qc >= 2:
                    g.wait_ge(qmm, qc - 1)
                g.dma_start(out=WQB[qc % 2][:], in_=wq[qc]).then_inc(wld, 16)

        @blk.tensor
        def _(t):
            for qc in range(NQC):
                t.wait_ge(wld, 16 * (qc + 1))
                if qc >= 2:
                    t.wait_ge(qtl, qc - 1)
                bs = (qc % 2) * 4
                for dc in range(NDC):
                    for ts in range(4):
                        t.matmul(QP[bs + ts][:], WQB[qc % 2][:, dc, :],
                                 XN[:, dc, ts * 512:(ts + 1) * 512],
                                 start=(dc == 0), stop=(dc == NDC - 1))
                t.maybe_drain_then_inc((qmm, 1))

        @blk.vector
        def _(v):
            v.wait_ge(ld2, 16)
            for qc in range(NQC):
                v.wait_ge(qmm, qc + 1)
                if qc >= 2:
                    v.wait_ge(qst, 16 * (qc - 1))
                bs = (qc % 2) * 4
                for ts in range(4):
                    v.tensor_scalar(out=QTILE[qc % 2][:, ts * 512:(ts + 1) * 512],
                                    in0=QP[bs + ts][:], scalar1=BQ[:, qc:qc + 1],
                                    scalar2=None, op0=A.add)
                v.maybe_drain_then_inc((qtl, 1))

        @blk.sync
        def _(sync):
            for qc in range(NQC):
                sync.wait_ge(qtl, qc + 1)
                sync.dma_start(out=q_sc[qc], in_=QTILE[qc % 2][:]).then_inc(qst, 16)
            sync.wait_ge(qst, 16 * NQC)
            if dbg and upto == 1:
                sync.dma_start(out=xn_dbg, in_=XN[:]).then_inc(qst, 16)
                sync.wait_ge(qst, 16 * NQC + 16)
    es1.close()
    if upto < 2:
        mid2.close(); mid.close(); root.close(); return nc

    es2 = ExitStack()
    RPB = es2.enter_context(nc.psum_tensor("RPB", [128, 256], F32))
    TPW = [es2.enter_context(nc.psum_tensor(f"TPW{i}", [16, 128], BF)) for i in range(2)]
    WBF = es2.enter_context(nc.sbuf_tensor("WBF", [128, 256], BF))
    rlg = es2.enter_context(nc.sbuf_tensor("rlg", [128, NE], F32))
    rm1 = es2.enter_context(nc.sbuf_tensor("rm1", [128, 1], F32))
    rnm = es2.enter_context(nc.sbuf_tensor("rnm", [128, 1], F32))
    rmsk = es2.enter_context(nc.sbuf_tensor("rmsk", [128, NE], F32))
    rlm = es2.enter_context(nc.sbuf_tensor("rlm", [128, NE], F32))
    rm2 = es2.enter_context(nc.sbuf_tensor("rm2", [128, 1], F32))
    rsel = es2.enter_context(nc.sbuf_tensor("rsel", [128, NE], F32))
    rel = es2.enter_context(nc.sbuf_tensor("rel", [128, NE], F32))
    rwn = es2.enter_context(nc.sbuf_tensor("rwn", [128, NE], F32))
    rden = es2.enter_context(nc.sbuf_tensor("rden", [128, 1], F32))
    rrd = es2.enter_context(nc.sbuf_tensor("rrd", [128, 1], F32))
    rwf = es2.enter_context(nc.sbuf_tensor("rwf", [128, NE], F32))
    rmm = es2.enter_context(nc.semaphore("rmm"))
    rwb = es2.enter_context(nc.semaphore("rwb"))
    rtw = es2.enter_context(nc.semaphore("rtw"))
    rwt = es2.enter_context(nc.semaphore("rwt"))
    rex = es2.enter_context(nc.semaphore("rex"))
    rnd = es2.enter_context(nc.semaphore("rnd"))
    rgd = es2.enter_context(nc.semaphore("rgd"))
    with nc.Block() as blk:
        @blk.tensor
        def _(t):
            t.wait_ge(ld1, 80)
            for sub in range(16):
                for dc in range(NDC):
                    t.matmul(RPB[:, sub * 16:sub * 16 + 16],
                             XN[:, dc, sub * 128:sub * 128 + 128],
                             GWS[:, dc, :], start=(dc == 0), stop=(dc == NDC - 1))
            t.maybe_drain_then_inc((rmm, 1))
            for sub in range(16):
                t.wait_ge(rwb, sub + 1)
                if sub >= 2:
                    t.wait_ge(rwt, sub - 1)
                t.transpose(TPW[sub % 2][:], WBF[:, sub * 16:sub * 16 + 16], idbf[:])
                t.maybe_drain_then_inc((rtw, 1))

        @blk.vector
        def _(v):
            g = [0]
            def G():
                g[0] += 1
                v.maybe_drain_then_inc((rgd, 1))
                v.wait_ge(rgd, g[0])
            v.wait_ge(rmm, 1)
            for sub in range(16):
                v.tensor_copy(rlg[:], RPB[:, sub * 16:sub * 16 + 16])
                G()
                v.reduce_max(rm1[:], rlg[:], axis=AX)
                G()
                v.tensor_scalar(out=rnm[:], in0=rm1[:], scalar1=-1.0, scalar2=None, op0=A.mult)
                v.tensor_scalar(out=rmsk[:], in0=rlg[:], scalar1=rm1[:], scalar2=None, op0=A.is_ge)
                G()
                v.scalar_tensor_tensor(out=rlm[:], in0=rmsk[:], scalar=-30000.0, in1=rlg[:],
                                       op0=A.mult, op1=A.add)
                G()
                v.reduce_max(rm2[:], rlm[:], axis=AX)
                G()
                v.tensor_scalar(out=rsel[:], in0=rlg[:], scalar1=rm2[:], scalar2=None, op0=A.is_ge)
                v.maybe_drain_then_inc((rnd, 1))
                v.wait_ge(rex, sub + 1)
                v.tensor_tensor(out=rwn[:], in0=rel[:], in1=rsel[:], op=A.mult)
                G()
                v.reduce_sum(rden[:], rwn[:], axis=AX)
                G()
                v.reciprocal(rrd[:], rden[:])
                G()
                v.tensor_scalar(out=rwf[:], in0=rwn[:], scalar1=rrd[:], scalar2=None, op0=A.mult)
                G()
                v.tensor_copy(WBF[:, sub * 16:sub * 16 + 16], rwf[:])
                v.maybe_drain_then_inc((rwb, 1))
            for sub in range(16):
                v.wait_ge(rtw, sub + 1)
                v.tensor_copy(WTB[:, sub * 128:(sub + 1) * 128], TPW[sub % 2][:])
                v.maybe_drain_then_inc((rwt, 1))

        @blk.scalar
        def _(act):
            for sub in range(16):
                act.wait_ge(rnd, sub + 1)
                act.activation(out=rel[:], in_=rlg[:], func=AF.Exp, bias=rnm[:], scale=1.0)
                act.maybe_drain_then_inc((rex, 1))
    es2.close()
    if upto < 3:
        mid2.close(); mid.close(); root.close(); return nc

    es3 = ExitStack()
    UB = [es3.enter_context(nc.sbuf_tensor(f"UB{i}", [128, NDC, 128], BF)) for i in range(2)]
    GB = [es3.enter_context(nc.sbuf_tensor(f"GB{i}", [128, NDC, 128], BF)) for i in range(2)]
    SG = es3.enter_context(nc.sbuf_tensor("SG", [128, TOK], BF))
    H1 = es3.enter_context(nc.sbuf_tensor("H1", [128, TOK], BF))
    HW = [es3.enter_context(nc.sbuf_tensor(f"HW{i}", [128, TOK], BF)) for i in range(2)]
    UP = [es3.enter_context(nc.psum_tensor(f"UP{i}", [128, 512], F32)) for i in range(4)]
    GP = [es3.enter_context(nc.psum_tensor(f"GP{i}", [128, 512], F32)) for i in range(4)]
    uld = es3.enter_context(nc.semaphore("uld"))
    ugm = es3.enter_context(nc.semaphore("ugm"))
    wpm = es3.enter_context(nc.semaphore("wpm"))
    sgd = es3.enter_context(nc.semaphore("sgd"))
    h1d = es3.enter_context(nc.semaphore("h1d"))
    hwd = es3.enter_context(nc.semaphore("hwd"))
    hst = es3.enter_context(nc.semaphore("hst"))
    with nc.Block() as blk:
        @blk.gpsimd
        def _(g):
            for ic in range(NIC):
                if ic >= 2:
                    g.wait_ge(ugm, ic - 1)
                g.dma_start(out=UB[ic % 2][:], in_=wu[ic]).then_inc(uld, 16)
                g.dma_start(out=GB[ic % 2][:], in_=wg[ic]).then_inc(uld, 16)

        @blk.tensor
        def _(t):
            for ic in range(NIC):
                t.wait_ge(uld, 32 * (ic + 1))
                if ic >= 1:
                    t.wait_ge(hwd, ic)
                    t.wait_ge(sgd, ic)
                for dc in range(NDC):
                    for ts in range(4):
                        t.matmul(UP[ts][:], UB[ic % 2][:, dc, :],
                                 XN[:, dc, ts * 512:(ts + 1) * 512],
                                 start=(dc == 0), stop=(dc == NDC - 1))
                    for ts in range(4):
                        t.matmul(GP[ts][:], GB[ic % 2][:, dc, :],
                                 XN[:, dc, ts * 512:(ts + 1) * 512],
                                 start=(dc == 0), stop=(dc == NDC - 1))
                t.maybe_drain_then_inc((ugm, 1))
                t.wait_ge(h1d, ic + 1)
                for ts in range(4):
                    t.matmul(UP[ts][:], S_sb[:, ic * 128:(ic + 1) * 128],
                             WTB[:, ts * 512:(ts + 1) * 512], start=True, stop=True)
                t.maybe_drain_then_inc((wpm, 1))

        @blk.scalar
        def _(act):
            for ic in range(NIC):
                act.wait_ge(ugm, ic + 1)
                for ts in range(4):
                    act.activation(out=SG[:, ts * 512:(ts + 1) * 512], in_=GP[ts][:],
                                   func=AF.Silu, bias=0.0, scale=1.0)
                act.maybe_drain_then_inc((sgd, 1))

        @blk.vector
        def _(v):
            for ic in range(NIC):
                v.wait_ge(sgd, ic + 1)
                for ts in range(4):
                    v.tensor_tensor(out=H1[:, ts * 512:(ts + 1) * 512], in0=UP[ts][:],
                                    in1=SG[:, ts * 512:(ts + 1) * 512], op=A.mult)
                v.maybe_drain_then_inc((h1d, 1))
                v.wait_ge(wpm, ic + 1)
                if ic >= 2:
                    v.wait_ge(hst, 16 * (ic - 1))
                for ts in range(4):
                    v.tensor_tensor(out=HW[ic % 2][:, ts * 512:(ts + 1) * 512], in0=UP[ts][:],
                                    in1=H1[:, ts * 512:(ts + 1) * 512], op=A.mult)
                v.maybe_drain_then_inc((hwd, 1))

        @blk.sync
        def _(sync):
            for ic in range(NIC):
                sync.wait_ge(hwd, ic + 1)
                sync.dma_start(out=h_sc[ic], in_=HW[ic % 2][:]).then_inc(hst, 16)
            sync.wait_ge(hst, 16 * NIC)
            if dbg:
                sync.dma_start(out=xn_dbg, in_=XN[:]).then_inc(hst, 16)
                sync.dma_start(out=wtb_dbg, in_=WTB[:]).then_inc(hst, 16)
                sync.wait_ge(hst, 16 * NIC + 32)
    es3.close()
    mid2.close()
    if upto < 4:
        mid.close(); root.close(); return nc

    es4 = ExitStack()
    WVB = [es4.enter_context(nc.sbuf_tensor(f"WVB{i}", [128, NDC, 128], BF)) for i in range(2)]
    KVT = [es4.enter_context(nc.sbuf_tensor(f"KVT{i}", [128, TOK], BF)) for i in range(2)]
    VTT = [es4.enter_context(nc.sbuf_tensor(f"VTT{i}", [128, 4, 128], BF)) for i in range(2)]
    KVP = [es4.enter_context(nc.psum_tensor(f"KVP{i}", [128, 512], F32)) for i in range(4)]
    VPT = [es4.enter_context(nc.psum_tensor(f"VPT{i}", [128, 4, 128], BF)) for i in range(2)]
    kwl = es4.enter_context(nc.semaphore("kwl"))
    kvm = es4.enter_context(nc.semaphore("kvm"))
    kte = es4.enter_context(nc.semaphore("kte"))
    ksk = es4.enter_context(nc.semaphore("ksk"))
    vst = es4.enter_context(nc.semaphore("vst"))
    vtm = es4.enter_context(nc.semaphore("vtm"))
    vte = es4.enter_context(nc.semaphore("vte"))
    ccs = es4.enter_context(nc.semaphore("ccs"))
    ccs2 = es4.enter_context(nc.semaphore("ccs2"))

    def kfree_evt(i):
        if i < NQC:
            return ("ksk", 16 * (i + 1))
        gv_ = i - NQC
        return ("vtm", 4 * (gv_ + 1))

    with nc.Block() as blk:
        @blk.sync
        def _(sync):
            for vc in range(NVC):
                if vc < NQC:
                    sync.wait_ge(kte, vc + 1)
                    sync.dma_start(out=k_own.rearrange("(c p) t -> c p t", p=128)[vc],
                                   in_=KVT[vc % 2][:]).then_inc(ksk, 16)
                else:
                    gv = vc - NQC
                    for bt in range(4):
                        gvb = gv * 4 + bt
                        sync.wait_ge(vte, gvb + 1)
                        for j in range(4):
                            tsub = bt * 4 + j
                            sync.dma_start(
                                out=v_own.rearrange("(s t) c -> s t c", t=128)[tsub][:, gv * 128:gv * 128 + 128],
                                in_=VTT[gvb % 2][:, j, :]).then_inc(vst, 16)
            sync.wait_ge(ksk, 16 * NQC)
            sync.wait_ge(vst, 16 * 4 * 64)

        @blk.gpsimd
        def _(g):
            for vc in range(NVC):
                if vc >= 2:
                    g.wait_ge(kvm, vc - 1)
                g.dma_start(out=WVB[vc % 2][:], in_=wkv[vc]).then_inc(kwl, 16)
            g.wait_ge(ksk, 16 * NQC)
            g.wait_ge(vst, 16 * 4 * 64)
            g.collective_compute(
                "AllGather", A.bypass,
                replica_groups=[[0, 1], [2, 3], [4, 5], [6, 7]],
                ins=[k_own.opt()],
                outs=[k_gath.opt()],
            ).then_inc(ccs, 1)
            g.collective_compute(
                "AllGather", A.bypass,
                replica_groups=[[0, 1], [2, 3], [4, 5], [6, 7]],
                ins=[v_own.opt()],
                outs=[v_gath.opt()],
            ).then_inc(ccs2, 1)
            g.wait_ge(ccs, 1)
            g.wait_ge(ccs2, 1)
            if dbg:
                g.dma_start(out=kg_dbg, in_=k_gath).then_inc(ccs2, 16)
                g.dma_start(out=vg_dbg, in_=v_gath).then_inc(ccs2, 16)
                g.wait_ge(ccs2, 33)

        @blk.scalar
        def _(act):
            for vc in range(NQC, NVC):
                gv = vc - NQC
                for bt in range(4):
                    gvb = gv * 4 + bt
                    act.wait_ge(vtm, gvb + 1)
                    if gvb >= 2:
                        act.wait_ge(vst, 16 * 4 * (gvb - 1))
                    act.copy(out=VTT[gvb % 2][:], in_=VPT[gvb % 2][:])
                    act.maybe_drain_then_inc((vte, 1))

        @blk.vector
        def _(v):
            v.wait_ge(ld2, 32)
            for vc in range(NVC):
                v.wait_ge(kvm, vc + 1)
                if vc >= 2:
                    sem, val = kfree_evt(vc - 2)
                    v.wait_ge(ksk if sem == "ksk" else vtm, val)
                for ts in range(4):
                    v.tensor_scalar(out=KVT[vc % 2][:, ts * 512:(ts + 1) * 512],
                                    in0=KVP[ts][:], scalar1=BKV[:, vc:vc + 1],
                                    scalar2=None, op0=A.add)
                v.maybe_drain_then_inc((kte, 1))

        @blk.tensor
        def _(t):
            for vc in range(NVC):
                t.wait_ge(kwl, 16 * (vc + 1))
                if vc >= 1:
                    t.wait_ge(kte, vc)
                for dc in range(NDC):
                    for ts in range(4):
                        t.matmul(KVP[ts][:], WVB[vc % 2][:, dc, :],
                                 XN[:, dc, ts * 512:(ts + 1) * 512],
                                 start=(dc == 0), stop=(dc == NDC - 1))
                t.maybe_drain_then_inc((kvm, 1))
                if vc >= NQC:
                    gv = vc - NQC
                    t.wait_ge(kte, vc + 1)
                    for bt in range(4):
                        gvb = gv * 4 + bt
                        if gvb >= 2:
                            t.wait_ge(vte, gvb - 1)
                        for j in range(4):
                            tsub = bt * 4 + j
                            t.transpose(VPT[gvb % 2][:, j, :],
                                        KVT[vc % 2][:, tsub * 128:(tsub + 1) * 128],
                                        idbf[:])
                        t.maybe_drain_then_inc((vtm, 1))
    es4.close()
    mid.close()
    if upto < 5:
        root.close(); return nc

    es5 = ExitStack()
    SALL = es5.enter_context(nc.sbuf_tensor("SALL", [128, NH, SFULL], F16))
    QS = [es5.enter_context(nc.sbuf_tensor(f"QS{i}", [128, NQC, 128], BF)) for i in range(2)]
    KBB = [es5.enter_context(nc.sbuf_tensor(f"KBB{i}", [128, NQC, 512], BF)) for i in range(2)]
    VBB = [es5.enter_context(nc.sbuf_tensor(f"VBB{i}", [128, 4, 2048], BF)) for i in range(2)]
    MS = [es5.enter_context(nc.sbuf_tensor(f"MS{i}", [128, 512], F32)) for i in range(2)]
    MBAL = es5.enter_context(nc.sbuf_tensor("MBAL", [128, NH * 8], F32))
    MMAX = es5.enter_context(nc.sbuf_tensor("MMAX", [128, NH], F32))
    NEGM = es5.enter_context(nc.sbuf_tensor("NEGM", [128, NH], F32))
    LSUM = es5.enter_context(nc.sbuf_tensor("LSUM", [128, NH], F32))
    RLV = es5.enter_context(nc.sbuf_tensor("RLV", [128, NH], F32))
    PTS = [es5.enter_context(nc.sbuf_tensor(f"PTS{i}", [128, 4, 128], F16)) for i in range(NH)]
    OSB = es5.enter_context(nc.sbuf_tensor("OSB", [128, NH * 256], F16))
    VALT = es5.enter_context(nc.sbuf_tensor("VALT", [128, NQC, 128], F16))
    SPS = [es5.enter_context(nc.psum_tensor(f"SPS{i}", [128, 512], F32)) for i in range(2)]
    PTP = [es5.enter_context(nc.psum_tensor(f"PTP{i}", [128, 4, 128], F16)) for i in range(2)]
    OPP = [es5.enter_context(nc.psum_tensor(f"OPP{i}", [128, 2, 256], F32)) for i in range(4)]
    qld = es5.enter_context(nc.semaphore("qld"))
    msl = es5.enter_context(nc.semaphore("msl"))
    msf = es5.enter_context(nc.semaphore("msf"))
    kld = es5.enter_context(nc.semaphore("kld"))
    vld = es5.enter_context(nc.semaphore("vld"))
    smm = es5.enter_context(nc.semaphore("smm"))
    sev = es5.enter_context(nc.semaphore("sev"))
    nmd = es5.enter_context(nc.semaphore("nmd"))
    expd = es5.enter_context(nc.semaphore("expd"))
    ptm = es5.enter_context(nc.semaphore("ptm"))
    ptsc = es5.enter_context(nc.semaphore("ptsc"))
    avm = es5.enter_context(nc.semaphore("avm"))
    osd = es5.enter_context(nc.semaphore("osd"))
    vst2 = es5.enter_context(nc.semaphore("vst2"))
    gd5 = es5.enter_context(nc.semaphore("gd5"))

    PTSC_AT_VALS = [8 * KB_PRE[s + 1] + 4 * (s + 1) for s in range(NSLOT)]
    with nc.Block() as blk:
        @blk.sync
        def _(sync):
            for s in range(NSLOT):
                kbc = KBCS[s]
                if s >= 2:
                    sync.wait_ge(smm, 8 * KB_PRE[s - 1])
                sync.dma_start(out=QS[s % 2][:],
                               in_=q_sc[:, :, s * 128:(s + 1) * 128].rearrange("q p t -> p q t")
                               ).then_inc(qld, 16)
                if s >= 2:
                    sync.wait_ge(msf, s - 1)
                sync.dma_start(out=MS[s % 2][:], in_=masks_in[s]).then_inc(msl, 16)
                for kb in range(kbc):
                    g = KB_PRE[s] + kb
                    if g >= 2:
                        sync.wait_ge(smm, 8 * (g - 1))
                    for i in range(4):
                        r, lt = i % 2, 2 * kb + i // 2
                        sync.dma_start(
                            out=KBB[g % 2][:, :, i * 128:(i + 1) * 128],
                            in_=k_gath[r * 2048:(r + 1) * 2048, lt * 128:(lt + 1) * 128]
                            .rearrange("(c p) t -> p c t", p=128)
                        ).then_inc(kld, 16)
                for kb2 in range(kbc):
                    g2 = KB_PRE[s] + kb2
                    if g2 >= 2:
                        sync.wait_ge(avm, g2 - 1)
                    for i in range(4):
                        r, lt = i % 2, 2 * kb2 + i // 2
                        sync.dma_start(
                            out=VBB[g2 % 2][:, i, :],
                            in_=v_gath[r * 2048 + lt * 128: r * 2048 + (lt + 1) * 128, :]
                        ).then_inc(vld, 16)
                sync.wait_ge(ptsc, PTSC_AT_VALS[s])
                sync.dma_start(out=vals_sc[s],
                               in_=VALT[:].rearrange("p q t -> p (q t)")).then_inc(vst2, 16)
            sync.wait_ge(vst2, 16 * NSLOT)

        @blk.tensor
        def _(t):
            ptpu = [0]
            for s in range(NSLOT):
                kbc = KBCS[s]
                t.wait_ge(qld, 16 * (s + 1))
                for kb in range(kbc):
                    g = KB_PRE[s] + kb
                    t.wait_ge(kld, 64 * (g + 1))
                    for h in range(NH):
                        ghb = 8 * g + h
                        if ghb >= 2:
                            t.wait_ge(sev, ghb - 1)
                        for qi in range(2):
                            t.matmul(SPS[ghb % 2][:], QS[s % 2][:, 2 * h + qi, :],
                                     KBB[g % 2][:, 2 * h + qi, :],
                                     start=(qi == 0), stop=(qi == 1))
                        t.maybe_drain_then_inc((smm, 1))
                if s >= 1:
                    t.wait_ge(osd, s)
                for kb2 in range(kbc):
                    g2 = KB_PRE[s] + kb2
                    t.wait_ge(vld, 64 * (g2 + 1))
                    pend = []
                    for h in range(NH + 1):
                        if h < NH:
                            t.wait_ge(expd, 8 * s + h + 1)
                            if ptpu[0] >= 2:
                                t.wait_ge(ptsc, ptpu[0] - 1)
                            for j in range(4):
                                t.transpose(PTP[ptpu[0] % 2][:, j, :],
                                            SALL[:, h, (kb2 * 4 + j) * 128:(kb2 * 4 + j) * 128 + 128],
                                            idfp[:])
                            t.maybe_drain_then_inc((ptm, 1))
                            ptpu[0] += 1
                            pend.append((h, ptpu[0]))
                        if (h >= 1 and pend) or h == NH:
                            hh, pcnt = pend.pop(0)
                            h_save = h
                            h = hh
                            t.wait_ge(ptsc, pcnt)
                            for j in range(4):
                                t.matmul(OPP[h // 2][:, h % 2, :], PTS[h][:, j, :],
                                         VBB[g2 % 2][:, j, h * 256:(h + 1) * 256],
                                         start=(kb2 == 0 and j == 0 and h % 2 == 0),
                                         stop=(kb2 == kbc - 1 and j == 3))
                            h = h_save
                    t.maybe_drain_then_inc((avm, 1))
                t.wait_ge(osd, s + 1)
                for vb in range(4):
                    if ptpu[0] >= 2:
                        t.wait_ge(ptsc, ptpu[0] - 1)
                    for j in range(4):
                        cc = 4 * vb + j
                        t.transpose(PTP[ptpu[0] % 2][:, j, :],
                                    OSB[:, cc * 128:(cc + 1) * 128], idfp[:])
                    t.maybe_drain_then_inc((ptm, 1))
                    ptpu[0] += 1

        @blk.vector
        def _(v):
            g5 = [0]
            def G():
                g5[0] += 1
                v.maybe_drain_then_inc((gd5, 1))
                v.wait_ge(gd5, g5[0])
            for s in range(NSLOT):
                kbc = KBCS[s]
                diag = kbc - 1
                for kb in range(kbc):
                    g = KB_PRE[s] + kb
                    if kb == diag:
                        v.wait_ge(msl, 16 * (s + 1))
                    for h in range(NH):
                        ghb = 8 * g + h
                        v.wait_ge(smm, ghb + 1)
                        if kb == diag:
                            v.tensor_tensor(out=SPS[ghb % 2][:], in0=SPS[ghb % 2][:],
                                            in1=MS[s % 2][:], op=A.add)
                        v.tensor_scalar(out=SALL[:, h, kb * 512:(kb + 1) * 512],
                                        in0=SPS[ghb % 2][:], scalar1=0.0, scalar2=None,
                                        op0=A.add, op1=A.max,
                                        accum_out=MBAL[:, h * 8 + kb:h * 8 + kb + 1]
                                        ).then_inc(sev, 1)
                        if kb == diag:
                            G()
                            v.reduce_max(MMAX[:, h:h + 1], MBAL[:, h * 8:h * 8 + kbc], axis=AX)
                            G()
                            v.tensor_scalar(out=NEGM[:, h:h + 1], in0=MMAX[:, h:h + 1],
                                            scalar1=-1.0, scalar2=None, op0=A.mult)
                            v.maybe_drain_then_inc((nmd, 1))
                v.maybe_drain_then_inc((msf, 1))
                for h in range(NH):
                    v.wait_ge(expd, 8 * s + h + 1)
                    v.reciprocal(RLV[:, h:h + 1], LSUM[:, h:h + 1])
                G()
                v.wait_ge(avm, KB_PRE[s + 1])
                for h in range(NH):
                    v.tensor_scalar(out=OSB[:, h * 256:(h + 1) * 256],
                                    in0=OPP[h // 2][:, h % 2, :],
                                    scalar1=RLV[:, h:h + 1], scalar2=None, op0=A.mult)
                v.maybe_drain_then_inc((osd, 1))

        @blk.scalar
        def _(act):
            ptpu = [0]
            for s in range(NSLOT):
                kbc = KBCS[s]
                klen = kbc * 512
                for h in range(NH):
                    act.wait_ge(nmd, 8 * s + h + 1)
                    act.activation(out=SALL[:, h, 0:klen], in_=SALL[:, h, 0:klen],
                                   func=AF.Exp, bias=NEGM[:, h:h + 1], scale=1.0,
                                   accum_out=LSUM[:, h:h + 1])
                    act.maybe_drain_then_inc((expd, 1))
                for kb2 in range(kbc):
                    for h in range(NH):
                        act.wait_ge(ptm, ptpu[0] + 1)
                        act.copy(out=PTS[h][:], in_=PTP[ptpu[0] % 2][:])
                        act.maybe_drain_then_inc((ptsc, 1))
                        ptpu[0] += 1
                if s >= 1:
                    act.wait_ge(vst2, 16 * s)
                for vb in range(4):
                    act.wait_ge(ptm, ptpu[0] + 1)
                    act.copy(out=VALT[:, 4 * vb:4 * vb + 4, :], in_=PTP[ptpu[0] % 2][:])
                    act.maybe_drain_then_inc((ptsc, 1))
                    ptpu[0] += 1
    es5.close()
    if upto < 6:
        root.close(); return nc

    es6 = ExitStack()
    VALB = es6.enter_context(nc.sbuf_tensor("VALB", [128, NSLOT, NQC, 128], F16))
    WOB = [es6.enter_context(nc.sbuf_tensor(f"WOB{i}", [128, NQC, 1024], BF)) for i in range(2)]
    SOT = [es6.enter_context(nc.sbuf_tensor(f"SOT{i}", [128, 1024], F32)) for i in range(2)]
    SPO = [es6.enter_context(nc.psum_tensor(f"SPO{i}", [128, 512], F32)) for i in range(8)]
    vll = es6.enter_context(nc.semaphore("vll"))
    wol = es6.enter_context(nc.semaphore("wol"))
    wof = es6.enter_context(nc.semaphore("wof"))
    somm = es6.enter_context(nc.semaphore("somm"))
    soev = es6.enter_context(nc.semaphore("soev"))
    soa = es6.enter_context(nc.semaphore("soa"))
    sst = es6.enter_context(nc.semaphore("sst"))
    with nc.Block() as blk:
        @blk.sync
        def _(sync):
            for sl in range(NSLOT):
                sync.dma_start(out=VALB[:, sl, :, :],
                               in_=vals_sc[sl].rearrange("p (q t) -> p q t", q=NQC)
                               ).then_inc(vll, 16)
            for jq in range(4):
                for grp in range(4):
                    gj = jq * 4 + grp
                    for sl in range(4):
                        gsl = gj * 4 + sl
                        sync.wait_ge(soev, gsl + 1)
                        sync.dma_start(out=s_sc[grp * 4 + sl][:, jq * 1024:(jq + 1) * 1024],
                                       in_=SOT[gsl % 2][:]).then_inc(sst, 16)
            sync.wait_ge(sst, 16 * 64)

        @blk.gpsimd
        def _(g):
            for jq in range(4):
                if jq >= 2:
                    g.wait_ge(wof, jq - 1)
                g.dma_start(out=WOB[jq % 2][:],
                            in_=wo[:, :, jq * 1024:(jq + 1) * 1024].rearrange("c p j -> p c j")
                            ).then_inc(wol, 16)

        @blk.tensor
        def _(t):
            t.wait_ge(vll, 16 * NSLOT)
            for jq in range(4):
                t.wait_ge(wol, 16 * (jq + 1))
                for grp in range(4):
                    gj = jq * 4 + grp
                    if gj >= 1:
                        t.wait_ge(soev, 4 * gj)
                    for cc in range(NQC):
                        for sl in range(4):
                            for jb in range(2):
                                t.matmul(SPO[sl * 2 + jb][:],
                                         VALB[:, grp * 4 + sl, cc, :],
                                         WOB[jq % 2][:, cc, jb * 512:(jb + 1) * 512],
                                         start=(cc == 0), stop=(cc == NQC - 1))
                    t.maybe_drain_then_inc((somm, 1))
                t.maybe_drain_then_inc((wof, 1))

        @blk.vector
        def _(v):
            for jq in range(4):
                for grp in range(4):
                    gj = jq * 4 + grp
                    v.wait_ge(somm, gj + 1)
                    for sl in range(4):
                        gsl = gj * 4 + sl
                        if gsl >= 2:
                            v.wait_ge(sst, 16 * (gsl - 1))
                        v.tensor_copy(SOT[gsl % 2][:, 0:512], SPO[sl * 2][:])
                        v.wait_ge(soa, gsl + 1)
                        v.maybe_drain_then_inc((soev, 1))

        @blk.scalar
        def _(act):
            for jq in range(4):
                for grp in range(4):
                    gj = jq * 4 + grp
                    act.wait_ge(somm, gj + 1)
                    for sl in range(4):
                        gsl = gj * 4 + sl
                        if gsl >= 2:
                            act.wait_ge(sst, 16 * (gsl - 1))
                        act.copy(out=SOT[gsl % 2][:, 512:1024], in_=SPO[sl * 2 + 1][:])
                        act.maybe_drain_then_inc((soa, 1))
    es6.close()
    if upto < 7:
        root.close(); return nc

    es7 = ExitStack()
    BOS = es7.enter_context(nc.sbuf_tensor("BOS", [1, D], BF))
    BOB = es7.enter_context(nc.sbuf_tensor("BOB", [128, D], BF))
    DT = es7.enter_context(nc.sbuf_tensor("DT", [128, NIC, 1024], BF))
    HTA = es7.enter_context(nc.sbuf_tensor("HTA", [128, 28, 512], BF))
    HTB = es7.enter_context(nc.sbuf_tensor("HTB", [128, 28, 512], BF))
    SS = es7.enter_context(nc.sbuf_tensor("SS", [128, 1024], F32))
    XS = es7.enter_context(nc.sbuf_tensor("XS", [128, 1024], F32))
    GG = [es7.enter_context(nc.sbuf_tensor(f"GG{i}", [128, 512], F32)) for i in range(2)]
    SFB = es7.enter_context(nc.sbuf_tensor("SFB", [128, 1024], F32))
    OUTB = [es7.enter_context(nc.sbuf_tensor(f"OUTB{i}", [128, 512], F32)) for i in range(2)]
    PSD = [es7.enter_context(nc.psum_tensor(f"PSD{i}", [128, 512], F32)) for i in range(8)]
    bld = es7.enter_context(nc.semaphore("bld"))
    bpm7 = es7.enter_context(nc.semaphore("bpm7"))
    bobd = es7.enter_context(nc.semaphore("bobd"))
    dld = es7.enter_context(nc.semaphore("dld"))
    dpf = es7.enter_context(nc.semaphore("dpf"))
    hld = es7.enter_context(nc.semaphore("hld"))
    haf = es7.enter_context(nc.semaphore("haf"))
    hbf = es7.enter_context(nc.semaphore("hbf"))
    dmm = es7.enter_context(nc.semaphore("dmm"))
    gsd = es7.enter_context(nc.semaphore("gsd"))
    gfr = es7.enter_context(nc.semaphore("gfr"))
    sxl = es7.enter_context(nc.semaphore("sxl"))
    ssfre = es7.enter_context(nc.semaphore("ssfre"))
    outd = es7.enter_context(nc.semaphore("outd"))
    ost = es7.enter_context(nc.semaphore("ost"))
    with nc.Block() as blk:
        @blk.gpsimd
        def _(g):
            g.dma_start(out=BOS[:], in_=bo_in).then_inc(bld, 16)
            for jq in range(4):
                for pc in range(8):
                    if jq >= 1:
                        if pc == 7:
                            g.wait_ge(dmm, 4 * jq)
                        else:
                            g.wait_ge(dpf, 7 * (jq - 1) + pc + 1)
                    g.dma_start(out=DT[:, 7 * pc:7 * pc + 7, :],
                                in_=wd[7 * pc:7 * pc + 7, :, jq * 1024:(jq + 1) * 1024]
                                .rearrange("i p j -> p i j")).then_inc(dld, 16)
                for tt in range(4):
                    gtt = 4 * jq + tt
                    if gtt >= 1:
                        g.wait_ge(haf, gtt)
                    for pc in range(4):
                        g.dma_start(out=HTA[:, 7 * pc:7 * pc + 7, :],
                                    in_=h_sc[7 * pc:7 * pc + 7, :, tt * 512:(tt + 1) * 512]
                                    .rearrange("i p t -> p i t")).then_inc(hld, 16)
                    if gtt >= 1:
                        g.wait_ge(hbf, gtt)
                    for pc in range(4):
                        g.dma_start(out=HTB[:, 7 * pc:7 * pc + 7, :],
                                    in_=h_sc[28 + 7 * pc:28 + 7 * pc + 7, :, tt * 512:(tt + 1) * 512]
                                    .rearrange("i p t -> p i t")).then_inc(hld, 16)

        @blk.tensor
        def _(t):
            t.wait_ge(bld, 16)
            for jb in range(8):
                t.matmul(PSD[jb][:], ones_bf[:], BOS[:, jb * 512:(jb + 1) * 512],
                         start=True, stop=True)
            t.maybe_drain_then_inc((bpm7, 1))
            for jq in range(4):
                for tt in range(4):
                    gtt = 4 * jq + tt
                    t.wait_ge(hld, 16 * 4 * (2 * gtt + 1))
                    if gtt == 0:
                        t.wait_ge(bobd, 1)
                    else:
                        t.wait_ge(gsd, 8 * gtt)
                    for ic in range(NIC):
                        if tt == 0 and ic % 7 == 0:
                            t.wait_ge(dld, 16 * (8 * jq + ic // 7 + 1))
                        if ic == 28:
                            t.wait_ge(hld, 16 * 4 * (2 * gtt + 2))
                        ht = HTA if ic < 28 else HTB
                        icl = ic if ic < 28 else ic - 28
                        for tsub in range(4):
                            for jb in range(2):
                                mm = t.matmul(PSD[tsub * 2 + jb][:],
                                              ht[:, icl, tsub * 128:(tsub + 1) * 128],
                                              DT[:, ic, jb * 512:(jb + 1) * 512],
                                              start=(ic == 0), stop=(ic == NIC - 1))
                                if tsub == 3 and jb == 1 and ic == 27:
                                    mm.then_inc(haf, 1)
                                if tsub == 3 and jb == 1 and ic == NIC - 1:
                                    mm.then_inc(hbf, 1)
                                if (tsub == 3 and jb == 1 and tt == 3
                                        and ic % 7 == 6 and ic not in (27, 55)):
                                    mm.then_inc(dpf, 1)
                                if tsub == 0 and jb == 0 and tt == 3 and ic == 28:
                                    mm.then_inc(dpf, 1)

                    t.maybe_drain_then_inc((dmm, 1))

        @blk.sync
        def _(sync):
            for jq in range(4):
                for tt in range(4):
                    for tsub in range(4):
                        ts7 = (4 * jq + tt) * 4 + tsub
                        slot = tt * 4 + tsub
                        if ts7 >= 1:
                            sync.wait_ge(ssfre, ts7)
                        sync.dma_start(out=SS[:],
                                       in_=s_sc[slot][:, jq * 1024:(jq + 1) * 1024]
                                       ).then_inc(sxl, 16)
                        sync.dma_start(out=XS[:],
                                       in_=x_mine[slot][:, jq * 1024:(jq + 1) * 1024]
                                       ).then_inc(sxl, 16)
                        for jb in range(2):
                            ko = ts7 * 2 + jb
                            sync.wait_ge(outd, ko + 1)
                            sync.dma_start(out=out_mine[slot][:, jq * 1024 + jb * 512:
                                                              jq * 1024 + jb * 512 + 512],
                                           in_=OUTB[ko % 2][:]).then_inc(ost, 16)
            sync.wait_ge(ost, 16 * 128)

        @blk.scalar
        def _(act):
            for jq in range(4):
                for tt in range(4):
                    gtt = 4 * jq + tt
                    act.wait_ge(dmm, gtt + 1)
                    for tsub in range(4):
                        for jb in range(2):
                            k = gtt * 8 + tsub * 2 + jb
                            if k >= 2:
                                act.wait_ge(gfr, k - 1)
                            act.activation(out=GG[k % 2][:], in_=PSD[tsub * 2 + jb][:],
                                           func=AF.Silu, bias=0.0, scale=1.0)
                            act.maybe_drain_then_inc((gsd, 1))

        @blk.vector
        def _(v):
            v.wait_ge(bpm7, 1)
            for jb in range(8):
                v.tensor_copy(BOB[:, jb * 512:(jb + 1) * 512], PSD[jb][:])
            v.maybe_drain_then_inc((bobd, 1))
            for jq in range(4):
                for tt in range(4):
                    for tsub in range(4):
                        ts7 = (4 * jq + tt) * 4 + tsub
                        v.wait_ge(sxl, 32 * (ts7 + 1))
                        v.tensor_tensor(out=SFB[:], in0=SS[:],
                                        in1=BOB[:, jq * 1024:(jq + 1) * 1024], op=A.add)
                        for jb in range(2):
                            k = (4 * jq + tt) * 8 + tsub * 2 + jb
                            ko = ts7 * 2 + jb
                            v.wait_ge(gsd, k + 1)
                            if ko >= 2:
                                v.wait_ge(ost, 16 * (ko - 1))
                            v.tensor_tensor(out=OUTB[ko % 2][:],
                                            in0=SFB[:, jb * 512:(jb + 1) * 512],
                                            in1=GG[k % 2][:], op=A.mult).then_inc(gfr, 1)
                            last = v.tensor_tensor(out=OUTB[ko % 2][:], in0=OUTB[ko % 2][:],
                                                   in1=XS[:, jb * 512:(jb + 1) * 512],
                                                   op=A.add)
                            if jb == 1:
                                last.then_inc(ssfre, 1)
                            v.maybe_drain_then_inc((outd, 1))
    es7.close()
    root.close()
    return nc



def _lay_w(wt):
    d_in, d_out = wt.shape
    ndc = d_in // 128
    noc = d_out // 128
    return np.ascontiguousarray(wt.reshape(ndc, 128, noc, 128).transpose(2, 1, 0, 3))


def _host_prepare(inputs):
    x = np.asarray(inputs["L0_attn_x"], np.float32)
    w_qkv = np.asarray(inputs["L0_attn_w_qkv"], np.float32)
    b_qkv = np.asarray(inputs["L0_attn_b_qkv"], np.float32)
    w_o = np.asarray(inputs["L0_attn_w_o"], np.float32)
    b_o = np.asarray(inputs["L0_attn_b_o"], np.float32)
    gate_w = np.asarray(inputs["L0_ffn_gate_w"], np.float32)
    u = np.asarray(inputs["L0_ffn_u"], np.float32)
    g = np.asarray(inputs["L0_ffn_g"], np.float32)
    d = np.asarray(inputs["L0_ffn_d"], np.float32)

    shared = {}
    shared["wq"] = _lay_w(np.ascontiguousarray(w_qkv[0:2048].T))
    shared["wkv"] = _lay_w(np.ascontiguousarray(w_qkv[2048:6144].T))
    shared["wu"] = _lay_w(np.ascontiguousarray(u.T))
    shared["wg"] = _lay_w(np.ascontiguousarray(g.T))
    shared["wd"] = np.ascontiguousarray(d.T.reshape(NIC, 128, D))
    shared["wo"] = np.ascontiguousarray(w_o.T.reshape(NQC, 128, D))
    shared["gwt"] = np.ascontiguousarray(gate_w.T.reshape(NDC, 128, NE).transpose(1, 0, 2))
    shared["bq_in"] = np.ascontiguousarray(b_qkv[0:2048].reshape(NQC, 128).T)
    shared["bkv_in"] = np.ascontiguousarray(b_qkv[2048:6144].reshape(NVC, 128).T)
    shared["bo_in"] = np.ascontiguousarray(b_o.reshape(1, D))
    shared["eye_in"] = np.eye(128, dtype=np.float32)
    shared["ones_in"] = np.ones((1, 128), np.float32)
    smat = np.zeros((NE, NIC * 128), np.float32)
    rows = np.arange(NIC * 128)
    smat[rows // 448, rows] = 1.0
    shared["smat_in"] = smat

    def masks_for(parity):
        m = np.zeros((NSLOT, 128, 512), np.float32)
        for s in range(NSLOT):
            qt = 2 * s + parity
            base = (KBCS[s] - 1) * 512
            qpos = qt * 128 + np.arange(128)[:, None]
            kpos = base + np.arange(512)[None, :]
            m[s] = np.where(kpos <= qpos, 0.0, -30000.0).astype(np.float32)
        return m

    masks = [masks_for(0), masks_for(1)]

    in_maps = []
    for c in range(8):
        b, p = c // 2, c % 2
        xb = x[b].reshape(32, 128, D)
        im = dict(shared)
        im["x_mine"] = np.ascontiguousarray(xb[p::2])
        im["masks_in"] = masks[p]
        for kk in im:
            assert im[kk].dtype == np.float32, (kk, im[kk].dtype)
        in_maps.append(im)
    return in_maps


def kernel(**inputs):
    in_maps = _host_prepare(inputs)
    nc = build_nc(dbg=False)
    res = run_bass_kernel_spmd(nc, in_maps, core_ids=list(range(8)))
    out = np.empty((4, SFULL, D), np.float32)
    for c in range(8):
        b, p = c // 2, c % 2
        om = np.asarray(res.results[c]["out_mine"])
        ov = out[b].reshape(32, 128, D)
        ov[p::2] = om
    return out
